# version 01
# baseline (speedup 1.0000x reference)
import math
import numpy as np
import ml_dtypes
import contextlib
import concourse.bass as bass
import concourse.mybir as mybir
from concourse.bass import IndirectOffsetOnAxis
from concourse.bass_utils import run_bass_kernel_spmd

F32 = mybir.dt.float32
BF16 = mybir.dt.bfloat16
I32 = mybir.dt.int32
AF = mybir.ActivationFunctionType
ALU = mybir.AluOpType
AX = mybir.AxisListType

S = 2048
T = 16
D = 1024
NCORE = 8
BF = ml_dtypes.bfloat16
KB = 1024


def host_consts():
    c = {}
    c["identf"] = np.eye(128, dtype=np.float32)
    c["ones_col_b"] = np.ones((128, 1), dtype=BF)
    c["ones_col_f"] = np.ones((128, 1), dtype=np.float32)
    c["ones_row_b"] = np.ones((1, 128), dtype=BF)
    c["ones_row_f"] = np.ones((1, 128), dtype=np.float32)
    c["tri_lt"] = (np.arange(128)[:, None] < np.arange(128)[None, :]).astype(np.float32)
    c["tid"] = (128.0 * np.arange(128)[:, None] + np.arange(128)[None, :]).astype(np.float32)
    tm = np.zeros((128, 4 * 512), dtype=BF)
    for j in range(4):
        f = np.arange(512)[None, :]
        pp = np.arange(128)[:, None]
        tm[:, j * 512:(j + 1) * 512] = (f - pp >= 128 * j).astype(BF)
    c["trimask"] = tm
    c["ramp4096"] = np.abs((32.0 * np.arange(128)[:, None] + np.arange(32)[None, :]) - 2048.0).astype(np.float32)
    c["dist2d"] = np.abs(np.arange(4096)[None, :] - 2048 - np.arange(128)[:, None]).astype(np.int16)
    hs = np.zeros((128, 8 * 16), dtype=BF)
    for cc in range(8):
        for p_ in range(128):
            hs[p_, 16 * cc + 2 * cc + (p_ >= 64)] = 1.0
    c["hsel"] = hs
    return c


CONST_SPECS = {
    "identf": ([128, 128], F32),
    "ones_col_b": ([128, 1], BF16), "ones_col_f": ([128, 1], F32),
    "ones_row_b": ([1, 128], BF16),
    "ones_row_f": ([1, 128], F32),
    "tri_lt": ([128, 128], F32), "tid": ([128, 128], F32),
    "trimask": ([128, 4 * 512], BF16), "ramp4096": ([128, 32], F32),
    "hsel": ([128, 8 * 16], BF16), "dist2d": ([128, 4096], mybir.dt.int16),
}


class Sem:
    def __init__(self, h):
        self.h = h
        self.n = 0

    def inc(self, inst, k=1):
        inst.then_inc(self.h, k)
        self.n += k
        return inst


class Prog:
    def __init__(self, name):
        self.nc = bass.Bass("TRN2", target_bir_lowering=False, name=name)
        self.stack = contextlib.ExitStack()
        self.const = {}

    def sem(self, name):
        return Sem(self.stack.enter_context(self.nc.semaphore(name)))

    def sb(self, name, shape, dtype):
        return self.stack.enter_context(self.nc.sbuf_tensor(name, shape, dtype))

    def ps(self, name, shape, dtype=F32):
        return self.stack.enter_context(self.nc.psum_tensor(name, shape, dtype))

    def arena(self, nbytes):
        self._arena = self.stack.enter_context(
            self.nc.sbuf_tensor("arena", [128, nbytes], mybir.dt.uint8,
                                is_transpose_dest=True))
        self._arena_addr = self.nc.lookup_mloc(self._arena).addr

    def at(self, name, off, shape, dtype):
        assert off % 64 == 0
        return self.nc.alloc_sbuf_tensor_at(name, shape, dtype,
                                            offset=self._arena_addr + off)

    def decl_fvals(self, values):
        self.fvals = list(values)
        nc = self.nc
        self.fvals_ap = nc.dram_tensor("c_fvals", [128, max(1, len(self.fvals))], F32,
                                       kind="ExternalInput").ap()
        self.fvals_sb = self.sb("k_fvals", [128, max(1, len(self.fvals))], F32)

    def fc(self, v):
        i = self.fvals.index(v)
        return self.fvals_sb[:, i:i + 1]

    def fvals_in_map(self):
        a = np.zeros((128, max(1, len(self.fvals))), np.float32)
        for i, v in enumerate(self.fvals):
            a[:, i] = v
        return {"c_fvals": a}

    def decl_consts(self, names=None):
        nc = self.nc
        self.const_aps = {}
        self.const_names = list(names) if names else list(CONST_SPECS)
        for nm in self.const_names:
            shape, dt = CONST_SPECS[nm]
            self.const_aps[nm] = nc.dram_tensor("c_" + nm, shape, dt, kind="ExternalInput").ap()
            self.const[nm] = self.sb("k_" + nm, shape, dt)

    def emit_const_loads(self, sync, ld):
        for nm in self.const_names:
            ld.inc(sync.dma_start(out=self.const[nm][:, :], in_=self.const_aps[nm][:, :]), 16)
        if hasattr(self, "fvals_ap"):
            ld.inc(sync.dma_start(out=self.fvals_sb[:, :], in_=self.fvals_ap[:, :]), 16)


def consts_in_map(names=None):
    hc = host_consts()
    names = names or list(CONST_SPECS)
    return {("c_" + k): hc[k] for k in names}


L0ATTN_CONSTS = ["identf", "ones_col_b", "ones_row_b", "ones_row_f"]



def emit_ln(p, blk, *, x_sb, zt, zf, stats, ld, ak, vk, tx, wait_ld, eps_ap):
    nc = p.nc
    ak0, vk0, tx0 = ak.n, vk.n, tx.n

    @blk.scalar
    def _(s):
        s.wait_ge(ld.h, wait_ld)
        for t in range(T):
            s.activation(out=zt[:, (t % 2) * D:(t % 2) * D + D],
                         in_=x_sb[:, t * D:(t + 1) * D], func=AF.Square,
                         accum_out=stats[:, T + t:T + t + 1])
            inst = s.activation(out=zt[:, (t % 2) * D:(t % 2) * D + D],
                                in_=x_sb[:, t * D:(t + 1) * D], func=AF.Copy,
                                accum_out=stats[:, t:t + 1])
        ak.inc(inst)

    @blk.vector
    def _(v):
        v.wait_ge(ak.h, ak0 + 1)
        vk.inc(v.tensor_scalar_mul(stats[:, 2 * T:3 * T], stats[:, 0:T], 1.0 / D))
        v.wait_ge(vk.h, vk0 + 1)
        m = stats[:, 2 * T:3 * T]
        v.tensor_mul(stats[:, 5 * T:6 * T], m, m)
        vk.inc(v.tensor_scalar(stats[:, 6 * T:7 * T], stats[:, T:2 * T], 1.0 / D,
                               None, op0=ALU.mult))
        v.wait_ge(vk.h, vk0 + 2)
        vk.inc(v.tensor_sub(stats[:, 6 * T:7 * T], stats[:, 6 * T:7 * T],
                            stats[:, 5 * T:6 * T]))

    @blk.scalar
    def _(s):
        s.wait_ge(vk.h, vk0 + 3)
        ak.inc(s.activation(out=stats[:, 5 * T:6 * T], in_=stats[:, 6 * T:7 * T],
                            func=AF.Sqrt, bias=eps_ap))

    @blk.vector
    def _(v):
        v.wait_ge(ak.h, ak0 + 2)
        vk.inc(v.reciprocal(stats[:, 3 * T:4 * T], stats[:, 5 * T:6 * T]))
        v.wait_ge(vk.h, vk0 + 4)
        vk.inc(v.scalar_tensor_tensor(stats[:, 4 * T:5 * T], stats[:, 2 * T:3 * T],
                                      -1.0, stats[:, 3 * T:4 * T],
                                      op0=ALU.mult, op1=ALU.mult))

    @blk.scalar
    def _(s):
        s.wait_ge(vk.h, vk0 + 5)
        for t in range(T):
            if t >= 2:
                s.wait_ge(tx.h, tx0 + 8 * 16 * (t - 1))
            ak.inc(s.activation(out=zt[:, (t % 2) * D:(t % 2) * D + D],
                                in_=x_sb[:, t * D:(t + 1) * D], func=AF.Identity,
                                scale=stats[:, 3 * T + t:3 * T + t + 1],
                                bias=stats[:, 4 * T + t:4 * T + t + 1]))

    @blk.sync
    def _(sync):
        for t in range(T):
            sync.wait_ge(ak.h, ak0 + 2 + t + 1)
            for c in range(8):
                tx.inc(sync.dma_start_transpose(
                    zf[:, c * S + t * 128:c * S + t * 128 + 128],
                    zt[:, (t % 2) * D + c * 128:(t % 2) * D + c * 128 + 128]), 16)


def build_l0_attn(debug=False):
    p = Prog("l0attn")
    nc = p.nc
    x_h = nc.dram_tensor("x", [S, D], F32, kind="ExternalInput").ap()
    wqkv_h = nc.dram_tensor("W_qkv", [D, 3 * D], F32, kind="ExternalInput").ap()
    bqkv_h = nc.dram_tensor("b_qkv", [3 * D], F32, kind="ExternalInput").ap()
    wout_h = nc.dram_tensor("W_out", [D, D], F32, kind="ExternalInput").ap()
    bout_h = nc.dram_tensor("b_out", [D], F32, kind="ExternalInput").ap()
    xout_h = nc.dram_tensor("xout", [S, D], F32, kind="ExternalOutput").ap()
    dbg_h = {}
    if debug:
        for nm, shape, dt in [("zf", [128, 8 * S], BF16), ("qf", [128, 8 * S], BF16),
                              ("kf", [128, 8 * S], BF16), ("vt", [128, T * D], BF16),
                              ("ctxn", [128, 8 * S], BF16), ("negMb", [128, 1], F32),
                              ("stats", [128, 8 * T], F32), ("es", [128, 16 * 512], BF16),
                              ("den", [1, 512], F32), ("rdb", [128, 512], F32),
                              ("bqk", [128, 32], F32)]:
            dbg_h[nm] = nc.dram_tensor("dbg_" + nm, shape, dt, kind="ExternalOutput").ap()

    with p.stack:
        p.decl_consts(L0ATTN_CONSTS)
        p.decl_fvals([1e-5])
        p.arena(192 * KB)
        x_sb = p.at("x_sb", 0, [128, T * D], F32)
        kf = p.at("kf_", 0, [128, 8 * S], BF16)
        vt = p.at("vt_", 32 * KB, [128, T * D], BF16)
        zf = p.at("zf_", 64 * KB, [128, 8 * S], BF16)
        es = p.at("es_", 64 * KB, [128, 16 * 512], BF16)
        sqq = p.at("sqq_", 80 * KB, [128, 2 * 512], BF16)
        t1 = p.at("t1_", 82 * KB, [128, 2 * 512], F32)
        rdb = p.at("rdb_", 86 * KB, [128, 512], F32)
        xt2 = p.at("xt2_", 88 * KB, [128, 2 * D], F32)
        wqkv = p.at("wqkv_", 96 * KB, [128, 8 * 3 * D], BF16)
        ctxn = p.at("ctxn_", 96 * KB, [128, 8 * S], BF16)
        wo = p.at("wo_", 128 * KB, [128, 8 * D], BF16)
        qf = p.at("qf_", 144 * KB, [128, 8 * S], BF16)
        boutb = p.at("boutb_", 144 * KB, [128, D], F32)
        zt = p.at("zt_", 176 * KB, [128, 2 * D], BF16)
        xo = p.at("xo_", 180 * KB, [128, 2 * D], F32)
        x2 = p.at("x2_", 0, [128, T * D], F32)

        stats = p.sb("stats", [128, 8 * T], F32)
        bqk_r = p.sb("bqk_r", [24, 128], F32)
        bqk = p.sb("bqk", [128, 32], F32)
        nrm = p.sb("nrm", [1, 16], F32)
        negMb = p.sb("negMb", [128, 1], F32)
        den_sb = p.sb("den_sb", [1, 512], F32)
        rd_row = p.sb("rd_row", [1, 512], F32)
        bout_bf = p.sb("bout_bf", [1, D], BF16)
        bvb = p.sb("bvb", [128, 8], BF16)
        bvwo = p.sb("bvwo", [1, D], BF16)

        ps_a = p.ps("ps_a", [128, 512])
        ps_b = p.ps("ps_b", [128, 512])
        ps_c = p.ps("ps_c", [128, 512])
        ps_d = p.ps("ps_d", [128, 512])
        ps_o = p.ps("ps_o", [128, 1024])
        ps_den = p.ps("ps_den", [1, 512])
        ps_bc = p.ps("ps_bc", [128, 512])

        ld = p.sem("ld")
        lg = p.sem("lg")
        tx = p.sem("tx")
        mm = p.sem("mm")
        ak = p.sem("ak")
        vk = p.sem("vk")
        st = p.sem("st")
        s2 = p.sem("s2")

        with nc.Block() as blk:
            @blk.sync
            def _(sync):
                p.emit_const_loads(sync, ld)
                for t in range(T):
                    ld.inc(sync.dma_start(out=x_sb[:, t * D:(t + 1) * D],
                                          in_=x_h[t * 128:(t + 1) * 128, :]), 16)
                ld.inc(sync.dma_start(out=bqk_r[:, :],
                                      in_=bqkv_h.rearrange("(a b) -> a b", b=128)), 16)

            @blk.gpsimd
            def _(g):
                for c in range(8):
                    lg.inc(g.dma_start(out=wqkv[:, c * 3072:(c + 1) * 3072],
                                       in_=wqkv_h[c * 128:(c + 1) * 128, :]), 16)
                lg.inc(g.dma_start(out=bout_bf[:, :], in_=bout_h[None, :]), 16)
        ld_all = ld.n
        lg_w = lg.n

        with nc.Block() as blk:
            emit_ln(p, blk, x_sb=x_sb, zt=zt, zf=zf, stats=stats,
                    ld=ld, ak=ak, vk=vk, tx=tx, wait_ld=ld_all, eps_ap=p.fc(1e-5))
        tx_zf = tx.n

        with nc.Block() as blk:
            mm0, ak0, vk0 = mm.n, ak.n, vk.n

            @blk.tensor
            def _(t):
                t.wait_ge(ld.h, ld_all)
                mm.inc(t.transpose(ps_bc[:, 0:24], bqk_r[:, :], p.const["identf"][0:24, 0:24]))
                t.wait_ge(tx.h, tx_zf)
                t.wait_ge(lg.h, lg_w)
                g = 0
                for proj in range(2):
                    for m_ in range(8):
                        for tc in range(4):
                            g += 1
                            psd = [ps_a, ps_b][g % 2]
                            if g > 2:
                                t.wait_ge(ak.h, ak0 + 2 + g - 2)
                            for c in range(8):
                                inst = t.matmul(
                                    psd[:, :],
                                    wqkv[:, c * 3072 + proj * 1024 + m_ * 128:
                                         c * 3072 + proj * 1024 + m_ * 128 + 128],
                                    zf[:, c * S + tc * 512:c * S + tc * 512 + 512],
                                    start=(c == 0), stop=(c == 7))
                            mm.inc(inst)
                for tt1 in range(1, T + 1):
                    if tt1 >= 2:
                        t.wait_ge(ak.h, ak0 + 66 + tt1 - 1)
                    for c in range(8):
                        for hh in range(2):
                            inst = t.matmul(
                                ps_o[:, hh * 512:(hh + 1) * 512],
                                zf[:, c * S + (tt1 - 1) * 128:c * S + (tt1 - 1) * 128 + 128],
                                wqkv[:, c * 3072 + 2048 + hh * 512:
                                     c * 3072 + 2048 + hh * 512 + 512],
                                start=(c == 0), stop=(c == 7))
                    mm.inc(inst)
                k = 0
                for which in range(2):
                    for nn_ in range(4):
                        ng = which * 4 + nn_ + 1
                        if ng >= 2:
                            t.wait_ge(vk.h, vk0 + ng - 1)
                        for c in range(8):
                            k += 1
                            t.wait_ge(ak.h, ak0 + 82 + k)
                            inst = t.matmul(ps_den[:, :], p.const["ones_col_b"][:, :],
                                            sqq[:, (c % 2) * 512:(c % 2) * 512 + 512],
                                            start=(c == 0), stop=(c == 7))
                            mm.inc(inst)
                t.wait_ge(vk.h, vk0 + 10)
                mm.inc(t.matmul(ps_bc[:, 100:101], p.const["ones_row_f"][:, :],
                                nrm[:, 8:9], start=True, stop=True))

            @blk.scalar
            def _(s):
                s.wait_ge(mm.h, mm0 + 1)
                ak.inc(s.activation(out=bqk[:, 8:32], in_=ps_bc[:, 0:24], func=AF.Copy))
                ak.inc(s.activation(out=bqk[:, 0:8], in_=ps_bc[:, 0:8], func=AF.Copy,
                                    scale=1.0 / 32.0))
                s.wait_ge(ak.h, ak0 + 2)
                g = 0
                for proj in range(2):
                    dst = qf if proj == 0 else kf
                    for m_ in range(8):
                        for tc in range(4):
                            g += 1
                            psd = [ps_a, ps_b][g % 2]
                            s.wait_ge(mm.h, mm0 + 1 + g)
                            ak.inc(s.activation(
                                out=dst[:, m_ * S + tc * 512:m_ * S + tc * 512 + 512],
                                in_=psd[:, :], func=AF.Identity,
                                scale=(1.0 / 32.0 if proj == 0 else 1.0),
                                bias=bqk[:, (0 if proj == 0 else 16) + m_:
                                         (0 if proj == 0 else 16) + m_ + 1]))
                for tt1 in range(1, T + 1):
                    s.wait_ge(mm.h, mm0 + 65 + tt1)
                    ak.inc(s.activation(out=vt[:, (tt1 - 1) * D:tt1 * D], in_=ps_o[:, :],
                                        func=AF.Copy))
                s.wait_ge(ak.h, ak0 + 82)
                k = 0
                for which in range(2):
                    src = qf if which == 0 else kf
                    for nn_ in range(4):
                        for c in range(8):
                            k += 1
                            if k > 2:
                                s.wait_ge(mm.h, mm0 + 81 + k - 2)
                            ak.inc(s.activation(
                                out=sqq[:, (c % 2) * 512:(c % 2) * 512 + 512],
                                in_=src[:, c * S + nn_ * 512:c * S + nn_ * 512 + 512],
                                func=AF.Square))
                s.wait_ge(vk.h, vk0 + 9)
                ak.inc(s.activation(out=nrm[:, 9:10], in_=nrm[:, 10:11], func=AF.Sqrt))
                ak.inc(s.activation(out=nrm[:, 11:12], in_=nrm[:, 12:13], func=AF.Sqrt))
                s.wait_ge(mm.h, mm0 + 146)
                ak.inc(s.activation(out=negMb[:, :], in_=ps_bc[:, 100:101], func=AF.Copy,
                                    scale=-1.0))

            @blk.vector
            def _(v):
                for ng in range(1, 9):
                    v.wait_ge(mm.h, mm0 + 81 + 8 * ng)
                    vk.inc(v.reduce_max(nrm[:, ng - 1:ng], ps_den[:, :], axis=AX.X))
                v.wait_ge(vk.h, vk0 + 8)
                v.reduce_max(nrm[:, 10:11], nrm[:, 0:4], axis=AX.X)
                vk.inc(v.reduce_max(nrm[:, 12:13], nrm[:, 4:8], axis=AX.X))
                v.wait_ge(ak.h, ak0 + 148)
                vk.inc(v.tensor_mul(nrm[:, 8:9], nrm[:, 9:10], nrm[:, 11:12]))
        ak_M = ak.n

        with nc.Block() as blk:
            mm0, ak0, vk0 = mm.n, ak.n, vk.n
            NT, NA, NV = 41, 26, 9

            @blk.gpsimd
            def _(g):
                g.wait_ge(mm.h, mm0)
                for c in range(8):
                    lg.inc(g.dma_start(out=wo[:, c * D:(c + 1) * D],
                                       in_=wout_h[c * 128:(c + 1) * 128, :]), 16)

            @blk.tensor
            def _(t):
                t.wait_ge(ak.h, ak_M)
                for qc in range(4):
                    mmB, akB, vkB = mm0 + NT * qc, ak0 + NA * qc, vk0 + NV * qc
                    q0 = qc * 512
                    for kt in range(16):
                        psd = [ps_a, ps_b][kt % 2]
                        if kt >= 2:
                            t.wait_ge(ak.h, akB + kt - 1)
                        elif qc > 0:
                            t.wait_ge(ak.h, akB - 10)
                        for c in range(8):
                            inst = t.matmul(
                                psd[:, :],
                                kf[:, c * S + kt * 128:c * S + kt * 128 + 128],
                                qf[:, c * S + q0:c * S + q0 + 512],
                                start=(c == 0), stop=(c == 7))
                        mm.inc(inst)
                    for kt in range(16):
                        t.wait_ge(ak.h, akB + kt + 1)
                        mm.inc(t.matmul(ps_den[:, :], p.const["ones_col_b"][:, :],
                                        es[:, kt * 512:(kt + 1) * 512],
                                        start=(kt == 0), stop=(kt == 15)))
                    t.wait_ge(vk.h, vkB + 1)
                    mm.inc(t.matmul(ps_bc[:, :], p.const["ones_row_f"][:, :],
                                    rd_row[:, :], start=True, stop=True))
                    for m_ in range(8):
                        psd = [ps_c, ps_d][m_ % 2]
                        if m_ >= 2:
                            t.wait_ge(vk.h, vkB + 1 + m_ - 1)
                        elif qc > 0:
                            t.wait_ge(vk.h, vkB)
                        for kt in range(16):
                            inst = t.matmul(
                                psd[:, :],
                                vt[:, kt * D + m_ * 128:kt * D + m_ * 128 + 128],
                                es[:, kt * 512:(kt + 1) * 512],
                                start=(kt == 0), stop=(kt == 15))
                        mm.inc(inst)

            @blk.scalar
            def _(s):
                for qc in range(4):
                    mmB, akB, vkB = mm0 + NT * qc, ak0 + NA * qc, vk0 + NV * qc
                    for kt in range(16):
                        psd = [ps_a, ps_b][kt % 2]
                        s.wait_ge(mm.h, mmB + kt + 1)
                        ak.inc(s.activation(out=es[:, kt * 512:(kt + 1) * 512],
                                            in_=psd[:, :], func=AF.Exp,
                                            bias=negMb[:, :]))
                    s.wait_ge(mm.h, mmB + 32)
                    ak.inc(s.activation(out=den_sb[:, :], in_=ps_den[:, :], func=AF.Copy))
                    s.wait_ge(mm.h, mmB + 33)
                    if qc > 0:
                        s.wait_ge(vk.h, vkB)
                    ak.inc(s.activation(out=rdb[:, :], in_=ps_bc[:, :], func=AF.Copy))
                    for m_ in range(8):
                        s.wait_ge(vk.h, vkB + 1 + m_ + 1)
                        ak.inc(s.activation(
                            out=ctxn[:, m_ * S + qc * 512:m_ * S + qc * 512 + 512],
                            in_=t1[:, (m_ % 2) * 512:(m_ % 2) * 512 + 512],
                            func=AF.Copy))

            @blk.vector
            def _(v):
                for qc in range(4):
                    mmB, akB, vkB = mm0 + NT * qc, ak0 + NA * qc, vk0 + NV * qc
                    v.wait_ge(ak.h, akB + 17)
                    vk.inc(v.reciprocal(rd_row[:, :], den_sb[:, :]))
                    for m_ in range(8):
                        psd = [ps_c, ps_d][m_ % 2]
                        v.wait_ge(mm.h, mmB + 33 + m_ + 1)
                        v.wait_ge(ak.h, akB + 18)
                        if m_ >= 2:
                            v.wait_ge(ak.h, akB + 18 + m_ - 1)
                        vk.inc(v.tensor_mul(t1[:, (m_ % 2) * 512:(m_ % 2) * 512 + 512],
                                            psd[:, :], rdb[:, :]))
        ak_p3 = ak.n
        vk_p3 = vk.n

        lg_all = lg.n
        with nc.Block() as blk:
            mm0, ak0, vk0, ld0, st0 = mm.n, ak.n, vk.n, ld.n, st.n

            @blk.sync
            def _(sync):
                sync.wait_ge(ak.h, ak_p3)
                for tt in range(T):
                    ld.inc(sync.dma_start(out=x2[:, tt * D:(tt + 1) * D],
                                          in_=x_h[tt * 128:(tt + 1) * 128, :]), 16)

            @blk.scalar
            def _(s):
                ak.inc(s.activation(out=bvb[:, :], in_=bqk[:, 24:32], func=AF.Copy))
                s.wait_ge(mm.h, mm0 + 1)
                ak.inc(s.activation(out=bvwo[:, 0:512], in_=ps_den[:, :], func=AF.Copy))
                s.wait_ge(mm.h, mm0 + 2)
                ak.inc(s.activation(out=bvwo[:, 512:1024], in_=ps_den[:, :], func=AF.Copy))
                s.wait_ge(mm.h, mm0 + 3)
                ak.inc(s.activation(out=boutb[:, 0:512], in_=ps_c[:, :], func=AF.Copy))
                s.wait_ge(mm.h, mm0 + 4)
                ak.inc(s.activation(out=boutb[:, 512:1024], in_=ps_d[:, :], func=AF.Copy))

            @blk.tensor
            def _(t):
                t.wait_ge(ak.h, ak0 + 1)
                t.wait_ge(lg.h, lg_all)
                t.wait_ge(vk.h, vk_p3)
                for hh in range(2):
                    if hh == 1:
                        t.wait_ge(ak.h, ak0 + 2)
                    for m_ in range(8):
                        inst = t.matmul(ps_den[:, :], bvb[:, m_:m_ + 1],
                                        wo[:, m_ * D + hh * 512:m_ * D + hh * 512 + 512],
                                        start=(m_ == 0), stop=(m_ == 7))
                    mm.inc(inst)
                for hh, psd in ((0, ps_c), (1, ps_d)):
                    t.wait_ge(ak.h, ak0 + 3)
                    t.matmul(psd[:, :], p.const["ones_row_b"][:, :],
                             bout_bf[:, hh * 512:hh * 512 + 512], start=True, stop=False)
                    inst = t.matmul(psd[:, :], p.const["ones_row_b"][:, :],
                                    bvwo[:, hh * 512:hh * 512 + 512], start=False, stop=True)
                    mm.inc(inst)
                for tt1 in range(1, T + 1):
                    if tt1 >= 2:
                        t.wait_ge(vk.h, vk0 + 2 * (tt1 - 1) - 1)
                    for m_ in range(8):
                        for hh in range(2):
                            inst = t.matmul(
                                ps_o[:, hh * 512:(hh + 1) * 512],
                                ctxn[:, m_ * S + (tt1 - 1) * 128:m_ * S + (tt1 - 1) * 128 + 128],
                                wo[:, m_ * D + hh * 512:m_ * D + hh * 512 + 512],
                                start=(m_ == 0), stop=(m_ == 7))
                    mm.inc(inst)

            @blk.vector
            def _(v):
                v.wait_ge(ak.h, ak0 + 5)
                v.wait_ge(ld.h, ld0 + 16 * T)
                for tt in range(T):
                    v.wait_ge(mm.h, mm0 + 4 + tt + 1)
                    if tt >= 2:
                        psem = [st, s2][tt % 2]
                        v.wait_ge(psem.h, 16 * ((tt - 2) // 2 + 1))
                    xoh = xo[:, (tt % 2) * D:(tt % 2) * D + D]
                    vk.inc(v.tensor_add(xoh, ps_o[:, :], boutb[:, :]))
                    v.wait_ge(vk.h, vk0 + 2 * tt + 1)
                    vk.inc(v.tensor_add(xoh, xoh, x2[:, tt * D:(tt + 1) * D]))

            @blk.gpsimd
            def _(g):
                for tt in range(T):
                    g.wait_ge(vk.h, vk0 + 2 * (tt + 1))
                    psem = [st, s2][tt % 2]
                    psem.inc(g.dma_start(out=xout_h[tt * 128:(tt + 1) * 128, :],
                                         in_=xo[:, (tt % 2) * D:(tt % 2) * D + D]), 16)
                g.wait_ge(st.h, st.n)
                g.wait_ge(s2.h, s2.n)
    return p



def build_l1_attn(debug=False):
    p = Prog("l1attn")
    nc = p.nc
    x_h = nc.dram_tensor("x", [S, D], F32, kind="ExternalInput").ap()
    wqkv_h = nc.dram_tensor("WqkvT", [D, 3 * D], F32, kind="ExternalInput").ap()
    bqkv_h = nc.dram_tensor("b_qkv", [3 * D], F32, kind="ExternalInput").ap()
    wout_h = nc.dram_tensor("WoT", [D, D], F32, kind="ExternalInput").ap()
    bout_h = nc.dram_tensor("b_out", [D], F32, kind="ExternalInput").ap()
    dd_h = nc.dram_tensor("dd", [1, 1], F32, kind="ExternalInput").ap()
    xout_h = nc.dram_tensor("xout", [S, D], F32, kind="ExternalOutput").ap()
    dbg_h = {}
    if debug:
        for nm, shape, dt in [("qf", [128, 8 * S], BF16), ("kf", [128, 8 * S], BF16),
                              ("vt", [128, T * D], BF16), ("ctxn", [128, 8 * S], BF16),
                              ("negMb", [128, 16], F32), ("es", [128, 16 * 512], BF16),
                              ("decay", [128, 4096], BF16)]:
            dbg_h[nm] = nc.dram_tensor("dbg_" + nm, shape, dt, kind="ExternalOutput").ap()

    with p.stack:
        p.decl_consts(["identf", "ones_col_b", "ones_row_b", "ones_row_f",
                       "dist2d", "hsel"])
        p.decl_fvals([1e-5])
        p.arena(188 * KB)
        x_sb = p.at("x_sb", 0, [128, T * D], F32)
        kf = p.at("kf_", 0, [128, 8 * S], BF16)
        vt = p.at("vt_", 32 * KB, [128, T * D], BF16)
        zf = p.at("zf_", 64 * KB, [128, 8 * S], BF16)
        ctxn = p.at("ctxn_", 64 * KB, [128, 8 * S], BF16)
        wqkv = p.at("wqkv_", 96 * KB, [128, 8 * 3 * D], BF16)
        es = p.at("es_", 96 * KB, [128, 16 * 512], BF16)
        wo = p.at("wo_", 112 * KB, [128, 8 * D], BF16)
        sd = p.at("sd_", 128 * KB, [128, 2 * 512], F32)
        t1 = p.at("t1_", 132 * KB, [128, 512], F32)
        rdb = p.at("rdb_", 134 * KB, [128, 512], F32)
        sqq = p.at("sqq_", 136 * KB, [128, 2 * 512], BF16)
        qf = p.at("qf_", 144 * KB, [128, 8 * S], BF16)
        xt2 = p.at("xt2_", 144 * KB, [128, 2 * D], F32)
        xo = p.at("xo_", 152 * KB, [128, 2 * D], F32)
        boutb = p.at("boutb_", 160 * KB, [128, D], F32)
        decay2d = p.at("decay2d_", 176 * KB, [128, 4096], BF16)
        zt = p.at("zt_", 184 * KB, [128, 2 * D], BF16)
        x2 = p.at("x2_", 0, [128, T * D], F32)

        stats = p.sb("stats", [128, 8 * T], F32)
        bqk_r = p.sb("bqk_r", [24, 128], F32)
        bqk = p.sb("bqk", [128, 32], F32)
        nrm = p.sb("nrm", [16, 24], F32)
        negMb = p.sb("negMb", [128, 16], F32)
        den_sb = p.sb("den_sb", [1, 512], F32)
        rd_row = p.sb("rd_row", [1, 512], F32)
        bout_bf = p.sb("bout_bf", [1, D], BF16)
        bvb = p.sb("bvb", [128, 8], BF16)
        bvwo = p.sb("bvwo", [1, D], BF16)
        mrow = p.sb("mrow", [1, 16], F32)
        dd_sb = p.sb("dd_sb", [1, 1], F32)
        aa = p.sb("aa", [1, 1], F32)
        nab = p.sb("nab", [128, 1], F32)

        ps_a = p.ps("ps_a", [128, 512])
        ps_b = p.ps("ps_b", [128, 512])
        ps_c = p.ps("ps_c", [128, 512])
        ps_d = p.ps("ps_d", [128, 512])
        ps_o = p.ps("ps_o", [128, 1024])
        ps_den = p.ps("ps_den", [16, 512])
        ps_bc = p.ps("ps_bc", [128, 512])

        ld = p.sem("ld")
        lg = p.sem("lg")
        tx = p.sem("tx")
        mm = p.sem("mm")
        ak = p.sem("ak")
        vk = p.sem("vk")
        st = p.sem("st")
        s2 = p.sem("s2")

        with nc.Block() as blk:
            @blk.sync
            def _(sync):
                p.emit_const_loads(sync, ld)
                for t in range(T):
                    ld.inc(sync.dma_start(out=x_sb[:, t * D:(t + 1) * D],
                                          in_=x_h[t * 128:(t + 1) * 128, :]), 16)
                ld.inc(sync.dma_start(out=bqk_r[:, :],
                                      in_=bqkv_h.rearrange("(a b) -> a b", b=128)), 16)
                ld.inc(sync.dma_start(out=dd_sb[:, :], in_=dd_h[:, :]), 16)

            @blk.gpsimd
            def _(g):
                for c in range(8):
                    lg.inc(g.dma_start(out=wqkv[:, c * 3072:(c + 1) * 3072],
                                       in_=wqkv_h[c * 128:(c + 1) * 128, :]), 16)
                lg.inc(g.dma_start(out=bout_bf[:, :], in_=bout_h[None, :]), 16)
        ld_all = ld.n
        lg_w = lg.n

        with nc.Block() as blk:
            emit_ln(p, blk, x_sb=x_sb, zt=zt, zf=zf, stats=stats,
                    ld=ld, ak=ak, vk=vk, tx=tx, wait_ld=ld_all, eps_ap=p.fc(1e-5))
            a_abs = ak.n + 1
            m_bc = mm.n + 1
            v_nab = vk.n + 1

            @blk.scalar
            def _(s):
                ak.inc(s.activation(out=aa[:, :], in_=dd_sb[:, :], func=AF.Abs))

            @blk.tensor
            def _(t):
                t.wait_ge(ak.h, a_abs)
                mm.inc(t.matmul(ps_bc[:, 0:1], p.const["ones_row_f"][:, :], aa[:, :],
                                start=True, stop=True))

            @blk.vector
            def _(v):
                v.wait_ge(mm.h, m_bc)
                vk.inc(v.tensor_scalar(nab[:, :], ps_bc[:, 0:1], -1.0, None, op0=ALU.mult))

            @blk.scalar
            def _(s):
                s.wait_ge(vk.h, v_nab)
                ak.inc(s.activation(out=decay2d[:, :], in_=p.const["dist2d"][:, :],
                                    func=AF.Exp, scale=nab[:, :]))
            a_dv = ak.n
        tx_zf = tx.n

        with nc.Block() as blk:
            mm0, ak0, vk0 = mm.n, ak.n, vk.n

            @blk.tensor
            def _(t):
                t.wait_ge(ld.h, ld_all)
                t.wait_ge(vk.h, v_nab)
                mm.inc(t.transpose(ps_bc[:, 0:24], bqk_r[:, :], p.const["identf"][0:24, 0:24]))
                t.wait_ge(tx.h, tx_zf)
                t.wait_ge(lg.h, lg_w)
                g = 0
                for proj in range(2):
                    for m_ in range(8):
                        for tc in range(4):
                            g += 1
                            psd = [ps_a, ps_b][g % 2]
                            if g > 2:
                                t.wait_ge(ak.h, ak0 + 2 + g - 2)
                            for c in range(8):
                                inst = t.matmul(
                                    psd[:, :],
                                    wqkv[:, c * 3072 + proj * 1024 + m_ * 128:
                                         c * 3072 + proj * 1024 + m_ * 128 + 128],
                                    zf[:, c * S + tc * 512:c * S + tc * 512 + 512],
                                    start=(c == 0), stop=(c == 7))
                            mm.inc(inst)
                for tt1 in range(1, T + 1):
                    if tt1 >= 2:
                        t.wait_ge(ak.h, ak0 + 66 + tt1 - 1)
                    for c in range(8):
                        for hh in range(2):
                            inst = t.matmul(
                                ps_o[:, hh * 512:(hh + 1) * 512],
                                zf[:, c * S + (tt1 - 1) * 128:c * S + (tt1 - 1) * 128 + 128],
                                wqkv[:, c * 3072 + 2048 + hh * 512:
                                     c * 3072 + 2048 + hh * 512 + 512],
                                start=(c == 0), stop=(c == 7))
                    mm.inc(inst)
                k = 0
                for which in range(2):
                    for nn_ in range(4):
                        ng = which * 4 + nn_ + 1
                        if ng >= 2:
                            t.wait_ge(vk.h, vk0 + ng - 1)
                        for c in range(8):
                            k += 1
                            t.wait_ge(ak.h, ak0 + 82 + k)
                            inst = t.matmul(ps_den[:, :], p.const["hsel"][:, 16 * c:16 * c + 16],
                                            sqq[:, (c % 2) * 512:(c % 2) * 512 + 512],
                                            start=(c == 0), stop=(c == 7))
                            mm.inc(inst)
                t.wait_ge(vk.h, vk0 + 11)
                mm.inc(t.transpose(ps_bc[0:1, 32:48], nrm[:, 12:13],
                                   p.const["identf"][0:16, 0:16]))
                t.wait_ge(ak.h, ak0 + 149)
                mm.inc(t.matmul(ps_bc[:, 64:80], p.const["ones_row_f"][:, :],
                                mrow[:, :], start=True, stop=True))

            @blk.scalar
            def _(s):
                s.wait_ge(mm.h, mm0 + 1)
                ak.inc(s.activation(out=bqk[:, 8:32], in_=ps_bc[:, 0:24], func=AF.Copy))
                ak.inc(s.activation(out=bqk[:, 0:8], in_=ps_bc[:, 0:8], func=AF.Copy,
                                    scale=1.0 / 8.0))
                s.wait_ge(ak.h, ak0 + 2)
                g = 0
                for proj in range(2):
                    dst = qf if proj == 0 else kf
                    for m_ in range(8):
                        for tc in range(4):
                            g += 1
                            psd = [ps_a, ps_b][g % 2]
                            s.wait_ge(mm.h, mm0 + 1 + g)
                            ak.inc(s.activation(
                                out=dst[:, m_ * S + tc * 512:m_ * S + tc * 512 + 512],
                                in_=psd[:, :], func=AF.Identity,
                                scale=(1.0 / 8.0 if proj == 0 else 1.0),
                                bias=bqk[:, (0 if proj == 0 else 16) + m_:
                                         (0 if proj == 0 else 16) + m_ + 1]))
                for tt1 in range(1, T + 1):
                    s.wait_ge(mm.h, mm0 + 65 + tt1)
                    ak.inc(s.activation(out=vt[:, (tt1 - 1) * D:tt1 * D], in_=ps_o[:, :],
                                        func=AF.Copy))
                s.wait_ge(ak.h, ak0 + 82)
                k = 0
                for which in range(2):
                    src = qf if which == 0 else kf
                    for nn_ in range(4):
                        for c in range(8):
                            k += 1
                            if k > 2:
                                s.wait_ge(mm.h, mm0 + 81 + k - 2)
                            ak.inc(s.activation(
                                out=sqq[:, (c % 2) * 512:(c % 2) * 512 + 512],
                                in_=src[:, c * S + nn_ * 512:c * S + nn_ * 512 + 512],
                                func=AF.Square))
                s.wait_ge(vk.h, vk0 + 10)
                ak.inc(s.activation(out=nrm[:, 16:17], in_=nrm[:, 8:9], func=AF.Sqrt))
                ak.inc(s.activation(out=nrm[:, 17:18], in_=nrm[:, 9:10], func=AF.Sqrt))
                s.wait_ge(mm.h, mm0 + 146)
                ak.inc(s.activation(out=mrow[:, :], in_=ps_bc[0:1, 32:48], func=AF.Copy))
                s.wait_ge(mm.h, mm0 + 147)
                ak.inc(s.activation(out=negMb[:, :], in_=ps_bc[:, 64:80], func=AF.Copy,
                                    scale=-1.0))

            @blk.vector
            def _(v):
                for ng in range(1, 9):
                    v.wait_ge(mm.h, mm0 + 81 + 8 * ng)
                    vk.inc(v.reduce_max(nrm[:, ng - 1:ng], ps_den[:, :], axis=AX.X))
                v.wait_ge(vk.h, vk0 + 8)
                vk.inc(v.reduce_max(nrm[:, 8:9], nrm[:, 0:4], axis=AX.X))
                vk.inc(v.reduce_max(nrm[:, 9:10], nrm[:, 4:8], axis=AX.X))
                v.wait_ge(ak.h, ak0 + 148)
                vk.inc(v.tensor_mul(nrm[:, 12:13], nrm[:, 16:17], nrm[:, 17:18]))
        ak_M = ak.n
        mm_p2 = mm.n

        with nc.Block() as blk:
            mm0, ak0, vk0 = mm.n, ak.n, vk.n
            NT, NA, NV = 34, 19, 18

            @blk.gpsimd
            def _(g):
                g.wait_ge(mm.h, mm0)
                for c in range(8):
                    lg.inc(g.dma_start(out=wo[:, c * D:(c + 1) * D],
                                       in_=wout_h[c * 128:(c + 1) * 128, :]), 16)

            @blk.tensor
            def _(t):
                t.wait_ge(ak.h, ak_M)
                for qc in range(4):
                    q0 = qc * 512
                    for h in range(16):
                        it = qc * 16 + h
                        mmB, akB, vkB = mm0 + NT * it, ak0 + NA * it, vk0 + NV * it
                        ch = h // 2
                        r0 = 64 * (h % 2)
                        for kt in range(16):
                            psd = [ps_a, ps_b][kt % 2]
                            if kt >= 2:
                                t.wait_ge(vk.h, vkB + kt - 1)
                            elif it > 0:
                                t.wait_ge(vk.h, vkB - 4 + kt)
                            inst = t.matmul(
                                psd[:, :],
                                kf[r0:r0 + 64, ch * S + kt * 128:ch * S + kt * 128 + 128],
                                qf[r0:r0 + 64, ch * S + q0:ch * S + q0 + 512],
                                start=True, stop=True)
                            mm.inc(inst)
                        for kt in range(16):
                            t.wait_ge(ak.h, akB + kt + 1)
                            mm.inc(t.matmul(ps_den[0:1, :], p.const["ones_col_b"][:, :],
                                            es[:, kt * 512:(kt + 1) * 512],
                                            start=(kt == 0), stop=(kt == 15)))
                        t.wait_ge(vk.h, vkB + 17)
                        mm.inc(t.matmul(ps_bc[:, :], p.const["ones_row_f"][:, :],
                                        rd_row[:, :], start=True, stop=True))
                        psd = [ps_c, ps_d][it % 2]
                        if it >= 2:
                            t.wait_ge(vk.h, vkB - NV)
                        for kt in range(16):
                            inst = t.matmul(
                                psd[0:64, :],
                                vt[:, kt * D + 64 * h:kt * D + 64 * h + 64],
                                es[:, kt * 512:(kt + 1) * 512],
                                start=(kt == 0), stop=(kt == 15))
                        mm.inc(inst)

            @blk.vector
            def _(v):
                v.wait_ge(ak.h, a_dv)
                for qc in range(4):
                    for h in range(16):
                        it = qc * 16 + h
                        mmB, akB, vkB = mm0 + NT * it, ak0 + NA * it, vk0 + NV * it
                        for kt in range(16):
                            psd = [ps_a, ps_b][kt % 2]
                            v.wait_ge(mm.h, mmB + kt + 1)
                            if kt >= 2:
                                v.wait_ge(ak.h, akB + kt - 1)
                            elif it > 0:
                                v.wait_ge(ak.h, akB - 5 + kt)
                            off = 2048 + 128 * (4 * qc - kt)
                            vk.inc(v.tensor_mul(sd[:, (kt % 2) * 512:(kt % 2) * 512 + 512],
                                                psd[:, :],
                                                decay2d[:, off:off + 512]))
                        v.wait_ge(ak.h, akB + 17)
                        vk.inc(v.reciprocal(rd_row[:, :], den_sb[:, :]))
                        v.wait_ge(mm.h, mmB + 34)
                        v.wait_ge(ak.h, akB + 18)
                        if it >= 1:
                            v.wait_ge(ak.h, akB)
                        psd = [ps_c, ps_d][it % 2]
                        vk.inc(v.tensor_mul(t1[0:64, :], psd[0:64, :], rdb[0:64, :]))

            @blk.scalar
            def _(s):
                for qc in range(4):
                    for h in range(16):
                        it = qc * 16 + h
                        mmB, akB, vkB = mm0 + NT * it, ak0 + NA * it, vk0 + NV * it
                        ch = h // 2
                        r0 = 64 * (h % 2)
                        for kt in range(16):
                            s.wait_ge(vk.h, vkB + kt + 1)
                            ak.inc(s.activation(out=es[:, kt * 512:(kt + 1) * 512],
                                                in_=sd[:, (kt % 2) * 512:(kt % 2) * 512 + 512],
                                                func=AF.Exp, bias=negMb[:, h:h + 1]))
                        s.wait_ge(mm.h, mmB + 32)
                        ak.inc(s.activation(out=den_sb[:, :], in_=ps_den[0:1, :], func=AF.Copy))
                        s.wait_ge(mm.h, mmB + 33)
                        if it > 0:
                            s.wait_ge(vk.h, vkB)
                        ak.inc(s.activation(out=rdb[:, :], in_=ps_bc[:, :], func=AF.Copy))
                        s.wait_ge(vk.h, vkB + 18)
                        ak.inc(s.activation(
                            out=ctxn[r0:r0 + 64, ch * S + qc * 512:ch * S + qc * 512 + 512],
                            in_=t1[0:64, :], func=AF.Copy))
        ak_p3 = ak.n
        vk_p3 = vk.n

        lg_all = lg.n
        with nc.Block() as blk:
            mm0, ak0, vk0, ld0, st0 = mm.n, ak.n, vk.n, ld.n, st.n

            @blk.sync
            def _(sync):
                sync.wait_ge(ak.h, ak_p3)
                for tt in range(T):
                    ld.inc(sync.dma_start(out=x2[:, tt * D:(tt + 1) * D],
                                          in_=x_h[tt * 128:(tt + 1) * 128, :]), 16)

            @blk.scalar
            def _(s):
                s.wait_ge(ak.h, ak_p3)
                ak.inc(s.activation(out=bvb[:, :], in_=bqk[:, 24:32], func=AF.Copy))
                s.wait_ge(mm.h, mm0 + 1)
                ak.inc(s.activation(out=bvwo[:, 0:512], in_=ps_den[0:1, :], func=AF.Copy))
                s.wait_ge(mm.h, mm0 + 2)
                ak.inc(s.activation(out=bvwo[:, 512:1024], in_=ps_den[0:1, :], func=AF.Copy))
                s.wait_ge(mm.h, mm0 + 3)
                ak.inc(s.activation(out=boutb[:, 0:512], in_=ps_c[:, :], func=AF.Copy))
                s.wait_ge(mm.h, mm0 + 4)
                ak.inc(s.activation(out=boutb[:, 512:1024], in_=ps_d[:, :], func=AF.Copy))

            @blk.tensor
            def _(t):
                t.wait_ge(ak.h, ak0 + 1)
                t.wait_ge(lg.h, lg_all)
                t.wait_ge(vk.h, vk_p3)
                for hh in range(2):
                    if hh == 1:
                        t.wait_ge(ak.h, ak0 + 2)
                    for m_ in range(8):
                        inst = t.matmul(ps_den[0:1, :], bvb[:, m_:m_ + 1],
                                        wo[:, m_ * D + hh * 512:m_ * D + hh * 512 + 512],
                                        start=(m_ == 0), stop=(m_ == 7))
                    mm.inc(inst)
                for hh, psd in ((0, ps_c), (1, ps_d)):
                    t.wait_ge(ak.h, ak0 + 3)
                    t.matmul(psd[:, :], p.const["ones_row_b"][:, :],
                             bout_bf[:, hh * 512:hh * 512 + 512], start=True, stop=False)
                    inst = t.matmul(psd[:, :], p.const["ones_row_b"][:, :],
                                    bvwo[:, hh * 512:hh * 512 + 512], start=False, stop=True)
                    mm.inc(inst)
                for tt1 in range(1, T + 1):
                    if tt1 >= 2:
                        t.wait_ge(vk.h, vk0 + 2 * (tt1 - 1) - 1)
                    for m_ in range(8):
                        for hh in range(2):
                            inst = t.matmul(
                                ps_o[:, hh * 512:(hh + 1) * 512],
                                ctxn[:, m_ * S + (tt1 - 1) * 128:m_ * S + (tt1 - 1) * 128 + 128],
                                wo[:, m_ * D + hh * 512:m_ * D + hh * 512 + 512],
                                start=(m_ == 0), stop=(m_ == 7))
                    mm.inc(inst)

            @blk.vector
            def _(v):
                v.wait_ge(ak.h, ak0 + 5)
                v.wait_ge(ld.h, ld0 + 16 * T)
                for tt in range(T):
                    v.wait_ge(mm.h, mm0 + 4 + tt + 1)
                    if tt >= 2:
                        psem = [st, s2][tt % 2]
                        v.wait_ge(psem.h, 16 * ((tt - 2) // 2 + 1))
                    xoh = xo[:, (tt % 2) * D:(tt % 2) * D + D]
                    vk.inc(v.tensor_add(xoh, ps_o[:, :], boutb[:, :]))
                    v.wait_ge(vk.h, vk0 + 2 * tt + 1)
                    vk.inc(v.tensor_add(xoh, xoh, x2[:, tt * D:(tt + 1) * D]))

            @blk.gpsimd
            def _(g):
                for tt in range(T):
                    g.wait_ge(vk.h, vk0 + 2 * (tt + 1))
                    psem = [st, s2][tt % 2]
                    psem.inc(g.dma_start(out=xout_h[tt * 128:(tt + 1) * 128, :],
                                         in_=xo[:, (tt % 2) * D:(tt % 2) * D + D]), 16)
                if dbg_h:
                    for nm, buf in [("qf", qf), ("kf", kf), ("vt", vt), ("ctxn", ctxn),
                                    ("negMb", negMb), ("es", es), ("decay", decay2d)]:
                        st.inc(g.dma_start(out=dbg_h[nm][:, :], in_=buf[:, :]), 16)
                g.wait_ge(st.h, st.n)
                g.wait_ge(s2.h, s2.n)
    return p




def build_l1_ffn(debug=False):
    p = Prog("l1ffn")
    nc = p.nc
    x_h = nc.dram_tensor("x", [S, D], F32, kind="ExternalInput").ap()
    g1_h = nc.dram_tensor("g1", [D], F32, kind="ExternalInput").ap()
    g2_h = nc.dram_tensor("g2", [D], F32, kind="ExternalInput").ap()
    wq_h = nc.dram_tensor("Wq", [D, D], F32, kind="ExternalInput").ap()
    wdkv_h = nc.dram_tensor("Wdkv", [D, 512], F32, kind="ExternalInput").ap()
    wuk_h = nc.dram_tensor("Wuk", [512, D], F32, kind="ExternalInput").ap()
    wuv_h = nc.dram_tensor("Wuv", [512, D], F32, kind="ExternalInput").ap()
    wo_h = nc.dram_tensor("Wo", [D, D], F32, kind="ExternalInput").ap()
    gw_h = nc.dram_tensor("gate_w", [D, 8], F32, kind="ExternalInput").ap()
    w1_h = nc.dram_tensor("w1", [8, D, 2048], F32, kind="ExternalInput").ap()
    w2_h = nc.dram_tensor("w2", [8, 2048, D], F32, kind="ExternalInput").ap()
    xout_h = nc.dram_tensor("xout", [S, D], F32, kind="ExternalOutput").ap()
    dbg_h = {}
    if debug:
        for nm, shape, dt in [("u_fm", [128, 8 * S], BF16),
                              ("ctxn", [128, 8 * S], BF16), ("negMb", [128, 16], F32),
                              ("t", [128, T * D], F32), ("u2", [128, 8 * S], BF16),
                              ("lgs", [128, 128], F32), ("cmb", [128, 128], F32),
                              ("kf", [128, 8 * S], BF16), ("vt", [128, T * D], BF16)]:
            dbg_h[nm] = nc.dram_tensor("dbg_" + nm, shape, dt, kind="ExternalOutput").ap()

    with p.stack:
        p.decl_consts(["identf", "ones_col_b", "ones_row_b", "ones_row_f",
                       "hsel", "trimask"])
        p.decl_fvals([1e-5, 1e-6])
        p.arena(190 * KB)
        u_fm = p.at("u_fm_", 0, [128, 8 * S], BF16)
        t_sb = p.at("t_sb_", 0, [128, T * D], F32)
        qf = p.at("qf_", 32 * KB, [128, 8 * S], BF16)
        kf = p.at("kf_", 64 * KB, [128, 8 * S], BF16)
        wob = p.at("wob_", 64 * KB, [128, 8 * D], BF16)
        w1b = p.at("w1b_", 64 * KB, [128, 8 * 2048], BF16)
        x3 = p.at("x3_", 64 * KB, [128, T * D], F32)
        vt = p.at("vt_", 96 * KB, [128, T * D], BF16)
        x1 = p.at("x1_", 96 * KB, [128, T * D], F32)
        w2b = p.at("w2b_", 96 * KB, [128, 16 * D], BF16)
        x2 = p.at("x2_", 80 * KB, [128, T * D], F32)
        cf = p.at("cf_", 128 * KB, [128, 4 * S], BF16)
        es = p.at("es_", 128 * KB, [128, 16 * 512], BF16)
        u2_fm = p.at("u2_fm_", 128 * KB, [128, 8 * S], BF16)
        ctxn = p.at("ctxn_", 144 * KB, [128, 8 * S], BF16)
        wbuf = p.at("wbuf_", 144 * KB, [128, 8 * D], BF16)
        hh = p.at("hh_", 160 * KB, [128, 16 * 512], BF16)
        zt = p.at("zt_", 176 * KB, [128, 2 * D], BF16)
        ut = p.at("ut_", 180 * KB, [128, 2 * D], BF16)
        zre = p.at("zre_", 176 * KB, [128, 2 * D], F32)
        xot = p.at("xot_", 176 * KB, [128, 2 * D], F32)
        sqq = p.at("sqq_", 184 * KB, [128, 2 * 512], BF16)
        t1 = p.at("t1_", 186 * KB, [128, 512], F32)
        rdb = p.at("rdb_", 188 * KB, [128, 512], F32)
        g1r = p.at("g1r_", 80 * KB, [1, D], F32)
        g1b = p.at("g1b_", 84 * KB, [128, D], F32)
        g2r = p.at("g2r_", 88 * KB, [1, D], F32)
        g2b = p.at("g2b_", 92 * KB, [128, D], F32)

        stats = p.sb("stats", [128, 12 * T], F32)
        nrm = p.sb("nrm", [16, 24], F32)
        negMb = p.sb("negMb", [128, 16], F32)
        den_sb = p.sb("den_sb", [1, 512], F32)
        rd_row = p.sb("rd_row", [1, 512], F32)
        mrow = p.sb("mrow", [1, 16], F32)
        m8 = p.sb("m8", [128, 16], F32)
        lgs = p.sb("lgs", [128, 128], F32)
        cmb = p.sb("cmb", [128, 128], F32)
        gwb = p.sb("gwb", [128, 64], BF16)

        ps_a = p.ps("ps_a", [128, 512])
        ps_b = p.ps("ps_b", [128, 512])
        ps_c = p.ps("ps_c", [128, 512])
        ps_d = p.ps("ps_d", [128, 512])
        ps_o = p.ps("ps_o", [128, 1024])
        ps_den = p.ps("ps_den", [16, 512])
        ps_bc = p.ps("ps_bc", [128, 512])

        ld = p.sem("ld")
        lg = p.sem("lg")
        tx = p.sem("tx")
        mm = p.sem("mm")
        ak = p.sem("ak")
        vk = p.sem("vk")
        st = p.sem("st")
        s2 = p.sem("s2")

        with nc.Block() as blk:
            mm0, ak0, vk0 = mm.n, ak.n, vk.n

            @blk.sync
            def _(sync):
                p.emit_const_loads(sync, ld)
                ld.inc(sync.dma_start(out=g1r[:, :], in_=g1_h[None, :]), 16)
                for t in range(T):
                    ld.inc(sync.dma_start(out=x1[:, t * D:(t + 1) * D],
                                          in_=x_h[t * 128:(t + 1) * 128, :]), 16)
            ld_all = ld.n

            @blk.sync
            def _(sync):
                for t in range(T):
                    sync.wait_ge(vk.h, vk0 + 7 + t + 1)
                    for c in range(8):
                        tx.inc(sync.dma_start_transpose(
                            u_fm[:, c * S + t * 128:c * S + t * 128 + 128],
                            ut[:, (t % 2) * D + c * 128:(t % 2) * D + c * 128 + 128]), 16)

            @blk.scalar
            def _(s):
                s.wait_ge(ld.h, ld_all)
                for t in range(T):
                    s.activation(out=zt[:, (t % 2) * D:(t % 2) * D + D],
                                 in_=x1[:, t * D:(t + 1) * D], func=AF.Square,
                                 accum_out=stats[:, T + t:T + t + 1])
                    inst = s.activation(out=zt[:, (t % 2) * D:(t % 2) * D + D],
                                        in_=x1[:, t * D:(t + 1) * D], func=AF.Copy,
                                        accum_out=stats[:, t:t + 1])
                ak.inc(inst)
                s.wait_ge(vk.h, vk0 + 3)
                ak.inc(s.activation(out=stats[:, 10 * T:11 * T], in_=stats[:, 7 * T:8 * T],
                                    func=AF.Sqrt, bias=p.fc(1e-5)))
                s.wait_ge(vk.h, vk0 + 6)
                ak.inc(s.activation(out=stats[:, 11 * T:12 * T], in_=stats[:, 11 * T:12 * T],
                                    func=AF.Sqrt, bias=p.fc(1e-6)))
                s.wait_ge(mm.h, mm0 + 1)
                ak.inc(s.activation(out=g1b[:, 0:512], in_=ps_bc[:, 0:512],
                                    func=AF.Copy))
                s.wait_ge(mm.h, mm0 + 2)
                ak.inc(s.activation(out=g1b[:, 512:1024], in_=ps_bc[:, 0:512],
                                    func=AF.Copy))
                s.wait_ge(vk.h, vk0 + 7)
                for t in range(T):
                    if t >= 2:
                        s.wait_ge(vk.h, vk0 + 7 + t - 1)
                    ak.inc(s.activation(out=zt[:, (t % 2) * D:(t % 2) * D + D],
                                        in_=x1[:, t * D:(t + 1) * D], func=AF.Identity,
                                        scale=stats[:, 3 * T + t:3 * T + t + 1],
                                        bias=stats[:, 4 * T + t:4 * T + t + 1]))

            @blk.vector
            def _(v):
                v.wait_ge(ak.h, ak0 + 1)
                vk.inc(v.tensor_scalar_mul(stats[:, 2 * T:3 * T], stats[:, 0:T],
                                           1.0 / D))
                v.wait_ge(vk.h, vk0 + 1)
                m = stats[:, 2 * T:3 * T]
                v.tensor_mul(stats[:, 10 * T:11 * T], m, m)
                vk.inc(v.tensor_scalar(stats[:, 11 * T:12 * T], stats[:, T:2 * T],
                                       1.0 / D, None, op0=ALU.mult))
                v.wait_ge(vk.h, vk0 + 2)
                vk.inc(v.tensor_sub(stats[:, 7 * T:8 * T], stats[:, 11 * T:12 * T],
                                    stats[:, 10 * T:11 * T]))
                v.wait_ge(ak.h, ak0 + 2)
                vk.inc(v.reciprocal(stats[:, 3 * T:4 * T], stats[:, 10 * T:11 * T]))
                v.wait_ge(vk.h, vk0 + 4)
                v.scalar_tensor_tensor(stats[:, 4 * T:5 * T], stats[:, 2 * T:3 * T],
                                       -1.0, stats[:, 3 * T:4 * T],
                                       op0=ALU.mult, op1=ALU.mult)
                vk.inc(v.tensor_mul(stats[:, 10 * T:11 * T], stats[:, 3 * T:4 * T],
                                    stats[:, 3 * T:4 * T]))
                v.wait_ge(vk.h, vk0 + 5)
                vk.inc(v.tensor_mul(stats[:, 11 * T:12 * T], stats[:, 7 * T:8 * T],
                                    stats[:, 10 * T:11 * T]))
                v.wait_ge(ak.h, ak0 + 3)
                vk.inc(v.reciprocal(stats[:, 6 * T:7 * T], stats[:, 11 * T:12 * T]))
                v.wait_ge(ak.h, ak0 + 5)
                for t in range(T):
                    v.wait_ge(ak.h, ak0 + 5 + t + 1)
                    if t >= 2:
                        v.wait_ge(tx.h, 128 * (t - 1))
                    vk.inc(v.scalar_tensor_tensor(
                        ut[:, (t % 2) * D:(t % 2) * D + D],
                        zt[:, (t % 2) * D:(t % 2) * D + D],
                        stats[:, 6 * T + t:6 * T + t + 1],
                        g1b[:, :], op0=ALU.mult, op1=ALU.mult))

            @blk.tensor
            def _(t):
                t.wait_ge(ld.h, ld_all)
                mm.inc(t.matmul(ps_bc[:, 0:512], p.const["ones_row_f"][:, :],
                                g1r[:, 0:512], start=True, stop=True))
                t.wait_ge(ak.h, ak0 + 4)
                mm.inc(t.matmul(ps_bc[:, 0:512], p.const["ones_row_f"][:, :],
                                g1r[:, 512:1024], start=True, stop=True))
        tx_u = tx.n
        ak_pa, vk_pa, mm_pa = ak.n, vk.n, mm.n

        with nc.Block() as blk:
            mm0, ak0, vk0, lg0 = mm.n, ak.n, vk.n, lg.n

            @blk.gpsimd
            def _(g):
                for c in range(8):
                    lg.inc(g.dma_start(out=wbuf[:, c * D:(c + 1) * D],
                                       in_=wq_h[c * 128:(c + 1) * 128, :]), 16)
                g.wait_ge(mm.h, mm0 + 32)
                for c in range(8):
                    lg.inc(g.dma_start(out=wbuf[:, c * 512:(c + 1) * 512],
                                       in_=wdkv_h[c * 128:(c + 1) * 128, :]), 16)
                g.wait_ge(mm.h, mm0 + 48)
                for c in range(4):
                    lg.inc(g.dma_start(out=wbuf[:, c * D:(c + 1) * D],
                                       in_=wuk_h[c * 128:(c + 1) * 128, :]), 16)
                for c in range(4):
                    lg.inc(g.dma_start(out=wbuf[:, 4 * D + c * D:4 * D + (c + 1) * D],
                                       in_=wuv_h[c * 128:(c + 1) * 128, :]), 16)

            @blk.tensor
            def _(t):
                t.wait_ge(tx.h, tx_u)
                t.wait_ge(lg.h, lg0 + 128)
                g = 0
                for m_ in range(8):
                    for tc in range(4):
                        g += 1
                        psd = [ps_a, ps_b][g % 2]
                        if g > 2:
                            t.wait_ge(ak.h, ak0 + g - 2)
                        for c in range(8):
                            inst = t.matmul(
                                psd[:, :], wbuf[:, c * D + m_ * 128:c * D + m_ * 128 + 128],
                                u_fm[:, c * S + tc * 512:c * S + tc * 512 + 512],
                                start=(c == 0), stop=(c == 7))
                        mm.inc(inst)
                t.wait_ge(lg.h, lg0 + 256)
                for m_ in range(4):
                    for tc in range(4):
                        g += 1
                        psd = [ps_a, ps_b][g % 2]
                        t.wait_ge(ak.h, ak0 + g - 2)
                        for c in range(8):
                            inst = t.matmul(
                                psd[:, :],
                                wbuf[:, c * 512 + m_ * 128:c * 512 + m_ * 128 + 128],
                                u_fm[:, c * S + tc * 512:c * S + tc * 512 + 512],
                                start=(c == 0), stop=(c == 7))
                        mm.inc(inst)
                t.wait_ge(lg.h, lg0 + 384)
                for m_ in range(8):
                    for tc in range(4):
                        g += 1
                        psd = [ps_a, ps_b][g % 2]
                        t.wait_ge(ak.h, ak0 + g - 2)
                        for c in range(4):
                            inst = t.matmul(
                                psd[:, :],
                                wbuf[:, c * D + m_ * 128:c * D + m_ * 128 + 128],
                                cf[:, c * S + tc * 512:c * S + tc * 512 + 512],
                                start=(c == 0), stop=(c == 3))
                        mm.inc(inst)
                for tt1 in range(1, T + 1):
                    if tt1 >= 2:
                        t.wait_ge(ak.h, ak0 + 80 + tt1 - 1)
                    for c in range(4):
                        for hhx in range(2):
                            inst = t.matmul(
                                ps_o[:, hhx * 512:(hhx + 1) * 512],
                                cf[:, c * S + (tt1 - 1) * 128:c * S + (tt1 - 1) * 128 + 128],
                                wbuf[:, 4 * D + c * D + hhx * 512:
                                     4 * D + c * D + hhx * 512 + 512],
                                start=(c == 0), stop=(c == 3))
                    mm.inc(inst)
                k = 0
                for which in range(2):
                    for nn_ in range(4):
                        ng = which * 4 + nn_ + 1
                        if ng >= 2:
                            t.wait_ge(vk.h, vk0 + ng - 1)
                        for c in range(8):
                            k += 1
                            t.wait_ge(ak.h, ak0 + 96 + k)
                            inst = t.matmul(ps_den[:, :],
                                            p.const["hsel"][:, 16 * c:16 * c + 16],
                                            sqq[:, (c % 2) * 512:(c % 2) * 512 + 512],
                                            start=(c == 0), stop=(c == 7))
                            mm.inc(inst)
                t.wait_ge(vk.h, vk0 + 11)
                mm.inc(t.transpose(ps_bc[0:1, 32:48], nrm[:, 12:13],
                                   p.const["identf"][0:16, 0:16]))
                t.wait_ge(ak.h, ak0 + 163)
                mm.inc(t.matmul(ps_bc[:, 64:80], p.const["ones_row_f"][:, :],
                                mrow[:, :], start=True, stop=True))

            @blk.scalar
            def _(s):
                g = 0
                for m_ in range(8):
                    for tc in range(4):
                        g += 1
                        psd = [ps_a, ps_b][g % 2]
                        s.wait_ge(mm.h, mm0 + g)
                        ak.inc(s.activation(
                            out=qf[:, m_ * S + tc * 512:m_ * S + tc * 512 + 512],
                            in_=psd[:, :], func=AF.Copy, scale=1.0 / 8.0))
                for m_ in range(4):
                    for tc in range(4):
                        g += 1
                        psd = [ps_a, ps_b][g % 2]
                        s.wait_ge(mm.h, mm0 + g)
                        ak.inc(s.activation(
                            out=cf[:, m_ * S + tc * 512:m_ * S + tc * 512 + 512],
                            in_=psd[:, :], func=AF.Copy))
                for m_ in range(8):
                    for tc in range(4):
                        g += 1
                        psd = [ps_a, ps_b][g % 2]
                        s.wait_ge(mm.h, mm0 + g)
                        ak.inc(s.activation(
                            out=kf[:, m_ * S + tc * 512:m_ * S + tc * 512 + 512],
                            in_=psd[:, :], func=AF.Copy))
                for tt1 in range(1, T + 1):
                    s.wait_ge(mm.h, mm0 + 80 + tt1)
                    ak.inc(s.activation(out=vt[:, (tt1 - 1) * D:tt1 * D], in_=ps_o[:, :],
                                        func=AF.Copy))
                s.wait_ge(ak.h, ak0 + 96)
                k = 0
                for which in range(2):
                    src = qf if which == 0 else kf
                    for nn_ in range(4):
                        for c in range(8):
                            k += 1
                            if k > 2:
                                s.wait_ge(mm.h, mm0 + 96 + k - 2)
                            ak.inc(s.activation(
                                out=sqq[:, (c % 2) * 512:(c % 2) * 512 + 512],
                                in_=src[:, c * S + nn_ * 512:c * S + nn_ * 512 + 512],
                                func=AF.Square))
                s.wait_ge(vk.h, vk0 + 10)
                ak.inc(s.activation(out=nrm[:, 16:17], in_=nrm[:, 8:9], func=AF.Sqrt))
                ak.inc(s.activation(out=nrm[:, 17:18], in_=nrm[:, 9:10], func=AF.Sqrt))
                s.wait_ge(mm.h, mm0 + 161)
                ak.inc(s.activation(out=mrow[:, :], in_=ps_bc[0:1, 32:48], func=AF.Copy))
                s.wait_ge(mm.h, mm0 + 162)
                ak.inc(s.activation(out=negMb[:, :], in_=ps_bc[:, 64:80], func=AF.Copy,
                                    scale=-1.0))

            @blk.vector
            def _(v):
                for ng in range(1, 9):
                    v.wait_ge(mm.h, mm0 + 96 + 8 * ng)
                    vk.inc(v.reduce_max(nrm[:, ng - 1:ng], ps_den[:, :], axis=AX.X))
                v.wait_ge(vk.h, vk0 + 8)
                vk.inc(v.reduce_max(nrm[:, 8:9], nrm[:, 0:4], axis=AX.X))
                vk.inc(v.reduce_max(nrm[:, 9:10], nrm[:, 4:8], axis=AX.X))
                v.wait_ge(ak.h, ak0 + 162)
                vk.inc(v.tensor_mul(nrm[:, 12:13], nrm[:, 16:17], nrm[:, 17:18]))
        ak_pc, vk_pc, mm_pc = ak.n, vk.n, mm.n

        ktns = [4 * qc + 4 for qc in range(4)]
        mmBs, akBs = [], []
        cm_, ca_ = mm.n, ak.n
        for qc in range(4):
            for h in range(16):
                mmBs.append(cm_)
                akBs.append(ca_)
                cm_ += 2 * ktns[qc] + 2
                ca_ += ktns[qc] + 3
        vk0_pd = vk.n
        with nc.Block() as blk:
            @blk.tensor
            def _(t):
                t.wait_ge(ak.h, ak_pc)
                for qc in range(4):
                    ktn = ktns[qc]
                    q0 = qc * 512
                    for h in range(16):
                        it = qc * 16 + h
                        mmB, akB = mmBs[it], akBs[it]
                        vkB = vk0_pd + 6 * it
                        ch = h // 2
                        r0 = 64 * (h % 2)
                        for kt in range(ktn):
                            psd = [ps_a, ps_b][kt % 2]
                            if kt >= 2:
                                t.wait_ge(ak.h, akB + kt - 1)
                            elif it > 0:
                                t.wait_ge(ak.h, akB - 3)
                            inst = t.matmul(
                                psd[:, :],
                                kf[r0:r0 + 64, ch * S + kt * 128:ch * S + kt * 128 + 128],
                                qf[r0:r0 + 64, ch * S + q0:ch * S + q0 + 512],
                                start=True, stop=True)
                            mm.inc(inst)
                        for kt in range(ktn):
                            if kt < 4 * qc:
                                t.wait_ge(ak.h, akB + kt + 1)
                            else:
                                t.wait_ge(vk.h, vkB + (kt - 4 * qc) + 1)
                            mm.inc(t.matmul(ps_den[0:1, :], p.const["ones_col_b"][:, :],
                                            es[:, kt * 512:(kt + 1) * 512],
                                            start=(kt == 0), stop=(kt == ktn - 1)))
                        t.wait_ge(vk.h, vkB + 5)
                        mm.inc(t.matmul(ps_bc[:, :], p.const["ones_row_f"][:, :],
                                        rd_row[:, :], start=True, stop=True))
                        psd = [ps_c, ps_d][it % 2]
                        if it >= 2:
                            t.wait_ge(vk.h, vkB - 6)
                        for kt in range(ktn):
                            inst = t.matmul(
                                psd[0:64, :],
                                vt[:, kt * D + 64 * h:kt * D + 64 * h + 64],
                                es[:, kt * 512:(kt + 1) * 512],
                                start=(kt == 0), stop=(kt == ktn - 1))
                        mm.inc(inst)

            @blk.vector
            def _(v):
                for qc in range(4):
                    ktn = ktns[qc]
                    for h in range(16):
                        it = qc * 16 + h
                        mmB, akB = mmBs[it], akBs[it]
                        vkB = vk0_pd + 6 * it
                        for j in range(4):
                            kt = 4 * qc + j
                            v.wait_ge(ak.h, akB + kt + 1)
                            vk.inc(v.tensor_mul(es[:, kt * 512:(kt + 1) * 512],
                                                es[:, kt * 512:(kt + 1) * 512],
                                                p.const["trimask"][:, j * 512:(j + 1) * 512]))
                        v.wait_ge(ak.h, akB + ktn + 1)
                        vk.inc(v.reciprocal(rd_row[:, :], den_sb[:, :]))
                        v.wait_ge(mm.h, mmB + 2 * ktn + 2)
                        v.wait_ge(ak.h, akB + ktn + 2)
                        if it >= 1:
                            v.wait_ge(ak.h, akB)
                        psd = [ps_c, ps_d][it % 2]
                        vk.inc(v.tensor_mul(t1[0:64, :], psd[0:64, :], rdb[0:64, :]))

            @blk.scalar
            def _(s):
                for qc in range(4):
                    ktn = ktns[qc]
                    for h in range(16):
                        it = qc * 16 + h
                        mmB, akB = mmBs[it], akBs[it]
                        vkB = vk0_pd + 6 * it
                        ch = h // 2
                        r0 = 64 * (h % 2)
                        for kt in range(ktn):
                            psd = [ps_a, ps_b][kt % 2]
                            s.wait_ge(mm.h, mmB + kt + 1)
                            ak.inc(s.activation(out=es[:, kt * 512:(kt + 1) * 512],
                                                in_=psd[:, :], func=AF.Exp,
                                                bias=negMb[:, h:h + 1]))
                        s.wait_ge(mm.h, mmB + 2 * ktn)
                        ak.inc(s.activation(out=den_sb[:, :], in_=ps_den[0:1, :],
                                            func=AF.Copy))
                        s.wait_ge(mm.h, mmB + 2 * ktn + 1)
                        if it > 0:
                            s.wait_ge(vk.h, vkB)
                        ak.inc(s.activation(out=rdb[:, :], in_=ps_bc[:, :], func=AF.Copy))
                        s.wait_ge(vk.h, vkB + 6)
                        ak.inc(s.activation(
                            out=ctxn[r0:r0 + 64, ch * S + qc * 512:ch * S + qc * 512 + 512],
                            in_=t1[0:64, :], func=AF.Copy))
        ak_pd, vk_pd, mm_pd = ak.n, vk.n, mm.n

        with nc.Block() as blk:
            mm0, ak0, vk0 = mm.n, ak.n, vk.n

            @blk.gpsimd
            def _(g):
                g.wait_ge(mm.h, mm_pd)
                for c in range(8):
                    lg.inc(g.dma_start(out=wob[:, c * D:(c + 1) * D],
                                       in_=wo_h[c * 128:(c + 1) * 128, :]), 16)
            lg_wo = lg.n

            @blk.sync
            def _(sync):
                sync.wait_ge(mm.h, mm_pd)
                for t in range(T):
                    ld.inc(sync.dma_start(out=x2[:, t * D:(t + 1) * D],
                                          in_=x_h[t * 128:(t + 1) * 128, :]), 16)
            ld_x2 = ld.n

            @blk.tensor
            def _(t):
                t.wait_ge(ak.h, ak_pd)
                t.wait_ge(lg.h, lg_wo)
                for tt1 in range(1, T + 1):
                    if tt1 >= 2:
                        t.wait_ge(vk.h, vk0 + tt1 - 1)
                    for m_ in range(8):
                        for hhx in range(2):
                            inst = t.matmul(
                                ps_o[:, hhx * 512:(hhx + 1) * 512],
                                ctxn[:, m_ * S + (tt1 - 1) * 128:
                                     m_ * S + (tt1 - 1) * 128 + 128],
                                wob[:, m_ * D + hhx * 512:m_ * D + hhx * 512 + 512],
                                start=(m_ == 0), stop=(m_ == 7))
                    mm.inc(inst)

            @blk.scalar
            def _(s):
                s.wait_ge(ld.h, ld_x2)
                s.wait_ge(ak.h, ak_pd)
                for tt in range(T):
                    if tt >= 2:
                        s.wait_ge(vk.h, vk0 + tt - 1)
                    ak.inc(s.activation(out=zre[:, (tt % 2) * D:(tt % 2) * D + D],
                                        in_=x2[:, tt * D:(tt + 1) * D], func=AF.Identity,
                                        scale=stats[:, 3 * T + tt:3 * T + tt + 1],
                                        bias=stats[:, 4 * T + tt:4 * T + tt + 1]))

            @blk.vector
            def _(v):
                for tt in range(T):
                    v.wait_ge(mm.h, mm0 + tt + 1)
                    v.wait_ge(ak.h, ak0 + tt + 1)
                    vk.inc(v.tensor_add(t_sb[:, tt * D:(tt + 1) * D],
                                        zre[:, (tt % 2) * D:(tt % 2) * D + D],
                                        ps_o[:, :]))
        ak_pe, vk_pe, mm_pe = ak.n, vk.n, mm.n

        with nc.Block() as blk:
            mm0, ak0, vk0, tx0 = mm.n, ak.n, vk.n, tx.n

            @blk.gpsimd
            def _(g):
                g.wait_ge(ak.h, ak_pe)
                lg.inc(g.dma_start(out=g2r[:, :], in_=g2_h[None, :]), 16)
                lg.inc(g.dma_start(out=gwb.rearrange("p (c e) -> p c e", e=8),
                                   in_=gw_h.rearrange("(c p) e -> p c e", p=128)), 16)
            lg_gw = lg.n

            @blk.scalar
            def _(s):
                s.wait_ge(vk.h, vk_pe)
                for t in range(T):
                    inst = s.activation(out=zre[:, (t % 2) * D:(t % 2) * D + D],
                                        in_=t_sb[:, t * D:(t + 1) * D], func=AF.Square,
                                        accum_out=stats[:, 8 * T + t:8 * T + t + 1])
                ak.inc(inst)
                s.wait_ge(vk.h, vk0 + 1)
                ak.inc(s.activation(out=stats[:, 10 * T:11 * T],
                                    in_=stats[:, 10 * T:11 * T],
                                    func=AF.Sqrt, bias=p.fc(1e-6)))
                s.wait_ge(mm.h, mm0 + 1)
                ak.inc(s.activation(out=g2b[:, 0:512], in_=ps_bc[:, 0:512],
                                    func=AF.Copy))
                s.wait_ge(mm.h, mm0 + 2)
                ak.inc(s.activation(out=g2b[:, 512:1024], in_=ps_bc[:, 0:512],
                                    func=AF.Copy))
                for tt in range(T):
                    s.wait_ge(mm.h, mm0 + 2 + tt + 1)
                    ak.inc(s.activation(out=lgs[:, tt * 8:(tt + 1) * 8],
                                        in_=ps_c[:, 0:8], func=AF.Copy))
                for tt in range(T):
                    s.wait_ge(vk.h, vk0 + 18 + 6 * tt + 2)
                    ak.inc(s.activation(out=m8[:, 8:9], in_=m8[:, 9:10],
                                        func=AF.Sigmoid))

            @blk.vector
            def _(v):
                v.wait_ge(ak.h, ak0 + 1)
                vk.inc(v.tensor_scalar_mul(stats[:, 10 * T:11 * T],
                                           stats[:, 8 * T:9 * T], 1.0 / D))
                v.wait_ge(ak.h, ak0 + 2)
                vk.inc(v.reciprocal(stats[:, 9 * T:10 * T], stats[:, 10 * T:11 * T]))
                v.wait_ge(ak.h, ak0 + 4)
                for t in range(T):
                    v.wait_ge(vk.h, vk0 + 2 + t)
                    if t >= 2:
                        v.wait_ge(tx.h, tx0 + 128 * (t - 1))
                    vk.inc(v.scalar_tensor_tensor(
                        ut[:, (t % 2) * D:(t % 2) * D + D],
                        t_sb[:, t * D:(t + 1) * D],
                        stats[:, 9 * T + t:9 * T + t + 1],
                        g2b[:, :], op0=ALU.mult, op1=ALU.mult))
                for tt in range(T):
                    s_ = vk0 + 18 + 6 * tt
                    v.wait_ge(ak.h, ak0 + 4 + tt + 1)
                    vk.inc(v.max(out=m8[:, 0:8], in_=lgs[:, tt * 8:(tt + 1) * 8]))
                    v.wait_ge(vk.h, s_ + 1)
                    vk.inc(v.tensor_sub(m8[:, 9:10], m8[:, 0:1], m8[:, 1:2]))
                    v.wait_ge(ak.h, ak0 + 20 + tt + 1)
                    vk.inc(v.tensor_scalar(m8[:, 10:11], m8[:, 8:9], -1.0, 1.0,
                                           op0=ALU.mult, op1=ALU.add))
                    v.wait_ge(vk.h, s_ + 3)
                    v.tensor_scalar(cmb[:, tt * 8:(tt + 1) * 8],
                                    lgs[:, tt * 8:(tt + 1) * 8], m8[:, 0:1], None,
                                    op0=ALU.is_equal)
                    vk.inc(v.tensor_scalar(t1[:, 0:8], lgs[:, tt * 8:(tt + 1) * 8],
                                           m8[:, 1:2], None, op0=ALU.is_equal))
                    v.wait_ge(vk.h, s_ + 4)
                    vk.inc(v.tensor_scalar(cmb[:, tt * 8:(tt + 1) * 8],
                                           cmb[:, tt * 8:(tt + 1) * 8], m8[:, 8:9],
                                           None, op0=ALU.mult))
                    v.wait_ge(vk.h, s_ + 5)
                    vk.inc(v.scalar_tensor_tensor(cmb[:, tt * 8:(tt + 1) * 8],
                                                  t1[:, 0:8], m8[:, 10:11],
                                                  cmb[:, tt * 8:(tt + 1) * 8],
                                                  op0=ALU.mult, op1=ALU.add))

            @blk.tensor
            def _(t):
                t.wait_ge(vk.h, vk_pe)
                t.wait_ge(lg.h, lg_gw)
                mm.inc(t.matmul(ps_bc[:, 0:512], p.const["ones_row_f"][:, :],
                                g2r[:, 0:512], start=True, stop=True))
                t.wait_ge(ak.h, ak0 + 3)
                mm.inc(t.matmul(ps_bc[:, 0:512], p.const["ones_row_f"][:, :],
                                g2r[:, 512:1024], start=True, stop=True))
                t.wait_ge(lg.h, lg_gw)
                for tt in range(T):
                    t.wait_ge(tx.h, tx0 + 128 * (tt + 1))
                    t.wait_ge(ak.h, ak0 + 4 + tt)
                    for c in range(8):
                        inst = t.matmul(ps_c[:, 0:8],
                                        u2_fm[:, c * S + tt * 128:c * S + tt * 128 + 128],
                                        gwb[:, c * 8:(c + 1) * 8],
                                        start=(c == 0), stop=(c == 7))
                    mm.inc(inst)

            @blk.sync
            def _(sync):
                for t in range(T):
                    sync.wait_ge(vk.h, vk0 + 2 + t + 1)
                    for c in range(8):
                        tx.inc(sync.dma_start_transpose(
                            u2_fm[:, c * S + t * 128:c * S + t * 128 + 128],
                            ut[:, (t % 2) * D + c * 128:(t % 2) * D + c * 128 + 128]), 16)
        ak_pf, vk_pf, mm_pf = ak.n, vk.n, mm.n

        with nc.Block() as blk:
            mm0, ak0, vk0, lg0 = mm.n, ak.n, vk.n, lg.n

            @blk.gpsimd
            def _(g):
                for e in range(8):
                    if e > 0:
                        g.wait_ge(mm.h, mm0 + 80 * e)
                    for c in range(8):
                        lg.inc(g.dma_start(out=w1b[:, c * 2048:(c + 1) * 2048],
                                           in_=w1_h[e, c * 128:(c + 1) * 128, :]), 16)
                    for c in range(16):
                        lg.inc(g.dma_start(out=w2b[:, c * D:(c + 1) * D],
                                           in_=w2_h[e, c * 128:(c + 1) * 128, :]), 16)

            @blk.tensor
            def _(t):
                t.wait_ge(vk.h, vk_pf)
                for e in range(8):
                    t.wait_ge(lg.h, lg0 + 384 * (e + 1))
                    gidx = 0
                    for tc in range(4):
                        for hc in range(16):
                            gidx += 1
                            psd = [ps_a, ps_b][gidx % 2]
                            if gidx > 2:
                                t.wait_ge(ak.h, ak0 + 64 * e + gidx - 2)
                            elif e > 0:
                                t.wait_ge(ak.h, ak0 + 64 * e - 2 + gidx)
                            for c in range(8):
                                inst = t.matmul(
                                    psd[:, :],
                                    w1b[:, c * 2048 + hc * 128:c * 2048 + hc * 128 + 128],
                                    u2_fm[:, c * S + tc * 512:c * S + tc * 512 + 512],
                                    start=(c == 0), stop=(c == 7))
                            mm.inc(inst)
                        t.wait_ge(ak.h, ak0 + 64 * e + 16 * (tc + 1))
                        for sub in range(4):
                            tt = tc * 4 + sub
                            gy = 16 * e + 4 * tc + sub
                            if gy > 0:
                                t.wait_ge(vk.h, vk0 + gy)
                            for hc in range(16):
                                for hhx in range(2):
                                    inst = t.matmul(
                                        ps_o[:, hhx * 512:(hhx + 1) * 512],
                                        hh[:, hc * 512 + sub * 128:
                                           hc * 512 + sub * 128 + 128],
                                        w2b[:, hc * D + hhx * 512:hc * D + hhx * 512 + 512],
                                        start=(hc == 0), stop=(hc == 15))
                            mm.inc(inst)

            @blk.scalar
            def _(s):
                for e in range(8):
                    for tc in range(4):
                        for hc in range(16):
                            s.wait_ge(mm.h, mm0 + 80 * e + 20 * tc + hc + 1)
                            psd = [ps_a, ps_b][(tc * 16 + hc + 1) % 2]
                            ak.inc(s.activation(
                                out=hh[:, hc * 512:(hc + 1) * 512],
                                in_=psd[:, :], func=AF.Silu))

            @blk.vector
            def _(v):
                for e in range(8):
                    for tc in range(4):
                        for sub in range(4):
                            tt = tc * 4 + sub
                            v.wait_ge(mm.h, mm0 + 80 * e + 20 * tc + 16 + sub + 1)
                            v.wait_ge(vk.h, vk0 + 16 * e + 4 * tc + sub)
                            vk.inc(v.scalar_tensor_tensor(
                                t_sb[:, tt * D:(tt + 1) * D],
                                ps_o[:, :], cmb[:, tt * 8 + e:tt * 8 + e + 1],
                                t_sb[:, tt * D:(tt + 1) * D],
                                op0=ALU.mult, op1=ALU.add))
        ak_pg, vk_pg, mm_pg = ak.n, vk.n, mm.n

        with nc.Block() as blk:
            ld0, vk0 = ld.n, vk.n

            @blk.sync
            def _(sync):
                sync.wait_ge(mm.h, mm_pg)
                for t in range(T):
                    ld.inc(sync.dma_start(out=x3[:, t * D:(t + 1) * D],
                                          in_=x_h[t * 128:(t + 1) * 128, :]), 16)
            ld_x3 = ld.n

            @blk.vector
            def _(v):
                v.wait_ge(ld.h, ld_x3)
                v.wait_ge(vk.h, vk_pg)
                for tt in range(T):
                    if tt >= 2:
                        psem = [st, s2][tt % 2]
                        v.wait_ge(psem.h, 16 * ((tt - 2) // 2 + 1))
                    vk.inc(v.scalar_tensor_tensor(
                        xot[:, (tt % 2) * D:(tt % 2) * D + D],
                        t_sb[:, tt * D:(tt + 1) * D], 0.5,
                        x3[:, tt * D:(tt + 1) * D],
                        op0=ALU.mult, op1=ALU.add))

            @blk.gpsimd
            def _(g):
                for tt in range(T):
                    g.wait_ge(vk.h, vk0 + tt + 1)
                    psem = [st, s2][tt % 2]
                    psem.inc(g.dma_start(out=xout_h[tt * 128:(tt + 1) * 128, :],
                                         in_=xot[:, (tt % 2) * D:(tt % 2) * D + D]), 16)
                if dbg_h:
                    for nm, buf in [("u_fm", u_fm), ("ctxn", ctxn), ("negMb", negMb),
                                    ("t", t_sb), ("u2", u2_fm), ("lgs", lgs),
                                    ("cmb", cmb), ("kf", kf), ("vt", vt)]:
                        st.inc(g.dma_start(out=dbg_h[nm][:, :], in_=buf[:, :]), 16)
                g.wait_ge(st.h, st.n)
                g.wait_ge(s2.h, s2.n)
    return p


def _run(prog, in_maps, **kw):
    return run_bass_kernel_spmd(prog.nc, in_maps, core_ids=list(range(NCORE)), **kw)


_P_L0ATTN = None


def run_l0_attn(x_full, W_qkv, b_qkv, W_out, b_out, **kw):
    global _P_L0ATTN
    if _P_L0ATTN is None:
        _P_L0ATTN = build_l0_attn()
    cm = consts_in_map(L0ATTN_CONSTS)
    cm.update(_P_L0ATTN.fvals_in_map())
    im = []
    for c in range(NCORE):
        d = dict(cm)
        d["x"] = np.ascontiguousarray(x_full[c])
        d["W_qkv"] = W_qkv
        d["b_qkv"] = b_qkv
        d["W_out"] = W_out
        d["b_out"] = b_out
        im.append(d)
    res = _run(_P_L0ATTN, im, **kw)
    return np.stack([r["xout"] for r in res.results], axis=0)



A1_CONSTS = ["identf", "ones_col_b", "ones_col_f", "ones_row_b", "ones_row_f"]


def build_l0_ffn_a1():
    p = Prog("l0ffn_a1")
    nc = p.nc
    x_h = nc.dram_tensor("x", [S, D], F32, kind="ExternalInput").ap()
    wgt_h = nc.dram_tensor("WgT", [D, 8], F32, kind="ExternalInput").ap()
    z_h = nc.dram_tensor("zout", [D, S], BF16, kind="ExternalOutput").ap()
    sc_h = nc.dram_tensor("scout", [8, S], F32, kind="ExternalOutput").ap()

    with p.stack:
        p.decl_consts(A1_CONSTS)
        p.decl_fvals([1e-5])
        p.arena(120 * KB)
        x_sb = p.at("x_sb", 0, [128, T * D], F32)
        zf = p.at("zf_", 64 * KB, [128, 8 * S], BF16)
        zt = p.at("zt_", 96 * KB, [128, 2 * D], BF16)
        scf = p.at("scf_", 100 * KB, [8, S], F32)
        stats = p.sb("stats", [128, 8 * T], F32)
        wgb = p.sb("wgb", [128, 64], BF16)
        ps_sc = p.ps("ps_sc", [8, S])

        ld = p.sem("ld")
        lg = p.sem("lg")
        tx = p.sem("tx")
        mm = p.sem("mm")
        ak = p.sem("ak")
        vk = p.sem("vk")
        st = p.sem("st")

        with nc.Block() as blk:
            @blk.sync
            def _(sync):
                p.emit_const_loads(sync, ld)
                for t in range(T):
                    ld.inc(sync.dma_start(out=x_sb[:, t * D:(t + 1) * D],
                                          in_=x_h[t * 128:(t + 1) * 128, :]), 16)

            @blk.gpsimd
            def _(g):
                lg.inc(g.dma_start(out=wgb.rearrange("p (c e) -> p c e", e=8),
                                   in_=wgt_h.rearrange("(c p) e -> p c e", p=128)), 16)
        ld_all = ld.n
        lg_w = lg.n

        with nc.Block() as blk:
            emit_ln(p, blk, x_sb=x_sb, zt=zt, zf=zf, stats=stats,
                    ld=ld, ak=ak, vk=vk, tx=tx, wait_ld=ld_all, eps_ap=p.fc(1e-5))

            @blk.gpsimd
            def _(g):
                g.wait_ge(tx.h, 2048)
                for c in range(8):
                    st.inc(g.dma_start(out=z_h[c * 128:(c + 1) * 128, :],
                                       in_=zf[:, c * S:(c + 1) * S]), 16)
        tx_zf = tx.n
        with nc.Block() as blk:
            @blk.tensor
            def _(t):
                t.wait_ge(tx.h, tx_zf)
                t.wait_ge(lg.h, lg_w)
                for nn in range(4):
                    for c in range(8):
                        inst = t.matmul(ps_sc[:, nn * 512:(nn + 1) * 512],
                                        wgb[:, c * 8:(c + 1) * 8],
                                        zf[:, c * S + nn * 512:c * S + nn * 512 + 512],
                                        start=(c == 0), stop=(c == 7))
                    mm.inc(inst)

            @blk.scalar
            def _(s):
                for nn in range(4):
                    s.wait_ge(mm.h, nn + 1)
                    ak.inc(s.activation(out=scf[:, nn * 512:(nn + 1) * 512],
                                        in_=ps_sc[:, nn * 512:(nn + 1) * 512],
                                        func=AF.Copy))

            @blk.gpsimd
            def _(g):
                g.wait_ge(ak.h, 22)
                st.inc(g.dma_start(out=sc_h[:, :], in_=scf[:, :]), 16)
                g.wait_ge(st.h, st.n)
    return p




class Seq:

    def __init__(self, sem):
        self.sem = sem
        self.script = []

    def op(self, eng, fn, inc=1):
        self.script.append((eng, fn, inc))

    def emit(self, blk, engines):
        base = self.sem.n
        offs = []
        c = base
        for (_, _, inc) in self.script:
            offs.append(c)
            c += inc
        for name, deco in engines.items():
            def body(eng, name=name):
                for i, (en, fn, inc) in enumerate(self.script):
                    if en != name:
                        continue
                    eng.wait_ge(self.sem.h, offs[i])
                    fn(eng).then_inc(self.sem.h, inc)
            deco(body)
        self.sem.n = c


def build_l0_ffn_a2():
    p = Prog("l0ffn_a2")
    nc = p.nc
    z_h = nc.dram_tensor("z_all", [8 * S, D], BF16, kind="ExternalInput").ap()
    sc_h = nc.dram_tensor("sce", [128, 128], F32, kind="ExternalInput").ap()
    w1_h = nc.dram_tensor("W1eT", [D, 2048], F32, kind="ExternalInput").ap()
    w3_h = nc.dram_tensor("W3eT", [D, 2048], F32, kind="ExternalInput").ap()
    w2_h = nc.dram_tensor("W2eT", [2048, D], F32, kind="ExternalInput").ap()
    y_h = nc.dram_tensor("yout", [S, D], BF16, kind="ExternalOutput").ap()
    cidx_t = nc.dram_tensor("cidx", [S, 1], I32, kind="ExternalOutput")
    w_hbm = nc.dram_tensor("w_scr", [128 * 128, 1], F32, kind="Internal")

    with p.stack:
        p.decl_consts(["identf", "ones_col_b", "ones_col_f", "ones_row_b",
                       "ones_row_f", "tri_lt", "tid"])
        p.decl_fvals([2048.0, 1.0, -1.0])
        p.arena(200 * KB)
        w1b = p.at("w1b_", 0, [128, 8 * 2048], BF16)
        w3b = p.at("w3b_", 32 * KB, [128, 8 * 2048], BF16)
        w2b = p.at("w2b_", 64 * KB, [128, 16 * D], BF16)
        zg = p.at("zg_", 96 * KB, [128, T * D], BF16)
        zgf = p.at("zgf_", 128 * KB, [128, 8 * S], BF16)
        hhs = p.at("hhs_", 160 * KB, [128, 16 * 512], BF16)
        sA = p.at("sA_", 176 * KB, [128, 2 * 512], BF16)
        sc = p.at("sc_", 178 * KB, [128, 128], F32)
        msk = p.at("msk_", 178 * KB + 512, [128, 128], F32)
        aux = p.at("aux_", 179 * KB, [128, 64], F32)
        strict = p.at("strict_", 180 * KB, [128, 128], F32)
        gtlo = p.at("gtlo_", 180 * KB + 512, [128, 128], F32)
        eq = p.at("eq_", 181 * KB, [128, 128], F32)
        scn = p.at("scn_", 181 * KB + 512, [128, 128], F32)
        zz = p.at("zz_", 182 * KB, [128, 128], F32)
        zrs = p.at("zrs_", 182 * KB + 512, [128, 128], F32)
        sel = p.at("sel_", 183 * KB, [128, 128], F32)
        rnk = p.at("rnk_", 183 * KB + 512, [128, 128], F32)
        w2d = p.at("w2d_", 184 * KB, [128, 128], F32)
        offs = p.at("offs_", 184 * KB + 512, [128, 128], F32)
        offsi = p.at("offsi_", 185 * KB, [128, 128], I32)
        tidi = p.at("tidi_", 185 * KB + 512, [128, 128], I32)
        cidxs = p.at("cidxs_", 186 * KB, [128, 16], I32)
        wslot = p.at("wslot_", 186 * KB + 64, [128, 16], F32)
        smll = p.at("smll_", 186 * KB + 128, [1, 16], F32)
        yt = p.at("yt_", 187 * KB, [128, 2 * D], BF16)

        ps_a = p.ps("ps_a", [128, 512])
        ps_b = p.ps("ps_b", [128, 512])
        ps_o = p.ps("ps_o", [128, 1024])
        ps_s = p.ps("ps_s", [128, 128])
        ps_1 = p.ps("ps_1", [1, 128])

        ld = p.sem("ld")
        lg = p.sem("lg")
        tx = p.sem("tx")
        sq = p.sem("sq")
        mm = p.sem("mm")
        ak = p.sem("ak")
        vk = p.sem("vk")
        st = p.sem("st")
        s2 = p.sem("s2")

        with nc.Block() as blk:
            @blk.sync
            def _(sync):
                p.emit_const_loads(sync, ld)
                ld.inc(sync.dma_start(out=sc[:, :], in_=sc_h[:, :]), 16)
            ld_all = ld.n

            @blk.gpsimd
            def _(g):
                for c in range(8):
                    lg.inc(g.dma_start(out=w1b[:, c * 2048:(c + 1) * 2048],
                                       in_=w1_h[c * 128:(c + 1) * 128, :]), 16)
                for c in range(8):
                    lg.inc(g.dma_start(out=w3b[:, c * 2048:(c + 1) * 2048],
                                       in_=w3_h[c * 128:(c + 1) * 128, :]), 16)
                for c in range(16):
                    lg.inc(g.dma_start(out=w2b[:, c * D:(c + 1) * D],
                                       in_=w2_h[c * 128:(c + 1) * 128, :]), 16)
        lg_w = lg.n

        NI = 34
        q = Seq(sq)
        OC = p.const["ones_col_f"]
        OR = p.const["ones_row_f"]
        V = lambda fn: q.op("vector", fn)
        A = lambda fn: q.op("scalar", fn)
        M = lambda fn: q.op("tensor", fn)
        G = lambda fn, inc=16: q.op("gpsimd", fn, inc)

        def first(v):
            v.wait_ge(ld.h, ld_all)
            return v.reduce_max(aux[:, 16:17], sc[:, :], axis=AX.X)
        V(first)
        V(lambda v: v.tensor_reduce(aux[:, 17:18], sc[:, :], axis=AX.X, op=ALU.min))
        M(lambda t: t.transpose(ps_s[0:1, 0:128], aux[:, 16:17],
                                p.const["identf"][:, :]))
        A(lambda s: s.activation(out=zz[0:1, 0:128], in_=ps_s[0:1, 0:128],
                                 func=AF.Copy))
        V(lambda v: v.reduce_max(smll[:, 0:1], zz[0:1, 0:128], axis=AX.X))
        M(lambda t: t.transpose(ps_s[0:1, 0:128], aux[:, 17:18],
                                p.const["identf"][:, :]))
        A(lambda s: s.activation(out=zz[0:1, 0:128], in_=ps_s[0:1, 0:128],
                                 func=AF.Copy))
        V(lambda v: v.tensor_reduce(smll[:, 1:2], zz[0:1, 0:128], axis=AX.X,
                                    op=ALU.min))
        M(lambda t: t.matmul(ps_s[:, 0:2], OR[:, :], smll[:, 0:2],
                             start=True, stop=True))
        A(lambda s: s.activation(out=aux[:, 6:7], in_=ps_s[:, 0:1], func=AF.Copy))
        A(lambda s: s.activation(out=aux[:, 1:2], in_=ps_s[:, 0:1],
                                 func=AF.Identity, bias=p.fc(1.0)))
        A(lambda s: s.activation(out=aux[:, 0:1], in_=ps_s[:, 1:2],
                                 func=AF.Identity, bias=p.fc(-1.0)))
        V(lambda v: v.tensor_scalar(aux[:, 7:8], aux[:, 6:7], -1.0, None,
                                    op0=ALU.mult))
        for _ in range(NI):
            V(lambda v: v.tensor_add(aux[:, 2:3], aux[:, 0:1], aux[:, 1:2]))
            V(lambda v: v.tensor_scalar_mul(aux[:, 2:3], aux[:, 2:3], 0.5))
            V(lambda v: v.tensor_scalar(msk[:, :], sc[:, :], aux[:, 2:3], 0.0,
                                        op0=ALU.is_gt, op1=ALU.add,
                                        accum_out=aux[:, 3:4]))
            M(lambda t: t.matmul(ps_1[:, 0:1], OC[:, :], aux[:, 3:4],
                                 start=True, stop=True))
            A(lambda s: s.activation(out=smll[:, 2:3], in_=ps_1[:, 0:1],
                                     func=AF.Copy))
            M(lambda t: t.matmul(ps_s[:, 4:5], OR[:, :], smll[:, 2:3],
                                 start=True, stop=True))
            A(lambda s: s.activation(out=aux[:, 4:5], in_=ps_s[:, 4:5],
                                     func=AF.Copy))
            V(lambda v: v.tensor_scalar(aux[:, 5:6], aux[:, 4:5], 2047.5, None,
                                        op0=ALU.is_gt))
            V(lambda v: v.tensor_sub(aux[:, 18:19], aux[:, 2:3], aux[:, 0:1]))
            V(lambda v: v.tensor_sub(aux[:, 19:20], aux[:, 1:2], aux[:, 2:3]))
            V(lambda v: v.tensor_mul(aux[:, 18:19], aux[:, 18:19], aux[:, 5:6]))
            V(lambda v: v.tensor_mul(aux[:, 19:20], aux[:, 19:20], aux[:, 5:6]))
            V(lambda v: v.tensor_add(aux[:, 0:1], aux[:, 0:1], aux[:, 18:19]))
            V(lambda v: v.tensor_add(aux[:, 1:2], aux[:, 2:3], aux[:, 19:20]))
        V(lambda v: v.tensor_scalar(strict[:, :], sc[:, :], aux[:, 1:2], 0.0,
                                    op0=ALU.is_gt, op1=ALU.add,
                                    accum_out=aux[:, 8:9]))
        V(lambda v: v.tensor_scalar(gtlo[:, :], sc[:, :], aux[:, 0:1], 0.0,
                                    op0=ALU.is_gt, op1=ALU.add,
                                    accum_out=aux[:, 9:10]))
        V(lambda v: v.tensor_sub(eq[:, :], gtlo[:, :], strict[:, :]))
        V(lambda v: v.tensor_sub(aux[:, 10:11], aux[:, 9:10], aux[:, 8:9]))
        V(lambda v: v.memset(zrs[:, :], 0.0))
        V(lambda v: v.tensor_tensor_scan(scn[:, :], eq[:, :], zrs[:, :], 0.0,
                                         op0=ALU.add, op1=ALU.add))
        M(lambda t: t.matmul(ps_1[:, 0:1], OC[:, :], aux[:, 8:9],
                             start=True, stop=True))
        A(lambda s: s.activation(out=smll[:, 2:3], in_=ps_1[:, 0:1],
                                 func=AF.Identity, scale=-1.0,
                                 bias=p.fc(2048.0)[0:1, :]))
        M(lambda t: t.matmul(ps_s[:, 4:5], OR[:, :], smll[:, 2:3],
                             start=True, stop=True))
        A(lambda s: s.activation(out=aux[:, 11:12], in_=ps_s[:, 4:5], func=AF.Copy))
        M(lambda t: t.matmul(ps_s[:, 8:9], p.const["tri_lt"][:, :], aux[:, 10:11],
                             start=True, stop=True))
        A(lambda s: s.activation(out=aux[:, 12:13], in_=ps_s[:, 8:9], func=AF.Copy))
        V(lambda v: v.tensor_scalar_add(rnk[:, :], scn[:, :], aux[:, 12:13]))
        V(lambda v: v.tensor_scalar(msk[:, :], rnk[:, :], aux[:, 11:12], None,
                                    op0=ALU.is_le))
        V(lambda v: v.tensor_mul(msk[:, :], eq[:, :], msk[:, :]))
        V(lambda v: v.tensor_add(sel[:, :], strict[:, :], msk[:, :]))
        A(lambda s: s.activation(out=zz[:, :], in_=sc[:, :], func=AF.Exp,
                                 bias=aux[:, 7:8]))
        V(lambda v: v.scalar_tensor_tensor(w2d[:, :], zz[:, :], 1.0, sel[:, :],
                                           op0=ALU.mult, op1=ALU.mult,
                                           accum_out=aux[:, 16:17]))
        M(lambda t: t.matmul(ps_1[:, 0:1], OC[:, :], aux[:, 16:17],
                             start=True, stop=True))
        A(lambda s: s.activation(out=smll[:, 3:4], in_=ps_1[:, 0:1], func=AF.Copy))
        V(lambda v: v.reciprocal(smll[:, 4:5], smll[:, 3:4]))
        V(lambda v: v.tensor_scalar_mul(smll[:, 5:6], smll[:, 4:5], 0.5))
        M(lambda t: t.matmul(ps_s[:, 4:5], OR[:, :], smll[:, 5:6],
                             start=True, stop=True))
        A(lambda s: s.activation(out=aux[:, 15:16], in_=ps_s[:, 4:5], func=AF.Copy))
        V(lambda v: v.tensor_scalar(w2d[:, :], w2d[:, :], aux[:, 15:16], None,
                                    op0=ALU.mult))
        V(lambda v: v.tensor_scalar(msk[:, :], sel[:, :], 1.0, 0.0,
                                    op0=ALU.mult, op1=ALU.add,
                                    accum_out=aux[:, 13:14]))
        V(lambda v: v.tensor_tensor_scan(scn[:, :], sel[:, :], zrs[:, :], 0.0,
                                         op0=ALU.add, op1=ALU.add))
        M(lambda t: t.matmul(ps_s[:, 8:9], p.const["tri_lt"][:, :], aux[:, 13:14],
                             start=True, stop=True))
        A(lambda s: s.activation(out=aux[:, 14:15], in_=ps_s[:, 8:9], func=AF.Copy))
        V(lambda v: v.tensor_sub(scn[:, :], scn[:, :], sel[:, :]))
        V(lambda v: v.tensor_scalar_add(rnk[:, :], scn[:, :], aux[:, 14:15]))
        V(lambda v: v.tensor_copy(offsi[:, :], rnk[:, :]))
        V(lambda v: v.tensor_single_scalar(tidi[:, :], offsi[:, :], 7,
                                           op=ALU.arith_shift_right))
        V(lambda v: v.tensor_single_scalar(offsi[:, :], offsi[:, :], 127,
                                           op=ALU.bitwise_and))
        V(lambda v: v.tensor_copy(scn[:, :], offsi[:, :]))
        V(lambda v: v.tensor_copy(offs[:, :], tidi[:, :]))
        V(lambda v: v.scalar_tensor_tensor(eq[:, :], offs[:, :], 16384.0,
                                           p.const["tid"][:, :],
                                           op0=ALU.mult, op1=ALU.add))
        V(lambda v: v.tensor_scalar(msk[:, :], scn[:, :], 1.0, None, op0=ALU.add))
        V(lambda v: v.tensor_mul(msk[:, :], msk[:, :], sel[:, :]))
        V(lambda v: v.tensor_scalar(msk[:, :], msk[:, :], 1.0, None,
                                    op0=ALU.subtract))
        for qq in range(128):
            V(lambda v, qq=qq: v.tensor_single_scalar(gtlo[:, :], msk[:, :],
                                                      float(qq), op=ALU.is_equal))
            V(lambda v, qq=qq: v.scalar_tensor_tensor(
                zrs[:, :], eq[:, :], 1.0, gtlo[:, :], op0=ALU.mult, op1=ALU.mult,
                accum_out=strict[:, qq:qq + 1]))
        M(lambda t: t.transpose(ps_s[:, :], strict[:, :], p.const["identf"][:, :]))
        A(lambda s: s.activation(out=w2d[:, :] if False else zz[:, :],
                                 in_=ps_s[:, :], func=AF.Copy))
        V(lambda v: v.tensor_copy(offsi[:, :], zz[:, :]))
        V(lambda v: v.tensor_single_scalar(tidi[:, :], offsi[:, :], 14,
                                           op=ALU.arith_shift_right))
        V(lambda v: v.tensor_single_scalar(offsi[:, :], offsi[:, :], 16383,
                                           op=ALU.bitwise_and))
        V(lambda v: v.tensor_copy(offs[:, :], offsi[:, :]))
        V(lambda v: v.tensor_copy(zz[:, :], tidi[:, :]))
        for g_ in range(16):
            V(lambda v, g_=g_: v.tensor_single_scalar(gtlo[:, :], zz[:, :],
                                                      float(g_), op=ALU.is_equal))
            V(lambda v, g_=g_: v.scalar_tensor_tensor(
                zrs[:, :], offs[:, :], 1.0, gtlo[:, :], op0=ALU.mult, op1=ALU.mult,
                accum_out=rnk[:, g_:g_ + 1]))
        V(lambda v: v.tensor_copy(cidxs[:, :], rnk[:, 0:16]))
        rbnd = [None]

        def _setup_reg(g):
            rbnd[0] = g.alloc_register("rbnd")
            return g.reg_mov(rbnd[0], 16383)
        G(_setup_reg, inc=1)
        G(lambda g: g.dma_start(out=w_hbm.ap().rearrange("(a b) c -> a (b c)", b=128),
                                in_=w2d[:, :]))
        for g_ in range(16):
            G(lambda g, g_=g_: g.indirect_dma_start(
                wslot[:, g_:g_ + 1], None,
                w_hbm.ap()[:, :],
                IndirectOffsetOnAxis(ap=cidxs[:, g_:g_ + 1], axis=0),
                bounds_check=rbnd[0], oob_is_err=False))
        for g_ in range(16):
            G(lambda g, g_=g_: g.indirect_dma_start(
                zg[:, g_ * D:(g_ + 1) * D], None,
                z_h, IndirectOffsetOnAxis(ap=cidxs[:, g_:g_ + 1], axis=0),
                bounds_check=rbnd[0], oob_is_err=False))
        for g_ in range(16):
            G(lambda g, g_=g_: g.dma_start(
                out=cidx_t.ap()[g_ * 128:(g_ + 1) * 128, :],
                in_=cidxs[:, g_:g_ + 1]))

        with nc.Block() as blk:
            q.emit(blk, {"vector": blk.vector, "scalar": blk.scalar,
                         "tensor": blk.tensor, "gpsimd": blk.gpsimd})
        sq_gate = sq.n

        with nc.Block() as blk:
            mm0, ak0, vk0, tx0 = mm.n, ak.n, vk.n, tx.n

            @blk.sync
            def _(sync):
                sync.wait_ge(sq.h, sq_gate)
                for t in range(T):
                    for c in range(8):
                        tx.inc(sync.dma_start_transpose(
                            zgf[:, c * S + t * 128:c * S + t * 128 + 128],
                            zg[:, t * D + c * 128:t * D + c * 128 + 128]), 16)

            @blk.tensor
            def _(t):
                t.wait_ge(tx.h, tx0 + 2048)
                t.wait_ge(lg.h, lg_w)
                for tc in range(4):
                    for hc in range(16):
                        if tc * 16 + hc > 0:
                            t.wait_ge(vk.h, vk0 + 16 * tc + hc)
                        for c in range(8):
                            i1 = t.matmul(
                                ps_a[:, :],
                                w1b[:, c * 2048 + hc * 128:c * 2048 + hc * 128 + 128],
                                zgf[:, c * S + tc * 512:c * S + tc * 512 + 512],
                                start=(c == 0), stop=(c == 7))
                        mm.inc(i1)
                        for c in range(8):
                            i3 = t.matmul(
                                ps_b[:, :],
                                w3b[:, c * 2048 + hc * 128:c * 2048 + hc * 128 + 128],
                                zgf[:, c * S + tc * 512:c * S + tc * 512 + 512],
                                start=(c == 0), stop=(c == 7))
                        mm.inc(i3)
                    t.wait_ge(vk.h, vk0 + 16 * (tc + 1))
                    for sub in range(4):
                        if sub >= 1:
                            t.wait_ge(ak.h, ak0 + 20 * tc + 16 + sub)
                        elif tc >= 1:
                            t.wait_ge(ak.h, ak0 + 20 * tc)
                        for hc in range(16):
                            for hhx in range(2):
                                iy = t.matmul(
                                    ps_o[:, hhx * 512:(hhx + 1) * 512],
                                    hhs[:, hc * 512 + sub * 128:hc * 512 + sub * 128 + 128],
                                    w2b[:, hc * D + hhx * 512:hc * D + hhx * 512 + 512],
                                    start=(hc == 0), stop=(hc == 15))
                        mm.inc(iy)

            @blk.scalar
            def _(s):
                for tc in range(4):
                    for hc in range(16):
                        s.wait_ge(mm.h, mm0 + 36 * tc + 2 * hc + 1)
                        if tc * 16 + hc >= 2:
                            s.wait_ge(vk.h, vk0 + 16 * tc + hc - 1)
                        ak.inc(s.activation(
                            out=sA[:, (hc % 2) * 512:(hc % 2) * 512 + 512],
                            in_=ps_a[:, :], func=AF.Silu))
                    for sub in range(4):
                        s.wait_ge(mm.h, mm0 + 36 * tc + 32 + sub + 1)
                        tt = 4 * tc + sub
                        if tt >= 2:
                            psem = [st, s2][tt % 2]
                            s.wait_ge(psem.h, 16 * ((tt - 2) // 2 + 1))
                        ak.inc(s.activation(
                            out=yt[:, (sub % 2) * D:(sub % 2) * D + D],
                            in_=ps_o[:, :], func=AF.Identity,
                            scale=wslot[:, 4 * tc + sub:4 * tc + sub + 1]))

            @blk.vector
            def _(v):
                for tc in range(4):
                    for hc in range(16):
                        v.wait_ge(mm.h, mm0 + 36 * tc + 2 * hc + 2)
                        v.wait_ge(ak.h, ak0 + 20 * tc + hc + 1)
                        vk.inc(v.tensor_mul(
                            hhs[:, hc * 512:(hc + 1) * 512],
                            ps_b[:, :],
                            sA[:, (hc % 2) * 512:(hc % 2) * 512 + 512]))

            @blk.gpsimd
            def _(g):
                for tc in range(4):
                    for sub in range(4):
                        tt = 4 * tc + sub
                        g.wait_ge(ak.h, ak0 + 20 * tc + 16 + sub + 1)
                        psem = [st, s2][tt % 2]
                        psem.inc(g.dma_start(out=y_h[tt * 128:(tt + 1) * 128, :],
                                             in_=yt[:, (sub % 2) * D:(sub % 2) * D + D]), 16)
                g.wait_ge(st.h, st.n)
                g.wait_ge(s2.h, s2.n)
    return p


def build_l0_ffn_a3(nexp=8):
    p = Prog("l0ffn_a3")
    nc = p.nc
    x_h = nc.dram_tensor("x", [S, D], F32, kind="ExternalInput").ap()
    y_h = nc.dram_tensor("y_all", [8 * S, D], BF16, kind="ExternalInput").ap()
    ci_h = nc.dram_tensor("cidx_cols", [128, 128], I32, kind="ExternalInput").ap()
    rb_h = nc.dram_tensor("rank_base", [1, 1], F32, kind="ExternalInput").ap()
    xout_h = nc.dram_tensor("xout", [S, D], F32, kind="ExternalOutput").ap()
    acc_t = nc.dram_tensor("acc", [S, D], BF16, kind="Internal")

    with p.stack:
        p.decl_consts(["identf", "ones_col_f", "ones_row_f"])
        p.arena(160 * KB)
        x_sb = p.at("x_sb", 0, [128, T * D], F32)
        ybuf = p.at("ybuf_", 64 * KB, [128, 2 * D], BF16)
        zrot = p.at("zrot_", 68 * KB, [128, D], BF16)
        accs = p.at("accs_", 72 * KB, [128, 2 * D], BF16)
        xo = p.at("xo_", 80 * KB, [128, 2 * D], F32)
        cif = p.at("cif_", 96 * KB, [128, 128], F32)
        cii = p.at("cii_", 96 * KB + 512, [128, 128], I32)
        loc = p.at("loc_", 97 * KB, [128, 128], F32)
        loci = p.at("loci_", 97 * KB + 512, [128, 128], I32)
        msk = p.at("msk_", 98 * KB, [128, 128], F32)
        rbb = p.at("rbb_", 98 * KB + 512, [128, 1], F32)

        ps_s = p.ps("ps_s", [128, 128])

        ld = p.sem("ld")
        lg = p.sem("lg")
        sq = p.sem("sq")
        mm = p.sem("mm")
        ak = p.sem("ak")
        vk = p.sem("vk")
        st = p.sem("st")
        s2 = p.sem("s2")

        with nc.Block() as blk:
            @blk.sync
            def _(sync):
                p.emit_const_loads(sync, ld)
                ld.inc(sync.dma_start(out=cii[:, :], in_=ci_h[:, :]), 16)
                ld.inc(sync.dma_start(out=rbb[0:1, 0:1], in_=rb_h[:, :]), 16)
                for t in range(T):
                    ld.inc(sync.dma_start(out=x_sb[:, t * D:(t + 1) * D],
                                          in_=x_h[t * 128:(t + 1) * 128, :]), 16)
        ld_all = ld.n

        import os as _os
        _noprep = _os.environ.get("A3_NOPREP") == "1"
        q = Seq(sq)

        def first(v):
            v.wait_ge(ld.h, ld_all)
            return v.tensor_copy(cif[:, :], cii[:, :])
        if not _noprep:
            q.op("vector", first)
            q.op("tensor", lambda t: t.matmul(ps_s[:, 0:1], p.const["ones_row_f"][:, :],
                                              rbb[0:1, 0:1], start=True, stop=True))
            q.op("scalar", lambda s: s.activation(out=rbb[:, :], in_=ps_s[:, 0:1],
                                                  func=AF.Copy))
            q.op("vector", lambda v: v.tensor_scalar(loc[:, :], cif[:, :], rbb[:, 0:1],
                                                     None, op0=ALU.subtract))
            q.op("vector", lambda v: v.tensor_single_scalar(msk[:, :], loc[:, :], 0.0,
                                                            op=ALU.is_lt))
            q.op("vector", lambda v: v.scalar_tensor_tensor(loc[:, :], msk[:, :], 1e9,
                                                            loc[:, :], op0=ALU.mult,
                                                            op1=ALU.add))
            q.op("vector", lambda v: v.tensor_copy(loci[:, :], loc[:, :]))

        def first2(v):
            if _noprep:
                v.wait_ge(ld.h, ld_all)
            return v.memset(zrot[:, :], 0.0)
        q.op("vector", first2)
        for t in range(T):
            q.op("gpsimd", lambda g, t=t: g.dma_start(
                out=acc_t.ap()[t * 128:(t + 1) * 128, :], in_=zrot[:, :]), 16)
        with nc.Block() as blk:
            q.emit(blk, {"vector": blk.vector, "scalar": blk.scalar,
                         "tensor": blk.tensor, "gpsimd": blk.gpsimd})
        sq_prep = sq.n

        with nc.Block() as blk:
            lg0, sq0 = lg.n, sq.n

            @blk.sync
            def _(sync):
                sync.wait_ge(sq.h, sq_prep)
                for i in range(nexp * 16):
                    if i >= 2:
                        sync.wait_ge(sq.h, sq0 + 16 * (i - 1))
                    psem = [ld, lg][i % 2]
                    psem.inc(sync.dma_start(
                        out=ybuf[:, (i % 2) * D:(i % 2) * D + D],
                        in_=y_h[i * 128:(i + 1) * 128, :]), 16)

            @blk.gpsimd
            def _(g):
                rb = g.alloc_register("rb_a3")
                g.reg_mov(rb, S - 1)
                nld = ld_all
                nlg = lg0
                for e in range(nexp):
                    if e > 0:
                        g.wait_ge(sq.h, sq0 + 256 * e)
                    for g_ in range(16):
                        i = e * 16 + g_
                        if i % 2 == 0:
                            nld += 16
                            g.wait_ge(ld.h, nld)
                        else:
                            nlg += 16
                            g.wait_ge(lg.h, nlg)
                        sq.inc(g.indirect_dma_start(
                            acc_t.ap()[:, :],
                            IndirectOffsetOnAxis(ap=loci[:, i:i + 1], axis=0),
                            ybuf[:, (i % 2) * D:(i % 2) * D + D], None,
                            bounds_check=rb, oob_is_err=False,
                            compute_op=ALU.add), 16)
        sq_sc = sq.n

        _noread = _os.environ.get("A3_NOREAD") == "1"
        with nc.Block() as blk:
            lg1, vk0 = lg.n, vk.n

            if _noread:
                @blk.gpsimd
                def _(g):
                    g.wait_ge(sq.h, sq_sc)
                    for t in range(T):
                        st.inc(g.dma_start(out=xout_h[t * 128:(t + 1) * 128, :],
                                           in_=x_sb[:, t * D:(t + 1) * D]), 16)
                    g.wait_ge(st.h, st.n)
            else:
                mm0 = mm.n

                def load(g, t):
                    psem = [lg, mm][t % 2]
                    psem.inc(g.dma_start(
                        out=accs[:, (t % 2) * D:(t % 2) * D + D],
                        in_=acc_t.ap()[t * 128:(t + 1) * 128, :]), 16)

                @blk.gpsimd
                def _(g):
                    g.wait_ge(sq.h, sq_sc)
                    load(g, 0)
                    load(g, 1)
                    for t in range(T):
                        g.wait_ge(vk.h, vk0 + t + 1)
                        psem = [st, s2][t % 2]
                        psem.inc(g.dma_start(out=xout_h[t * 128:(t + 1) * 128, :],
                                             in_=xo[:, (t % 2) * D:(t % 2) * D + D]), 16)
                        if t + 2 < T:
                            load(g, t + 2)
                    g.wait_ge(st.h, st.n)
                    g.wait_ge(s2.h, s2.n)

                @blk.vector
                def _(v):
                    for t in range(T):
                        psem = [lg, mm][t % 2]
                        base = lg1 if t % 2 == 0 else mm0
                        v.wait_ge(psem.h, base + 16 * (t // 2 + 1))
                        if t >= 2:
                            psem = [st, s2][t % 2]
                            v.wait_ge(psem.h, 16 * ((t - 2) // 2 + 1))
                        vk.inc(v.tensor_add(xo[:, (t % 2) * D:(t % 2) * D + D],
                                            x_sb[:, t * D:(t + 1) * D],
                                            accs[:, (t % 2) * D:(t % 2) * D + D]))
    return p


_P_A1 = _P_A2 = _P_A3 = None


def run_l0_ffn(x_full, Wg, W1, W2, W3, **kw):
    global _P_A1, _P_A2, _P_A3
    if _P_A1 is None:
        _P_A1 = build_l0_ffn_a1()
        _P_A2 = build_l0_ffn_a2()
        _P_A3 = build_l0_ffn_a3()
    cm1 = consts_in_map(A1_CONSTS)
    cm1.update(_P_A1.fvals_in_map())
    im = []
    for c in range(NCORE):
        d = dict(cm1)
        d["x"] = np.ascontiguousarray(x_full[c])
        d["WgT"] = np.ascontiguousarray(Wg.T)
        im.append(d)
    r1 = _run(_P_A1, im, **kw)
    z_all = np.concatenate([np.ascontiguousarray(r["zout"].T) for r in r1.results], 0)
    sc_all = np.stack([r["scout"] for r in r1.results], 0)
    cm2 = consts_in_map(["identf", "ones_col_b", "ones_col_f", "ones_row_b",
                         "ones_row_f", "tri_lt", "tid"])
    cm2.update(_P_A2.fvals_in_map())
    im = []
    for e in range(NCORE):
        d = dict(cm2)
        d["z_all"] = z_all
        d["sce"] = np.ascontiguousarray(
            sc_all[:, e, :].reshape(16384).reshape(128, 128))
        d["W1eT"] = np.ascontiguousarray(W1[e].T)
        d["W3eT"] = np.ascontiguousarray(W3[e].T)
        d["W2eT"] = np.ascontiguousarray(W2[e].T)
        im.append(d)
    r2 = _run(_P_A2, im, **kw)
    y_all = np.concatenate([r["yout"] for r in r2.results], 0)
    cidx_all = np.stack([r["cidx"].reshape(2048) for r in r2.results], 0)
    cm3 = consts_in_map(["identf", "ones_col_f", "ones_row_f"])
    im = []
    cidx_cols = np.zeros((128, 128), np.int32)
    for e in range(8):
        for g in range(16):
            cidx_cols[:, e * 16 + g] = cidx_all[e, g * 128:(g + 1) * 128]
    for c in range(NCORE):
        d = dict(cm3)
        d["x"] = np.ascontiguousarray(x_full[c])
        d["y_all"] = y_all
        d["cidx_cols"] = cidx_cols
        d["rank_base"] = np.full((1, 1), 2048.0 * c, np.float32)
        im.append(d)
    r3 = _run(_P_A3, im, **kw)
    return np.stack([r["xout"] for r in r3.results], 0)


_P_L1ATTN = _P_L1FFN = None


def run_l1_attn(x_full, Wqkv_w, b_qkv, out_w, b_out, dd, **kw):
    global _P_L1ATTN
    if _P_L1ATTN is None:
        _P_L1ATTN = build_l1_attn()
    cm = consts_in_map(["identf", "ones_col_b", "ones_row_b", "ones_row_f",
                        "dist2d", "hsel"])
    cm.update(_P_L1ATTN.fvals_in_map())
    wqkvT = np.ascontiguousarray(Wqkv_w.T)
    woT = np.ascontiguousarray(out_w.T)
    ddr = np.asarray(dd, np.float32).reshape(1, 1)
    im = []
    for c in range(NCORE):
        d = dict(cm)
        d["x"] = np.ascontiguousarray(x_full[c])
        d["WqkvT"] = wqkvT
        d["b_qkv"] = b_qkv
        d["WoT"] = woT
        d["b_out"] = b_out
        d["dd"] = ddr
        im.append(d)
    res = _run(_P_L1ATTN, im, **kw)
    return np.stack([r["xout"] for r in res.results], axis=0)


def run_l1_ffn(x_full, g1, g2, Wq, Wdkv, Wuk, Wuv, Wo, gate_w, w1, w2, **kw):
    global _P_L1FFN
    if _P_L1FFN is None:
        _P_L1FFN = build_l1_ffn()
    cm = consts_in_map(["identf", "ones_col_b", "ones_row_b", "ones_row_f",
                        "hsel", "trimask"])
    cm.update(_P_L1FFN.fvals_in_map())
    im = []
    for c in range(NCORE):
        d = dict(cm)
        d["x"] = np.ascontiguousarray(x_full[c])
        d.update(g1=g1, g2=g2, Wq=Wq, Wdkv=Wdkv, Wuk=Wuk, Wuv=Wuv, Wo=Wo,
                 gate_w=gate_w, w1=w1, w2=w2)
        im.append(d)
    res = _run(_P_L1FFN, im, **kw)
    return np.stack([r["xout"] for r in res.results], axis=0)


def kernel(**inputs):
    x = np.ascontiguousarray(np.asarray(inputs["L0_attn_x"], np.float32))
    Wg = np.asarray(inputs["L0_ffn_Wg"], np.float32)
    W1 = np.asarray(inputs["L0_ffn_W1"], np.float32)
    W2 = np.asarray(inputs["L0_ffn_W2"], np.float32)
    W3 = np.asarray(inputs["L0_ffn_W3"], np.float32)
    x = run_l0_ffn(x, Wg, W1, W2, W3)
    x = run_l0_attn(x,
                    np.asarray(inputs["L0_attn_W_qkv"], np.float32),
                    np.asarray(inputs["L0_attn_b_qkv"], np.float32),
                    np.asarray(inputs["L0_attn_W_out"], np.float32),
                    np.asarray(inputs["L0_attn_b_out"], np.float32))
    x = run_l0_ffn(x, Wg, W1, W2, W3)
    ffn1 = dict(
        g1=np.asarray(inputs["L1_ffn_g1"], np.float32),
        g2=np.asarray(inputs["L1_ffn_g2"], np.float32),
        Wq=np.asarray(inputs["L1_ffn_Wq"], np.float32),
        Wdkv=np.asarray(inputs["L1_ffn_Wdkv"], np.float32),
        Wuk=np.asarray(inputs["L1_ffn_Wuk"], np.float32),
        Wuv=np.asarray(inputs["L1_ffn_Wuv"], np.float32),
        Wo=np.asarray(inputs["L1_ffn_Wo"], np.float32),
        gate_w=np.asarray(inputs["L1_ffn_gate_w"], np.float32),
        w1=np.asarray(inputs["L1_ffn_w1"], np.float32),
        w2=np.asarray(inputs["L1_ffn_w2"], np.float32))
    x = run_l1_ffn(x, **ffn1)
    x = run_l1_attn(x,
                    np.asarray(inputs["L1_attn_Wqkv_w"], np.float32),
                    np.asarray(inputs["L1_attn_Wqkv_b"], np.float32),
                    np.asarray(inputs["L1_attn_out_w"], np.float32),
                    np.asarray(inputs["L1_attn_out_b"], np.float32),
                    np.asarray(inputs["L1_attn_distance_decay"], np.float32))
    x = run_l1_ffn(x, **ffn1)
    return np.ascontiguousarray(x.astype(np.float32))


# version 03
# speedup vs baseline: 1.0408x; 1.0408x over previous
import math
import numpy as np
import ml_dtypes
import contextlib
import concourse.bass as bass
import concourse.mybir as mybir
from concourse.bass import IndirectOffsetOnAxis
from concourse.bass_utils import run_bass_kernel_spmd

F32 = mybir.dt.float32
BF16 = mybir.dt.bfloat16
I32 = mybir.dt.int32
AF = mybir.ActivationFunctionType
ALU = mybir.AluOpType
AX = mybir.AxisListType

S = 2048
T = 16
D = 1024
NCORE = 8
BF = ml_dtypes.bfloat16
KB = 1024


def host_consts():
    c = {}
    c["identf"] = np.eye(128, dtype=np.float32)
    c["ones_col_b"] = np.ones((128, 1), dtype=BF)
    c["ones_col_f"] = np.ones((128, 1), dtype=np.float32)
    c["ones_row_b"] = np.ones((1, 128), dtype=BF)
    c["ones_row_f"] = np.ones((1, 128), dtype=np.float32)
    c["tri_lt"] = (np.arange(128)[:, None] < np.arange(128)[None, :]).astype(np.float32)
    c["tid"] = (128.0 * np.arange(128)[:, None] + np.arange(128)[None, :]).astype(np.float32)
    tm = np.zeros((128, 4 * 512), dtype=BF)
    for j in range(4):
        f = np.arange(512)[None, :]
        pp = np.arange(128)[:, None]
        tm[:, j * 512:(j + 1) * 512] = (f - pp >= 128 * j).astype(BF)
    c["trimask"] = tm
    c["ramp4096"] = np.abs((32.0 * np.arange(128)[:, None] + np.arange(32)[None, :]) - 2048.0).astype(np.float32)
    c["dist2d"] = np.abs(np.arange(4096)[None, :] - 2048 - np.arange(128)[:, None]).astype(np.int16)
    hs = np.zeros((128, 8 * 16), dtype=BF)
    for cc in range(8):
        for p_ in range(128):
            hs[p_, 16 * cc + 2 * cc + (p_ >= 64)] = 1.0
    c["hsel"] = hs
    return c


CONST_SPECS = {
    "identf": ([128, 128], F32),
    "ones_col_b": ([128, 1], BF16), "ones_col_f": ([128, 1], F32),
    "ones_row_b": ([1, 128], BF16),
    "ones_row_f": ([1, 128], F32),
    "tri_lt": ([128, 128], F32), "tid": ([128, 128], F32),
    "trimask": ([128, 4 * 512], BF16), "ramp4096": ([128, 32], F32),
    "hsel": ([128, 8 * 16], BF16), "dist2d": ([128, 4096], mybir.dt.int16),
}


class Sem:
    def __init__(self, h):
        self.h = h
        self.n = 0

    def inc(self, inst, k=1):
        inst.then_inc(self.h, k)
        self.n += k
        return inst


class Prog:
    def __init__(self, name):
        self.nc = bass.Bass("TRN2", target_bir_lowering=False, name=name)
        self.stack = contextlib.ExitStack()
        self.const = {}

    def sem(self, name):
        return Sem(self.stack.enter_context(self.nc.semaphore(name)))

    def sb(self, name, shape, dtype):
        return self.stack.enter_context(self.nc.sbuf_tensor(name, shape, dtype))

    def ps(self, name, shape, dtype=F32):
        return self.stack.enter_context(self.nc.psum_tensor(name, shape, dtype))

    def arena(self, nbytes):
        self._arena = self.stack.enter_context(
            self.nc.sbuf_tensor("arena", [128, nbytes], mybir.dt.uint8,
                                is_transpose_dest=True))
        self._arena_addr = self.nc.lookup_mloc(self._arena).addr

    def at(self, name, off, shape, dtype):
        assert off % 64 == 0
        return self.nc.alloc_sbuf_tensor_at(name, shape, dtype,
                                            offset=self._arena_addr + off)

    def decl_fvals(self, values):
        self.fvals = list(values)
        nc = self.nc
        self.fvals_ap = nc.dram_tensor("c_fvals", [128, max(1, len(self.fvals))], F32,
                                       kind="ExternalInput").ap()
        self.fvals_sb = self.sb("k_fvals", [128, max(1, len(self.fvals))], F32)

    def fc(self, v):
        i = self.fvals.index(v)
        return self.fvals_sb[:, i:i + 1]

    def fvals_in_map(self):
        a = np.zeros((128, max(1, len(self.fvals))), np.float32)
        for i, v in enumerate(self.fvals):
            a[:, i] = v
        return {"c_fvals": a}

    def decl_consts(self, names=None):
        nc = self.nc
        self.const_aps = {}
        self.const_names = list(names) if names else list(CONST_SPECS)
        for nm in self.const_names:
            shape, dt = CONST_SPECS[nm]
            self.const_aps[nm] = nc.dram_tensor("c_" + nm, shape, dt, kind="ExternalInput").ap()
            self.const[nm] = self.sb("k_" + nm, shape, dt)

    def emit_const_loads(self, sync, ld):
        for nm in self.const_names:
            ld.inc(sync.dma_start(out=self.const[nm][:, :], in_=self.const_aps[nm][:, :]), 16)
        if hasattr(self, "fvals_ap"):
            ld.inc(sync.dma_start(out=self.fvals_sb[:, :], in_=self.fvals_ap[:, :]), 16)


def consts_in_map(names=None):
    hc = host_consts()
    names = names or list(CONST_SPECS)
    return {("c_" + k): hc[k] for k in names}


L0ATTN_CONSTS = ["identf", "ones_col_b", "ones_row_b", "ones_row_f"]



def emit_ln(p, blk, *, x_sb, zt, zf, stats, ld, ak, vk, tx, wait_ld, eps_ap):
    nc = p.nc
    ak0, vk0, tx0 = ak.n, vk.n, tx.n
    rs = p.sem(f"lnrs{ak0}")
    rv = p.sem(f"lnrv{ak0}")
    rs0, rv0 = rs.n, rv.n

    @blk.scalar
    def _(s):
        s.wait_ge(ld.h, wait_ld)
        for t in range(T):
            if t >= 2:
                s.wait_ge(rs.h, rs0 + t - 1)
            inst = s.activation(out=zt[:, (t % 2) * D:(t % 2) * D + D],
                                in_=x_sb[:, t * D:(t + 1) * D], func=AF.Square,
                                accum_out=stats[:, T + t:T + t + 1])
            if t == T - 1:
                ak.inc(inst)
            else:
                rs.inc(inst)

    @blk.vector
    def _(v):
        v.wait_ge(ld.h, wait_ld)
        for t in range(T):
            rv.inc(v.tensor_reduce(stats[:, t:t + 1], x_sb[:, t * D:(t + 1) * D],
                                   axis=AX.X, op=ALU.add))

    @blk.vector
    def _(v):
        v.wait_ge(rs.h, rs0 + T - 1)
        v.wait_ge(ak.h, ak0 + 1)
        v.wait_ge(rv.h, rv0 + T)
        vk.inc(v.tensor_scalar_mul(stats[:, 2 * T:3 * T], stats[:, 0:T], 1.0 / D))
        v.wait_ge(vk.h, vk0 + 1)
        m = stats[:, 2 * T:3 * T]
        v.tensor_mul(stats[:, 5 * T:6 * T], m, m)
        vk.inc(v.tensor_scalar(stats[:, 6 * T:7 * T], stats[:, T:2 * T], 1.0 / D,
                               None, op0=ALU.mult))
        v.wait_ge(vk.h, vk0 + 2)
        vk.inc(v.tensor_sub(stats[:, 6 * T:7 * T], stats[:, 6 * T:7 * T],
                            stats[:, 5 * T:6 * T]))

    @blk.scalar
    def _(s):
        s.wait_ge(vk.h, vk0 + 3)
        ak.inc(s.activation(out=stats[:, 5 * T:6 * T], in_=stats[:, 6 * T:7 * T],
                            func=AF.Sqrt, bias=eps_ap))

    @blk.vector
    def _(v):
        v.wait_ge(ak.h, ak0 + 2)
        vk.inc(v.reciprocal(stats[:, 3 * T:4 * T], stats[:, 5 * T:6 * T]))
        v.wait_ge(vk.h, vk0 + 4)
        vk.inc(v.scalar_tensor_tensor(stats[:, 4 * T:5 * T], stats[:, 2 * T:3 * T],
                                      -1.0, stats[:, 3 * T:4 * T],
                                      op0=ALU.mult, op1=ALU.mult))

    @blk.scalar
    def _(s):
        s.wait_ge(vk.h, vk0 + 5)
        for t in range(T):
            if t >= 2:
                s.wait_ge(tx.h, tx0 + 8 * 16 * (t - 1))
            ak.inc(s.activation(out=zt[:, (t % 2) * D:(t % 2) * D + D],
                                in_=x_sb[:, t * D:(t + 1) * D], func=AF.Identity,
                                scale=stats[:, 3 * T + t:3 * T + t + 1],
                                bias=stats[:, 4 * T + t:4 * T + t + 1]))

    @blk.sync
    def _(sync):
        for t in range(T):
            sync.wait_ge(ak.h, ak0 + 2 + t + 1)
            if t >= 1:
                sync.wait_ge(tx.h, tx0 + 128 * t)
            for c in range(8):
                tx.inc(sync.dma_start_transpose(
                    zf[:, c * S + t * 128:c * S + t * 128 + 128],
                    zt[:, (t % 2) * D + c * 128:(t % 2) * D + c * 128 + 128]), 16)


def build_l0_attn(debug=False):
    p = Prog("l0attn")
    nc = p.nc
    x_h = nc.dram_tensor("x", [S, D], F32, kind="ExternalInput").ap()
    wqkv_h = nc.dram_tensor("W_qkv", [D, 3 * D], F32, kind="ExternalInput").ap()
    bqkv_h = nc.dram_tensor("b_qkv", [3 * D], F32, kind="ExternalInput").ap()
    wout_h = nc.dram_tensor("W_out", [D, D], F32, kind="ExternalInput").ap()
    bout_h = nc.dram_tensor("b_out", [D], F32, kind="ExternalInput").ap()
    xout_h = nc.dram_tensor("xout", [S, D], F32, kind="ExternalOutput").ap()
    dbg_h = {}
    if debug:
        for nm, shape, dt in [("zf", [128, 8 * S], BF16), ("qf", [128, 8 * S], BF16),
                              ("kf", [128, 8 * S], BF16), ("vt", [128, T * D], BF16),
                              ("ctxn", [128, 8 * S], BF16), ("negMb", [128, 1], F32),
                              ("stats", [128, 8 * T], F32), ("es", [128, 16 * 512], BF16),
                              ("den", [1, 512], F32), ("rdb", [128, 512], F32),
                              ("bqk", [128, 32], F32)]:
            dbg_h[nm] = nc.dram_tensor("dbg_" + nm, shape, dt, kind="ExternalOutput").ap()

    with p.stack:
        p.decl_consts(L0ATTN_CONSTS)
        p.decl_fvals([1e-5])
        p.arena(192 * KB)
        x_sb = p.at("x_sb", 0, [128, T * D], F32)
        kf = p.at("kf_", 0, [128, 8 * S], BF16)
        vt = p.at("vt_", 32 * KB, [128, T * D], BF16)
        zf = p.at("zf_", 64 * KB, [128, 8 * S], BF16)
        es = p.at("es_", 64 * KB, [128, 16 * 512], BF16)
        sqq = p.at("sqq_", 80 * KB, [128, 2 * 512], BF16)
        t1 = p.at("t1_", 82 * KB, [128, 2 * 512], F32)
        rdb = p.at("rdb_", 86 * KB, [128, 512], F32)
        xt2 = p.at("xt2_", 88 * KB, [128, 2 * D], F32)
        wqkv = p.at("wqkv_", 96 * KB, [128, 8 * 3 * D], BF16)
        ctxn = p.at("ctxn_", 96 * KB, [128, 8 * S], BF16)
        wo = p.at("wo_", 128 * KB, [128, 8 * D], BF16)
        qf = p.at("qf_", 144 * KB, [128, 8 * S], BF16)
        boutb = p.at("boutb_", 144 * KB, [128, D], F32)
        zt = p.at("zt_", 176 * KB, [128, 2 * D], BF16)
        xo = p.at("xo_", 180 * KB, [128, 2 * D], F32)
        x2 = p.at("x2_", 0, [128, T * D], F32)

        stats = p.sb("stats", [128, 8 * T], F32)
        bqk_r = p.sb("bqk_r", [24, 128], F32)
        bqk = p.sb("bqk", [128, 32], F32)
        nrm = p.sb("nrm", [1, 16], F32)
        negMb = p.sb("negMb", [128, 1], F32)
        den_sb = p.sb("den_sb", [1, 512], F32)
        rd_row = p.sb("rd_row", [1, 512], F32)
        bout_bf = p.sb("bout_bf", [1, D], BF16)
        bvb = p.sb("bvb", [128, 8], BF16)
        bvwo = p.sb("bvwo", [1, D], BF16)

        ps_a = p.ps("ps_a", [128, 512])
        ps_b = p.ps("ps_b", [128, 512])
        ps_c = p.ps("ps_c", [128, 512])
        ps_d = p.ps("ps_d", [128, 512])
        ps_o = p.ps("ps_o", [128, 1024])
        ps_den = p.ps("ps_den", [1, 512])
        ps_bc = p.ps("ps_bc", [128, 512])

        ld = p.sem("ld")
        lg = p.sem("lg")
        tx = p.sem("tx")
        mm = p.sem("mm")
        ak = p.sem("ak")
        vk = p.sem("vk")
        st = p.sem("st")
        s2 = p.sem("s2")

        with nc.Block() as blk:
            @blk.sync
            def _(sync):
                p.emit_const_loads(sync, ld)
                for t in range(T):
                    ld.inc(sync.dma_start(out=x_sb[:, t * D:(t + 1) * D],
                                          in_=x_h[t * 128:(t + 1) * 128, :]), 16)
                ld.inc(sync.dma_start(out=bqk_r[:, :],
                                      in_=bqkv_h.rearrange("(a b) -> a b", b=128)), 16)

            @blk.gpsimd
            def _(g):
                for c in range(8):
                    lg.inc(g.dma_start(out=wqkv[:, c * 3072:(c + 1) * 3072],
                                       in_=wqkv_h[c * 128:(c + 1) * 128, :]), 16)
                lg.inc(g.dma_start(out=bout_bf[:, :], in_=bout_h[None, :]), 16)
        ld_all = ld.n
        lg_w = lg.n

        with nc.Block() as blk:
            emit_ln(p, blk, x_sb=x_sb, zt=zt, zf=zf, stats=stats,
                    ld=ld, ak=ak, vk=vk, tx=tx, wait_ld=ld_all, eps_ap=p.fc(1e-5))
        tx_zf = tx.n

        with nc.Block() as blk:
            mm0, ak0, vk0 = mm.n, ak.n, vk.n

            @blk.tensor
            def _(t):
                t.wait_ge(ld.h, ld_all)
                mm.inc(t.transpose(ps_bc[:, 0:24], bqk_r[:, :], p.const["identf"][0:24, 0:24]))
                t.wait_ge(tx.h, tx_zf)
                t.wait_ge(lg.h, lg_w)
                g = 0
                for proj in range(2):
                    for m_ in range(8):
                        for tc in range(4):
                            g += 1
                            psd = [ps_a, ps_b][g % 2]
                            if g > 2:
                                t.wait_ge(ak.h, ak0 + 2 + g - 2)
                            for c in range(8):
                                inst = t.matmul(
                                    psd[:, :],
                                    wqkv[:, c * 3072 + proj * 1024 + m_ * 128:
                                         c * 3072 + proj * 1024 + m_ * 128 + 128],
                                    zf[:, c * S + tc * 512:c * S + tc * 512 + 512],
                                    start=(c == 0), stop=(c == 7))
                            mm.inc(inst)
                for tt1 in range(1, T + 1):
                    if tt1 >= 2:
                        t.wait_ge(ak.h, ak0 + 66 + tt1 - 1)
                    for c in range(8):
                        for hh in range(2):
                            inst = t.matmul(
                                ps_o[:, hh * 512:(hh + 1) * 512],
                                zf[:, c * S + (tt1 - 1) * 128:c * S + (tt1 - 1) * 128 + 128],
                                wqkv[:, c * 3072 + 2048 + hh * 512:
                                     c * 3072 + 2048 + hh * 512 + 512],
                                start=(c == 0), stop=(c == 7))
                    mm.inc(inst)
                k = 0
                for which in range(2):
                    for nn_ in range(4):
                        ng = which * 4 + nn_ + 1
                        if ng >= 2:
                            t.wait_ge(vk.h, vk0 + ng - 1)
                        for c in range(8):
                            k += 1
                            t.wait_ge(ak.h, ak0 + 82 + k)
                            inst = t.matmul(ps_den[:, :], p.const["ones_col_b"][:, :],
                                            sqq[:, (c % 2) * 512:(c % 2) * 512 + 512],
                                            start=(c == 0), stop=(c == 7))
                            mm.inc(inst)
                t.wait_ge(vk.h, vk0 + 10)
                mm.inc(t.matmul(ps_bc[:, 100:101], p.const["ones_row_f"][:, :],
                                nrm[:, 8:9], start=True, stop=True))

            @blk.scalar
            def _(s):
                s.wait_ge(mm.h, mm0 + 1)
                ak.inc(s.activation(out=bqk[:, 8:32], in_=ps_bc[:, 0:24], func=AF.Copy))
                ak.inc(s.activation(out=bqk[:, 0:8], in_=ps_bc[:, 0:8], func=AF.Copy,
                                    scale=1.0 / 32.0))
                s.wait_ge(ak.h, ak0 + 2)
                g = 0
                for proj in range(2):
                    dst = qf if proj == 0 else kf
                    for m_ in range(8):
                        for tc in range(4):
                            g += 1
                            psd = [ps_a, ps_b][g % 2]
                            s.wait_ge(mm.h, mm0 + 1 + g)
                            ak.inc(s.activation(
                                out=dst[:, m_ * S + tc * 512:m_ * S + tc * 512 + 512],
                                in_=psd[:, :], func=AF.Identity,
                                scale=(1.0 / 32.0 if proj == 0 else 1.0),
                                bias=bqk[:, (0 if proj == 0 else 16) + m_:
                                         (0 if proj == 0 else 16) + m_ + 1]))
                for tt1 in range(1, T + 1):
                    s.wait_ge(mm.h, mm0 + 65 + tt1)
                    ak.inc(s.activation(out=vt[:, (tt1 - 1) * D:tt1 * D], in_=ps_o[:, :],
                                        func=AF.Copy))
                s.wait_ge(ak.h, ak0 + 82)
                k = 0
                for which in range(2):
                    src = qf if which == 0 else kf
                    for nn_ in range(4):
                        for c in range(8):
                            k += 1
                            if k > 2:
                                s.wait_ge(mm.h, mm0 + 81 + k - 2)
                            ak.inc(s.activation(
                                out=sqq[:, (c % 2) * 512:(c % 2) * 512 + 512],
                                in_=src[:, c * S + nn_ * 512:c * S + nn_ * 512 + 512],
                                func=AF.Square))
                s.wait_ge(vk.h, vk0 + 9)
                ak.inc(s.activation(out=nrm[:, 9:10], in_=nrm[:, 10:11], func=AF.Sqrt))
                ak.inc(s.activation(out=nrm[:, 11:12], in_=nrm[:, 12:13], func=AF.Sqrt))
                s.wait_ge(mm.h, mm0 + 146)
                ak.inc(s.activation(out=negMb[:, :], in_=ps_bc[:, 100:101], func=AF.Copy,
                                    scale=-1.0))

            @blk.vector
            def _(v):
                for ng in range(1, 9):
                    v.wait_ge(mm.h, mm0 + 81 + 8 * ng)
                    vk.inc(v.reduce_max(nrm[:, ng - 1:ng], ps_den[:, :], axis=AX.X))
                v.wait_ge(vk.h, vk0 + 8)
                v.reduce_max(nrm[:, 10:11], nrm[:, 0:4], axis=AX.X)
                vk.inc(v.reduce_max(nrm[:, 12:13], nrm[:, 4:8], axis=AX.X))
                v.wait_ge(ak.h, ak0 + 148)
                vk.inc(v.tensor_mul(nrm[:, 8:9], nrm[:, 9:10], nrm[:, 11:12]))
        ak_M = ak.n

        with nc.Block() as blk:
            mm0, ak0, vk0 = mm.n, ak.n, vk.n
            NT, NA, NV = 41, 26, 9

            @blk.gpsimd
            def _(g):
                g.wait_ge(mm.h, mm0)
                for c in range(8):
                    lg.inc(g.dma_start(out=wo[:, c * D:(c + 1) * D],
                                       in_=wout_h[c * 128:(c + 1) * 128, :]), 16)

            @blk.tensor
            def _(t):
                t.wait_ge(ak.h, ak_M)
                for qc in range(4):
                    mmB, akB, vkB = mm0 + NT * qc, ak0 + NA * qc, vk0 + NV * qc
                    q0 = qc * 512
                    for kt in range(16):
                        psd = [ps_a, ps_b][kt % 2]
                        if kt >= 2:
                            t.wait_ge(ak.h, akB + kt - 1)
                        elif qc > 0:
                            t.wait_ge(ak.h, akB - 10)
                        for c in range(8):
                            inst = t.matmul(
                                psd[:, :],
                                kf[:, c * S + kt * 128:c * S + kt * 128 + 128],
                                qf[:, c * S + q0:c * S + q0 + 512],
                                start=(c == 0), stop=(c == 7))
                        mm.inc(inst)
                    for kt in range(16):
                        t.wait_ge(ak.h, akB + kt + 1)
                        mm.inc(t.matmul(ps_den[:, :], p.const["ones_col_b"][:, :],
                                        es[:, kt * 512:(kt + 1) * 512],
                                        start=(kt == 0), stop=(kt == 15)))
                    t.wait_ge(vk.h, vkB + 1)
                    mm.inc(t.matmul(ps_bc[:, :], p.const["ones_row_f"][:, :],
                                    rd_row[:, :], start=True, stop=True))
                    for m_ in range(8):
                        psd = [ps_c, ps_d][m_ % 2]
                        if m_ >= 2:
                            t.wait_ge(vk.h, vkB + 1 + m_ - 1)
                        elif qc > 0:
                            t.wait_ge(vk.h, vkB)
                        for kt in range(16):
                            inst = t.matmul(
                                psd[:, :],
                                vt[:, kt * D + m_ * 128:kt * D + m_ * 128 + 128],
                                es[:, kt * 512:(kt + 1) * 512],
                                start=(kt == 0), stop=(kt == 15))
                        mm.inc(inst)

            @blk.scalar
            def _(s):
                for qc in range(4):
                    mmB, akB, vkB = mm0 + NT * qc, ak0 + NA * qc, vk0 + NV * qc
                    for kt in range(16):
                        psd = [ps_a, ps_b][kt % 2]
                        s.wait_ge(mm.h, mmB + kt + 1)
                        ak.inc(s.activation(out=es[:, kt * 512:(kt + 1) * 512],
                                            in_=psd[:, :], func=AF.Exp,
                                            bias=negMb[:, :]))
                    s.wait_ge(mm.h, mmB + 32)
                    ak.inc(s.activation(out=den_sb[:, :], in_=ps_den[:, :], func=AF.Copy))
                    s.wait_ge(mm.h, mmB + 33)
                    if qc > 0:
                        s.wait_ge(vk.h, vkB)
                    ak.inc(s.activation(out=rdb[:, :], in_=ps_bc[:, :], func=AF.Copy))
                    for m_ in range(8):
                        s.wait_ge(vk.h, vkB + 1 + m_ + 1)
                        ak.inc(s.activation(
                            out=ctxn[:, m_ * S + qc * 512:m_ * S + qc * 512 + 512],
                            in_=t1[:, (m_ % 2) * 512:(m_ % 2) * 512 + 512],
                            func=AF.Copy))

            @blk.vector
            def _(v):
                for qc in range(4):
                    mmB, akB, vkB = mm0 + NT * qc, ak0 + NA * qc, vk0 + NV * qc
                    v.wait_ge(ak.h, akB + 17)
                    vk.inc(v.reciprocal(rd_row[:, :], den_sb[:, :]))
                    for m_ in range(8):
                        psd = [ps_c, ps_d][m_ % 2]
                        v.wait_ge(mm.h, mmB + 33 + m_ + 1)
                        v.wait_ge(ak.h, akB + 18)
                        if m_ >= 2:
                            v.wait_ge(ak.h, akB + 18 + m_ - 1)
                        vk.inc(v.tensor_mul(t1[:, (m_ % 2) * 512:(m_ % 2) * 512 + 512],
                                            psd[:, :], rdb[:, :]))
        ak_p3 = ak.n
        vk_p3 = vk.n

        lg_all = lg.n
        with nc.Block() as blk:
            mm0, ak0, vk0, ld0, st0 = mm.n, ak.n, vk.n, ld.n, st.n

            @blk.sync
            def _(sync):
                sync.wait_ge(ak.h, ak_p3)
                for tt in range(T):
                    ld.inc(sync.dma_start(out=x2[:, tt * D:(tt + 1) * D],
                                          in_=x_h[tt * 128:(tt + 1) * 128, :]), 16)

            @blk.scalar
            def _(s):
                ak.inc(s.activation(out=bvb[:, :], in_=bqk[:, 24:32], func=AF.Copy))
                s.wait_ge(mm.h, mm0 + 1)
                ak.inc(s.activation(out=bvwo[:, 0:512], in_=ps_den[:, :], func=AF.Copy))
                s.wait_ge(mm.h, mm0 + 2)
                ak.inc(s.activation(out=bvwo[:, 512:1024], in_=ps_den[:, :], func=AF.Copy))
                s.wait_ge(mm.h, mm0 + 3)
                ak.inc(s.activation(out=boutb[:, 0:512], in_=ps_c[:, :], func=AF.Copy))
                s.wait_ge(mm.h, mm0 + 4)
                ak.inc(s.activation(out=boutb[:, 512:1024], in_=ps_d[:, :], func=AF.Copy))

            @blk.tensor
            def _(t):
                t.wait_ge(ak.h, ak0 + 1)
                t.wait_ge(lg.h, lg_all)
                t.wait_ge(vk.h, vk_p3)
                for hh in range(2):
                    if hh == 1:
                        t.wait_ge(ak.h, ak0 + 2)
                    for m_ in range(8):
                        inst = t.matmul(ps_den[:, :], bvb[:, m_:m_ + 1],
                                        wo[:, m_ * D + hh * 512:m_ * D + hh * 512 + 512],
                                        start=(m_ == 0), stop=(m_ == 7))
                    mm.inc(inst)
                for hh, psd in ((0, ps_c), (1, ps_d)):
                    t.wait_ge(ak.h, ak0 + 3)
                    t.matmul(psd[:, :], p.const["ones_row_b"][:, :],
                             bout_bf[:, hh * 512:hh * 512 + 512], start=True, stop=False)
                    inst = t.matmul(psd[:, :], p.const["ones_row_b"][:, :],
                                    bvwo[:, hh * 512:hh * 512 + 512], start=False, stop=True)
                    mm.inc(inst)
                for tt1 in range(1, T + 1):
                    if tt1 >= 2:
                        t.wait_ge(vk.h, vk0 + 2 * (tt1 - 1) - 1)
                    for m_ in range(8):
                        for hh in range(2):
                            inst = t.matmul(
                                ps_o[:, hh * 512:(hh + 1) * 512],
                                ctxn[:, m_ * S + (tt1 - 1) * 128:m_ * S + (tt1 - 1) * 128 + 128],
                                wo[:, m_ * D + hh * 512:m_ * D + hh * 512 + 512],
                                start=(m_ == 0), stop=(m_ == 7))
                    mm.inc(inst)

            @blk.vector
            def _(v):
                v.wait_ge(ak.h, ak0 + 5)
                v.wait_ge(ld.h, ld0 + 16 * T)
                for tt in range(T):
                    v.wait_ge(mm.h, mm0 + 4 + tt + 1)
                    if tt >= 2:
                        psem = [st, s2][tt % 2]
                        v.wait_ge(psem.h, 16 * ((tt - 2) // 2 + 1))
                    xoh = xo[:, (tt % 2) * D:(tt % 2) * D + D]
                    vk.inc(v.tensor_add(xoh, ps_o[:, :], boutb[:, :]))
                    v.wait_ge(vk.h, vk0 + 2 * tt + 1)
                    vk.inc(v.tensor_add(xoh, xoh, x2[:, tt * D:(tt + 1) * D]))

            @blk.gpsimd
            def _(g):
                for tt in range(T):
                    g.wait_ge(vk.h, vk0 + 2 * (tt + 1))
                    psem = [st, s2][tt % 2]
                    psem.inc(g.dma_start(out=xout_h[tt * 128:(tt + 1) * 128, :],
                                         in_=xo[:, (tt % 2) * D:(tt % 2) * D + D]), 16)
                g.wait_ge(st.h, st.n)
                g.wait_ge(s2.h, s2.n)
    return p



def build_l1_attn(debug=False):
    p = Prog("l1attn")
    nc = p.nc
    x_h = nc.dram_tensor("x", [S, D], F32, kind="ExternalInput").ap()
    wqkv_h = nc.dram_tensor("WqkvT", [D, 3 * D], F32, kind="ExternalInput").ap()
    bqkv_h = nc.dram_tensor("b_qkv", [3 * D], F32, kind="ExternalInput").ap()
    wout_h = nc.dram_tensor("WoT", [D, D], F32, kind="ExternalInput").ap()
    bout_h = nc.dram_tensor("b_out", [D], F32, kind="ExternalInput").ap()
    dd_h = nc.dram_tensor("dd", [1, 1], F32, kind="ExternalInput").ap()
    xout_h = nc.dram_tensor("xout", [S, D], F32, kind="ExternalOutput").ap()
    dbg_h = {}
    if debug:
        for nm, shape, dt in [("qf", [128, 8 * S], BF16), ("kf", [128, 8 * S], BF16),
                              ("vt", [128, T * D], BF16), ("ctxn", [128, 8 * S], BF16),
                              ("negMb", [128, 16], F32), ("es", [128, 16 * 512], BF16),
                              ("decay", [128, 4096], BF16)]:
            dbg_h[nm] = nc.dram_tensor("dbg_" + nm, shape, dt, kind="ExternalOutput").ap()

    with p.stack:
        p.decl_consts(["identf", "ones_col_b", "ones_row_b", "ones_row_f",
                       "dist2d", "hsel"])
        p.decl_fvals([1e-5])
        p.arena(188 * KB)
        x_sb = p.at("x_sb", 0, [128, T * D], F32)
        kf = p.at("kf_", 0, [128, 8 * S], BF16)
        vt = p.at("vt_", 32 * KB, [128, T * D], BF16)
        zf = p.at("zf_", 64 * KB, [128, 8 * S], BF16)
        ctxn = p.at("ctxn_", 64 * KB, [128, 8 * S], BF16)
        wqkv = p.at("wqkv_", 96 * KB, [128, 8 * 3 * D], BF16)
        es = p.at("es_", 96 * KB, [128, 16 * 512], BF16)
        wo = p.at("wo_", 112 * KB, [128, 8 * D], BF16)
        sd = p.at("sd_", 128 * KB, [128, 2 * 512], F32)
        t1 = p.at("t1_", 132 * KB, [128, 512], F32)
        rdb = p.at("rdb_", 134 * KB, [128, 512], F32)
        sqq = p.at("sqq_", 136 * KB, [128, 2 * 512], BF16)
        qf = p.at("qf_", 144 * KB, [128, 8 * S], BF16)
        xt2 = p.at("xt2_", 144 * KB, [128, 2 * D], F32)
        xo = p.at("xo_", 152 * KB, [128, 2 * D], F32)
        boutb = p.at("boutb_", 160 * KB, [128, D], F32)
        decay2d = p.at("decay2d_", 176 * KB, [128, 4096], BF16)
        zt = p.at("zt_", 184 * KB, [128, 2 * D], BF16)
        x2 = p.at("x2_", 0, [128, T * D], F32)

        stats = p.sb("stats", [128, 8 * T], F32)
        bqk_r = p.sb("bqk_r", [24, 128], F32)
        bqk = p.sb("bqk", [128, 32], F32)
        nrm = p.sb("nrm", [16, 24], F32)
        negMb = p.sb("negMb", [128, 16], F32)
        den_sb = p.sb("den_sb", [1, 512], F32)
        rd_row = p.sb("rd_row", [1, 512], F32)
        bout_bf = p.sb("bout_bf", [1, D], BF16)
        bvb = p.sb("bvb", [128, 8], BF16)
        bvwo = p.sb("bvwo", [1, D], BF16)
        mrow = p.sb("mrow", [1, 16], F32)
        dd_sb = p.sb("dd_sb", [1, 1], F32)
        aa = p.sb("aa", [1, 1], F32)
        nab = p.sb("nab", [128, 1], F32)

        ps_a = p.ps("ps_a", [128, 512])
        ps_b = p.ps("ps_b", [128, 512])
        ps_c = p.ps("ps_c", [128, 512])
        ps_d = p.ps("ps_d", [128, 512])
        ps_o = p.ps("ps_o", [128, 1024])
        ps_den = p.ps("ps_den", [16, 512])
        ps_bc = p.ps("ps_bc", [128, 512])

        ld = p.sem("ld")
        lg = p.sem("lg")
        tx = p.sem("tx")
        mm = p.sem("mm")
        ak = p.sem("ak")
        vk = p.sem("vk")
        st = p.sem("st")
        s2 = p.sem("s2")

        with nc.Block() as blk:
            @blk.sync
            def _(sync):
                p.emit_const_loads(sync, ld)
                for t in range(T):
                    ld.inc(sync.dma_start(out=x_sb[:, t * D:(t + 1) * D],
                                          in_=x_h[t * 128:(t + 1) * 128, :]), 16)
                ld.inc(sync.dma_start(out=bqk_r[:, :],
                                      in_=bqkv_h.rearrange("(a b) -> a b", b=128)), 16)
                ld.inc(sync.dma_start(out=dd_sb[:, :], in_=dd_h[:, :]), 16)

            @blk.gpsimd
            def _(g):
                for c in range(8):
                    lg.inc(g.dma_start(out=wqkv[:, c * 3072:(c + 1) * 3072],
                                       in_=wqkv_h[c * 128:(c + 1) * 128, :]), 16)
                lg.inc(g.dma_start(out=bout_bf[:, :], in_=bout_h[None, :]), 16)
        ld_all = ld.n
        lg_w = lg.n

        with nc.Block() as blk:
            emit_ln(p, blk, x_sb=x_sb, zt=zt, zf=zf, stats=stats,
                    ld=ld, ak=ak, vk=vk, tx=tx, wait_ld=ld_all, eps_ap=p.fc(1e-5))
            a_abs = ak.n + 1
            m_bc = mm.n + 1
            v_nab = vk.n + 1

            @blk.scalar
            def _(s):
                ak.inc(s.activation(out=aa[:, :], in_=dd_sb[:, :], func=AF.Abs))

            @blk.tensor
            def _(t):
                t.wait_ge(ak.h, a_abs)
                mm.inc(t.matmul(ps_bc[:, 0:1], p.const["ones_row_f"][:, :], aa[:, :],
                                start=True, stop=True))

            @blk.vector
            def _(v):
                v.wait_ge(mm.h, m_bc)
                vk.inc(v.tensor_scalar(nab[:, :], ps_bc[:, 0:1], -1.0, None, op0=ALU.mult))

            @blk.scalar
            def _(s):
                s.wait_ge(vk.h, v_nab)
                ak.inc(s.activation(out=decay2d[:, :], in_=p.const["dist2d"][:, :],
                                    func=AF.Exp, scale=nab[:, :]))
            a_dv = ak.n
        tx_zf = tx.n

        with nc.Block() as blk:
            mm0, ak0, vk0 = mm.n, ak.n, vk.n

            @blk.tensor
            def _(t):
                t.wait_ge(ld.h, ld_all)
                t.wait_ge(vk.h, v_nab)
                mm.inc(t.transpose(ps_bc[:, 0:24], bqk_r[:, :], p.const["identf"][0:24, 0:24]))
                t.wait_ge(tx.h, tx_zf)
                t.wait_ge(lg.h, lg_w)
                g = 0
                for proj in range(2):
                    for m_ in range(8):
                        for tc in range(4):
                            g += 1
                            psd = [ps_a, ps_b][g % 2]
                            if g > 2:
                                t.wait_ge(ak.h, ak0 + 2 + g - 2)
                            for c in range(8):
                                inst = t.matmul(
                                    psd[:, :],
                                    wqkv[:, c * 3072 + proj * 1024 + m_ * 128:
                                         c * 3072 + proj * 1024 + m_ * 128 + 128],
                                    zf[:, c * S + tc * 512:c * S + tc * 512 + 512],
                                    start=(c == 0), stop=(c == 7))
                            mm.inc(inst)
                for tt1 in range(1, T + 1):
                    if tt1 >= 2:
                        t.wait_ge(ak.h, ak0 + 66 + tt1 - 1)
                    for c in range(8):
                        for hh in range(2):
                            inst = t.matmul(
                                ps_o[:, hh * 512:(hh + 1) * 512],
                                zf[:, c * S + (tt1 - 1) * 128:c * S + (tt1 - 1) * 128 + 128],
                                wqkv[:, c * 3072 + 2048 + hh * 512:
                                     c * 3072 + 2048 + hh * 512 + 512],
                                start=(c == 0), stop=(c == 7))
                    mm.inc(inst)
                k = 0
                for which in range(2):
                    for nn_ in range(4):
                        ng = which * 4 + nn_ + 1
                        if ng >= 2:
                            t.wait_ge(vk.h, vk0 + ng - 1)
                        for c in range(8):
                            k += 1
                            t.wait_ge(ak.h, ak0 + 82 + k)
                            inst = t.matmul(ps_den[:, :], p.const["hsel"][:, 16 * c:16 * c + 16],
                                            sqq[:, (c % 2) * 512:(c % 2) * 512 + 512],
                                            start=(c == 0), stop=(c == 7))
                            mm.inc(inst)
                t.wait_ge(vk.h, vk0 + 11)
                mm.inc(t.transpose(ps_bc[0:1, 32:48], nrm[:, 12:13],
                                   p.const["identf"][0:16, 0:16]))
                t.wait_ge(ak.h, ak0 + 149)
                mm.inc(t.matmul(ps_bc[:, 64:80], p.const["ones_row_f"][:, :],
                                mrow[:, :], start=True, stop=True))

            @blk.scalar
            def _(s):
                s.wait_ge(mm.h, mm0 + 1)
                ak.inc(s.activation(out=bqk[:, 8:32], in_=ps_bc[:, 0:24], func=AF.Copy))
                ak.inc(s.activation(out=bqk[:, 0:8], in_=ps_bc[:, 0:8], func=AF.Copy,
                                    scale=1.0 / 8.0))
                s.wait_ge(ak.h, ak0 + 2)
                g = 0
                for proj in range(2):
                    dst = qf if proj == 0 else kf
                    for m_ in range(8):
                        for tc in range(4):
                            g += 1
                            psd = [ps_a, ps_b][g % 2]
                            s.wait_ge(mm.h, mm0 + 1 + g)
                            ak.inc(s.activation(
                                out=dst[:, m_ * S + tc * 512:m_ * S + tc * 512 + 512],
                                in_=psd[:, :], func=AF.Identity,
                                scale=(1.0 / 8.0 if proj == 0 else 1.0),
                                bias=bqk[:, (0 if proj == 0 else 16) + m_:
                                         (0 if proj == 0 else 16) + m_ + 1]))
                for tt1 in range(1, T + 1):
                    s.wait_ge(mm.h, mm0 + 65 + tt1)
                    ak.inc(s.activation(out=vt[:, (tt1 - 1) * D:tt1 * D], in_=ps_o[:, :],
                                        func=AF.Copy))
                s.wait_ge(ak.h, ak0 + 82)
                k = 0
                for which in range(2):
                    src = qf if which == 0 else kf
                    for nn_ in range(4):
                        for c in range(8):
                            k += 1
                            if k > 2:
                                s.wait_ge(mm.h, mm0 + 81 + k - 2)
                            ak.inc(s.activation(
                                out=sqq[:, (c % 2) * 512:(c % 2) * 512 + 512],
                                in_=src[:, c * S + nn_ * 512:c * S + nn_ * 512 + 512],
                                func=AF.Square))
                s.wait_ge(vk.h, vk0 + 10)
                ak.inc(s.activation(out=nrm[:, 16:17], in_=nrm[:, 8:9], func=AF.Sqrt))
                ak.inc(s.activation(out=nrm[:, 17:18], in_=nrm[:, 9:10], func=AF.Sqrt))
                s.wait_ge(mm.h, mm0 + 146)
                ak.inc(s.activation(out=mrow[:, :], in_=ps_bc[0:1, 32:48], func=AF.Copy))
                s.wait_ge(mm.h, mm0 + 147)
                ak.inc(s.activation(out=negMb[:, :], in_=ps_bc[:, 64:80], func=AF.Copy,
                                    scale=-1.0))

            @blk.vector
            def _(v):
                for ng in range(1, 9):
                    v.wait_ge(mm.h, mm0 + 81 + 8 * ng)
                    vk.inc(v.reduce_max(nrm[:, ng - 1:ng], ps_den[:, :], axis=AX.X))
                v.wait_ge(vk.h, vk0 + 8)
                vk.inc(v.reduce_max(nrm[:, 8:9], nrm[:, 0:4], axis=AX.X))
                vk.inc(v.reduce_max(nrm[:, 9:10], nrm[:, 4:8], axis=AX.X))
                v.wait_ge(ak.h, ak0 + 148)
                vk.inc(v.tensor_mul(nrm[:, 12:13], nrm[:, 16:17], nrm[:, 17:18]))
        ak_M = ak.n
        mm_p2 = mm.n

        with nc.Block() as blk:
            mm0, ak0, vk0 = mm.n, ak.n, vk.n
            NT, NA, NV = 34, 19, 18

            @blk.gpsimd
            def _(g):
                g.wait_ge(mm.h, mm0)
                for c in range(8):
                    lg.inc(g.dma_start(out=wo[:, c * D:(c + 1) * D],
                                       in_=wout_h[c * 128:(c + 1) * 128, :]), 16)

            @blk.tensor
            def _(t):
                t.wait_ge(ak.h, ak_M)
                for qc in range(4):
                    q0 = qc * 512
                    for h in range(16):
                        it = qc * 16 + h
                        mmB, akB, vkB = mm0 + NT * it, ak0 + NA * it, vk0 + NV * it
                        ch = h // 2
                        r0 = 64 * (h % 2)
                        for kt in range(16):
                            psd = [ps_a, ps_b][kt % 2]
                            if kt >= 2:
                                t.wait_ge(vk.h, vkB + kt - 1)
                            elif it > 0:
                                t.wait_ge(vk.h, vkB - 4 + kt)
                            inst = t.matmul(
                                psd[:, :],
                                kf[r0:r0 + 64, ch * S + kt * 128:ch * S + kt * 128 + 128],
                                qf[r0:r0 + 64, ch * S + q0:ch * S + q0 + 512],
                                start=True, stop=True)
                            mm.inc(inst)
                        for kt in range(16):
                            t.wait_ge(ak.h, akB + kt + 1)
                            mm.inc(t.matmul(ps_den[0:1, :], p.const["ones_col_b"][:, :],
                                            es[:, kt * 512:(kt + 1) * 512],
                                            start=(kt == 0), stop=(kt == 15)))
                        t.wait_ge(vk.h, vkB + 17)
                        mm.inc(t.matmul(ps_bc[:, :], p.const["ones_row_f"][:, :],
                                        rd_row[:, :], start=True, stop=True))
                        psd = [ps_c, ps_d][it % 2]
                        if it >= 2:
                            t.wait_ge(vk.h, vkB - NV)
                        for kt in range(16):
                            inst = t.matmul(
                                psd[0:64, :],
                                vt[:, kt * D + 64 * h:kt * D + 64 * h + 64],
                                es[:, kt * 512:(kt + 1) * 512],
                                start=(kt == 0), stop=(kt == 15))
                        mm.inc(inst)

            @blk.vector
            def _(v):
                v.wait_ge(ak.h, a_dv)
                for qc in range(4):
                    for h in range(16):
                        it = qc * 16 + h
                        mmB, akB, vkB = mm0 + NT * it, ak0 + NA * it, vk0 + NV * it
                        for kt in range(16):
                            psd = [ps_a, ps_b][kt % 2]
                            v.wait_ge(mm.h, mmB + kt + 1)
                            if kt >= 2:
                                v.wait_ge(ak.h, akB + kt - 1)
                            elif it > 0:
                                v.wait_ge(ak.h, akB - 5 + kt)
                            off = 2048 + 128 * (4 * qc - kt)
                            vk.inc(v.tensor_mul(sd[:, (kt % 2) * 512:(kt % 2) * 512 + 512],
                                                psd[:, :],
                                                decay2d[:, off:off + 512]))
                        v.wait_ge(ak.h, akB + 17)
                        vk.inc(v.reciprocal(rd_row[:, :], den_sb[:, :]))
                        v.wait_ge(mm.h, mmB + 34)
                        v.wait_ge(ak.h, akB + 18)
                        if it >= 1:
                            v.wait_ge(ak.h, akB)
                        psd = [ps_c, ps_d][it % 2]
                        vk.inc(v.tensor_mul(t1[0:64, :], psd[0:64, :], rdb[0:64, :]))

            @blk.scalar
            def _(s):
                for qc in range(4):
                    for h in range(16):
                        it = qc * 16 + h
                        mmB, akB, vkB = mm0 + NT * it, ak0 + NA * it, vk0 + NV * it
                        ch = h // 2
                        r0 = 64 * (h % 2)
                        for kt in range(16):
                            s.wait_ge(vk.h, vkB + kt + 1)
                            ak.inc(s.activation(out=es[:, kt * 512:(kt + 1) * 512],
                                                in_=sd[:, (kt % 2) * 512:(kt % 2) * 512 + 512],
                                                func=AF.Exp, bias=negMb[:, h:h + 1]))
                        s.wait_ge(mm.h, mmB + 32)
                        ak.inc(s.activation(out=den_sb[:, :], in_=ps_den[0:1, :], func=AF.Copy))
                        s.wait_ge(mm.h, mmB + 33)
                        if it > 0:
                            s.wait_ge(vk.h, vkB)
                        ak.inc(s.activation(out=rdb[:, :], in_=ps_bc[:, :], func=AF.Copy))
                        s.wait_ge(vk.h, vkB + 18)
                        ak.inc(s.activation(
                            out=ctxn[r0:r0 + 64, ch * S + qc * 512:ch * S + qc * 512 + 512],
                            in_=t1[0:64, :], func=AF.Copy))
        ak_p3 = ak.n
        vk_p3 = vk.n

        lg_all = lg.n
        with nc.Block() as blk:
            mm0, ak0, vk0, ld0, st0 = mm.n, ak.n, vk.n, ld.n, st.n

            @blk.sync
            def _(sync):
                sync.wait_ge(ak.h, ak_p3)
                for tt in range(T):
                    ld.inc(sync.dma_start(out=x2[:, tt * D:(tt + 1) * D],
                                          in_=x_h[tt * 128:(tt + 1) * 128, :]), 16)

            @blk.scalar
            def _(s):
                s.wait_ge(ak.h, ak_p3)
                ak.inc(s.activation(out=bvb[:, :], in_=bqk[:, 24:32], func=AF.Copy))
                s.wait_ge(mm.h, mm0 + 1)
                ak.inc(s.activation(out=bvwo[:, 0:512], in_=ps_den[0:1, :], func=AF.Copy))
                s.wait_ge(mm.h, mm0 + 2)
                ak.inc(s.activation(out=bvwo[:, 512:1024], in_=ps_den[0:1, :], func=AF.Copy))
                s.wait_ge(mm.h, mm0 + 3)
                ak.inc(s.activation(out=boutb[:, 0:512], in_=ps_c[:, :], func=AF.Copy))
                s.wait_ge(mm.h, mm0 + 4)
                ak.inc(s.activation(out=boutb[:, 512:1024], in_=ps_d[:, :], func=AF.Copy))

            @blk.tensor
            def _(t):
                t.wait_ge(ak.h, ak0 + 1)
                t.wait_ge(lg.h, lg_all)
                t.wait_ge(vk.h, vk_p3)
                for hh in range(2):
                    if hh == 1:
                        t.wait_ge(ak.h, ak0 + 2)
                    for m_ in range(8):
                        inst = t.matmul(ps_den[0:1, :], bvb[:, m_:m_ + 1],
                                        wo[:, m_ * D + hh * 512:m_ * D + hh * 512 + 512],
                                        start=(m_ == 0), stop=(m_ == 7))
                    mm.inc(inst)
                for hh, psd in ((0, ps_c), (1, ps_d)):
                    t.wait_ge(ak.h, ak0 + 3)
                    t.matmul(psd[:, :], p.const["ones_row_b"][:, :],
                             bout_bf[:, hh * 512:hh * 512 + 512], start=True, stop=False)
                    inst = t.matmul(psd[:, :], p.const["ones_row_b"][:, :],
                                    bvwo[:, hh * 512:hh * 512 + 512], start=False, stop=True)
                    mm.inc(inst)
                for tt1 in range(1, T + 1):
                    if tt1 >= 2:
                        t.wait_ge(vk.h, vk0 + 2 * (tt1 - 1) - 1)
                    for m_ in range(8):
                        for hh in range(2):
                            inst = t.matmul(
                                ps_o[:, hh * 512:(hh + 1) * 512],
                                ctxn[:, m_ * S + (tt1 - 1) * 128:m_ * S + (tt1 - 1) * 128 + 128],
                                wo[:, m_ * D + hh * 512:m_ * D + hh * 512 + 512],
                                start=(m_ == 0), stop=(m_ == 7))
                    mm.inc(inst)

            @blk.vector
            def _(v):
                v.wait_ge(ak.h, ak0 + 5)
                v.wait_ge(ld.h, ld0 + 16 * T)
                for tt in range(T):
                    v.wait_ge(mm.h, mm0 + 4 + tt + 1)
                    if tt >= 2:
                        psem = [st, s2][tt % 2]
                        v.wait_ge(psem.h, 16 * ((tt - 2) // 2 + 1))
                    xoh = xo[:, (tt % 2) * D:(tt % 2) * D + D]
                    vk.inc(v.tensor_add(xoh, ps_o[:, :], boutb[:, :]))
                    v.wait_ge(vk.h, vk0 + 2 * tt + 1)
                    vk.inc(v.tensor_add(xoh, xoh, x2[:, tt * D:(tt + 1) * D]))

            @blk.gpsimd
            def _(g):
                for tt in range(T):
                    g.wait_ge(vk.h, vk0 + 2 * (tt + 1))
                    psem = [st, s2][tt % 2]
                    psem.inc(g.dma_start(out=xout_h[tt * 128:(tt + 1) * 128, :],
                                         in_=xo[:, (tt % 2) * D:(tt % 2) * D + D]), 16)
                if dbg_h:
                    for nm, buf in [("qf", qf), ("kf", kf), ("vt", vt), ("ctxn", ctxn),
                                    ("negMb", negMb), ("es", es), ("decay", decay2d)]:
                        st.inc(g.dma_start(out=dbg_h[nm][:, :], in_=buf[:, :]), 16)
                g.wait_ge(st.h, st.n)
                g.wait_ge(s2.h, s2.n)
    return p




def build_l1_ffn(debug=False):
    p = Prog("l1ffn")
    nc = p.nc
    x_h = nc.dram_tensor("x", [S, D], F32, kind="ExternalInput").ap()
    g1_h = nc.dram_tensor("g1", [D], F32, kind="ExternalInput").ap()
    g2_h = nc.dram_tensor("g2", [D], F32, kind="ExternalInput").ap()
    wq_h = nc.dram_tensor("Wq", [D, D], F32, kind="ExternalInput").ap()
    wdkv_h = nc.dram_tensor("Wdkv", [D, 512], F32, kind="ExternalInput").ap()
    wuk_h = nc.dram_tensor("Wuk", [512, D], F32, kind="ExternalInput").ap()
    wuv_h = nc.dram_tensor("Wuv", [512, D], F32, kind="ExternalInput").ap()
    wo_h = nc.dram_tensor("Wo", [D, D], F32, kind="ExternalInput").ap()
    gw_h = nc.dram_tensor("gate_w", [D, 8], F32, kind="ExternalInput").ap()
    w1_h = nc.dram_tensor("w1", [8, D, 2048], F32, kind="ExternalInput").ap()
    w2_h = nc.dram_tensor("w2", [8, 2048, D], F32, kind="ExternalInput").ap()
    xout_h = nc.dram_tensor("xout", [S, D], F32, kind="ExternalOutput").ap()
    dbg_h = {}
    if debug:
        for nm, shape, dt in [("u_fm", [128, 8 * S], BF16),
                              ("ctxn", [128, 8 * S], BF16), ("negMb", [128, 16], F32),
                              ("t", [128, T * D], F32), ("u2", [128, 8 * S], BF16),
                              ("lgs", [128, 128], F32), ("cmb", [128, 128], F32),
                              ("kf", [128, 8 * S], BF16), ("vt", [128, T * D], BF16)]:
            dbg_h[nm] = nc.dram_tensor("dbg_" + nm, shape, dt, kind="ExternalOutput").ap()

    with p.stack:
        p.decl_consts(["identf", "ones_col_b", "ones_row_b", "ones_row_f",
                       "hsel", "trimask"])
        p.decl_fvals([1e-5, 1e-6])
        p.arena(190 * KB)
        u_fm = p.at("u_fm_", 0, [128, 8 * S], BF16)
        t_sb = p.at("t_sb_", 0, [128, T * D], F32)
        qf = p.at("qf_", 32 * KB, [128, 8 * S], BF16)
        kf = p.at("kf_", 64 * KB, [128, 8 * S], BF16)
        wob = p.at("wob_", 64 * KB, [128, 8 * D], BF16)
        w1b = p.at("w1b_", 64 * KB, [128, 8 * 2048], BF16)
        x3 = p.at("x3_", 64 * KB, [128, T * D], F32)
        vt = p.at("vt_", 96 * KB, [128, T * D], BF16)
        x1 = p.at("x1_", 96 * KB, [128, T * D], F32)
        w2b = p.at("w2b_", 96 * KB, [128, 16 * D], BF16)
        x2 = p.at("x2_", 80 * KB, [128, T * D], F32)
        cf = p.at("cf_", 128 * KB, [128, 4 * S], BF16)
        es = p.at("es_", 128 * KB, [128, 16 * 512], BF16)
        u2_fm = p.at("u2_fm_", 128 * KB, [128, 8 * S], BF16)
        ctxn = p.at("ctxn_", 144 * KB, [128, 8 * S], BF16)
        wbuf = p.at("wbuf_", 144 * KB, [128, 8 * D], BF16)
        hh = p.at("hh_", 160 * KB, [128, 16 * 512], BF16)
        zt = p.at("zt_", 176 * KB, [128, 2 * D], BF16)
        ut = p.at("ut_", 180 * KB, [128, 2 * D], BF16)
        zre = p.at("zre_", 176 * KB, [128, 2 * D], F32)
        xot = p.at("xot_", 176 * KB, [128, 2 * D], F32)
        sqq = p.at("sqq_", 184 * KB, [128, 2 * 512], BF16)
        t1 = p.at("t1_", 186 * KB, [128, 512], F32)
        rdb = p.at("rdb_", 188 * KB, [128, 512], F32)
        g1r = p.at("g1r_", 80 * KB, [1, D], F32)
        g1b = p.at("g1b_", 84 * KB, [128, D], F32)
        g2r = p.at("g2r_", 88 * KB, [1, D], F32)
        g2b = p.at("g2b_", 92 * KB, [128, D], F32)

        stats = p.sb("stats", [128, 12 * T], F32)
        nrm = p.sb("nrm", [16, 24], F32)
        negMb = p.sb("negMb", [128, 16], F32)
        den_sb = p.sb("den_sb", [1, 512], F32)
        rd_row = p.sb("rd_row", [1, 512], F32)
        mrow = p.sb("mrow", [1, 16], F32)
        m8 = p.sb("m8", [128, 16], F32)
        lgs = p.sb("lgs", [128, 128], F32)
        cmb = p.sb("cmb", [128, 128], F32)
        gwb = p.sb("gwb", [128, 64], BF16)

        ps_a = p.ps("ps_a", [128, 512])
        ps_b = p.ps("ps_b", [128, 512])
        ps_c = p.ps("ps_c", [128, 512])
        ps_d = p.ps("ps_d", [128, 512])
        ps_o = p.ps("ps_o", [128, 1024])
        ps_den = p.ps("ps_den", [16, 512])
        ps_bc = p.ps("ps_bc", [128, 512])

        ld = p.sem("ld")
        lg = p.sem("lg")
        tx = p.sem("tx")
        mm = p.sem("mm")
        ak = p.sem("ak")
        vk = p.sem("vk")
        st = p.sem("st")
        s2 = p.sem("s2")
        lw = p.sem("lw")

        with nc.Block() as blk:
            mm0, ak0, vk0 = mm.n, ak.n, vk.n

            @blk.sync
            def _(sync):
                p.emit_const_loads(sync, ld)
                ld.inc(sync.dma_start(out=g1r[:, :], in_=g1_h[None, :]), 16)
                for t in range(T):
                    ld.inc(sync.dma_start(out=x1[:, t * D:(t + 1) * D],
                                          in_=x_h[t * 128:(t + 1) * 128, :]), 16)
            ld_all = ld.n

            @blk.sync
            def _(sync):
                for t in range(T):
                    sync.wait_ge(vk.h, vk0 + 7 + t + 1)
                    if t >= 1:
                        sync.wait_ge(tx.h, 128 * t)
                    for c in range(8):
                        tx.inc(sync.dma_start_transpose(
                            u_fm[:, c * S + t * 128:c * S + t * 128 + 128],
                            ut[:, (t % 2) * D + c * 128:(t % 2) * D + c * 128 + 128]), 16)

            @blk.scalar
            def _(s):
                s.wait_ge(ld.h, ld_all)
                for t in range(T):
                    s.activation(out=zt[:, (t % 2) * D:(t % 2) * D + D],
                                 in_=x1[:, t * D:(t + 1) * D], func=AF.Square,
                                 accum_out=stats[:, T + t:T + t + 1])
                    inst = s.activation(out=zt[:, (t % 2) * D:(t % 2) * D + D],
                                        in_=x1[:, t * D:(t + 1) * D], func=AF.Copy,
                                        accum_out=stats[:, t:t + 1])
                ak.inc(inst)
                s.wait_ge(vk.h, vk0 + 3)
                ak.inc(s.activation(out=stats[:, 10 * T:11 * T], in_=stats[:, 7 * T:8 * T],
                                    func=AF.Sqrt, bias=p.fc(1e-5)))
                s.wait_ge(vk.h, vk0 + 6)
                ak.inc(s.activation(out=stats[:, 11 * T:12 * T], in_=stats[:, 11 * T:12 * T],
                                    func=AF.Sqrt, bias=p.fc(1e-6)))
                s.wait_ge(mm.h, mm0 + 1)
                ak.inc(s.activation(out=g1b[:, 0:512], in_=ps_bc[:, 0:512],
                                    func=AF.Copy))
                s.wait_ge(mm.h, mm0 + 2)
                ak.inc(s.activation(out=g1b[:, 512:1024], in_=ps_bc[:, 0:512],
                                    func=AF.Copy))
                s.wait_ge(vk.h, vk0 + 7)
                for t in range(T):
                    if t >= 2:
                        s.wait_ge(vk.h, vk0 + 7 + t - 1)
                    ak.inc(s.activation(out=zt[:, (t % 2) * D:(t % 2) * D + D],
                                        in_=x1[:, t * D:(t + 1) * D], func=AF.Identity,
                                        scale=stats[:, 3 * T + t:3 * T + t + 1],
                                        bias=stats[:, 4 * T + t:4 * T + t + 1]))

            @blk.vector
            def _(v):
                v.wait_ge(ak.h, ak0 + 1)
                vk.inc(v.tensor_scalar_mul(stats[:, 2 * T:3 * T], stats[:, 0:T],
                                           1.0 / D))
                v.wait_ge(vk.h, vk0 + 1)
                m = stats[:, 2 * T:3 * T]
                v.tensor_mul(stats[:, 10 * T:11 * T], m, m)
                vk.inc(v.tensor_scalar(stats[:, 11 * T:12 * T], stats[:, T:2 * T],
                                       1.0 / D, None, op0=ALU.mult))
                v.wait_ge(vk.h, vk0 + 2)
                vk.inc(v.tensor_sub(stats[:, 7 * T:8 * T], stats[:, 11 * T:12 * T],
                                    stats[:, 10 * T:11 * T]))
                v.wait_ge(ak.h, ak0 + 2)
                vk.inc(v.reciprocal(stats[:, 3 * T:4 * T], stats[:, 10 * T:11 * T]))
                v.wait_ge(vk.h, vk0 + 4)
                v.scalar_tensor_tensor(stats[:, 4 * T:5 * T], stats[:, 2 * T:3 * T],
                                       -1.0, stats[:, 3 * T:4 * T],
                                       op0=ALU.mult, op1=ALU.mult)
                vk.inc(v.tensor_mul(stats[:, 10 * T:11 * T], stats[:, 3 * T:4 * T],
                                    stats[:, 3 * T:4 * T]))
                v.wait_ge(vk.h, vk0 + 5)
                vk.inc(v.tensor_mul(stats[:, 11 * T:12 * T], stats[:, 7 * T:8 * T],
                                    stats[:, 10 * T:11 * T]))
                v.wait_ge(ak.h, ak0 + 3)
                vk.inc(v.reciprocal(stats[:, 6 * T:7 * T], stats[:, 11 * T:12 * T]))
                v.wait_ge(ak.h, ak0 + 5)
                for t in range(T):
                    v.wait_ge(ak.h, ak0 + 5 + t + 1)
                    if t >= 2:
                        v.wait_ge(tx.h, 128 * (t - 1))
                    vk.inc(v.scalar_tensor_tensor(
                        ut[:, (t % 2) * D:(t % 2) * D + D],
                        zt[:, (t % 2) * D:(t % 2) * D + D],
                        stats[:, 6 * T + t:6 * T + t + 1],
                        g1b[:, :], op0=ALU.mult, op1=ALU.mult))

            @blk.tensor
            def _(t):
                t.wait_ge(ld.h, ld_all)
                mm.inc(t.matmul(ps_bc[:, 0:512], p.const["ones_row_f"][:, :],
                                g1r[:, 0:512], start=True, stop=True))
                t.wait_ge(ak.h, ak0 + 4)
                mm.inc(t.matmul(ps_bc[:, 0:512], p.const["ones_row_f"][:, :],
                                g1r[:, 512:1024], start=True, stop=True))
        tx_u = tx.n
        ak_pa, vk_pa, mm_pa = ak.n, vk.n, mm.n

        with nc.Block() as blk:
            mm0, ak0, vk0, lg0 = mm.n, ak.n, vk.n, lg.n

            @blk.gpsimd
            def _(g):
                for c in range(8):
                    lg.inc(g.dma_start(out=wbuf[:, c * D:(c + 1) * D],
                                       in_=wq_h[c * 128:(c + 1) * 128, :]), 16)
                g.wait_ge(mm.h, mm0 + 32)
                for c in range(8):
                    lg.inc(g.dma_start(out=wbuf[:, c * 512:(c + 1) * 512],
                                       in_=wdkv_h[c * 128:(c + 1) * 128, :]), 16)
                g.wait_ge(mm.h, mm0 + 48)
                for c in range(4):
                    lg.inc(g.dma_start(out=wbuf[:, c * D:(c + 1) * D],
                                       in_=wuk_h[c * 128:(c + 1) * 128, :]), 16)
                for c in range(4):
                    lg.inc(g.dma_start(out=wbuf[:, 4 * D + c * D:4 * D + (c + 1) * D],
                                       in_=wuv_h[c * 128:(c + 1) * 128, :]), 16)

            @blk.tensor
            def _(t):
                t.wait_ge(lg.h, lg0 + 128)
                g = 0
                for m_ in range(8):
                    for tc in range(4):
                        g += 1
                        psd = [ps_a, ps_b][g % 2]
                        if g > 2:
                            t.wait_ge(ak.h, ak0 + g - 2)
                        if m_ == 0:
                            t.wait_ge(tx.h, 128 * (4 * tc + 4))
                        for c in range(8):
                            inst = t.matmul(
                                psd[:, :], wbuf[:, c * D + m_ * 128:c * D + m_ * 128 + 128],
                                u_fm[:, c * S + tc * 512:c * S + tc * 512 + 512],
                                start=(c == 0), stop=(c == 7))
                        mm.inc(inst)
                t.wait_ge(lg.h, lg0 + 256)
                for m_ in range(4):
                    for tc in range(4):
                        g += 1
                        psd = [ps_a, ps_b][g % 2]
                        t.wait_ge(ak.h, ak0 + g - 2)
                        for c in range(8):
                            inst = t.matmul(
                                psd[:, :],
                                wbuf[:, c * 512 + m_ * 128:c * 512 + m_ * 128 + 128],
                                u_fm[:, c * S + tc * 512:c * S + tc * 512 + 512],
                                start=(c == 0), stop=(c == 7))
                        mm.inc(inst)
                t.wait_ge(lg.h, lg0 + 384)
                for m_ in range(8):
                    for tc in range(4):
                        g += 1
                        psd = [ps_a, ps_b][g % 2]
                        t.wait_ge(ak.h, ak0 + g - 2)
                        for c in range(4):
                            inst = t.matmul(
                                psd[:, :],
                                wbuf[:, c * D + m_ * 128:c * D + m_ * 128 + 128],
                                cf[:, c * S + tc * 512:c * S + tc * 512 + 512],
                                start=(c == 0), stop=(c == 3))
                        mm.inc(inst)
                for tt1 in range(1, T + 1):
                    if tt1 >= 2:
                        t.wait_ge(ak.h, ak0 + 80 + tt1 - 1)
                    for c in range(4):
                        for hhx in range(2):
                            inst = t.matmul(
                                ps_o[:, hhx * 512:(hhx + 1) * 512],
                                cf[:, c * S + (tt1 - 1) * 128:c * S + (tt1 - 1) * 128 + 128],
                                wbuf[:, 4 * D + c * D + hhx * 512:
                                     4 * D + c * D + hhx * 512 + 512],
                                start=(c == 0), stop=(c == 3))
                    mm.inc(inst)
                k = 0
                for which in range(2):
                    for nn_ in range(4):
                        ng = which * 4 + nn_ + 1
                        if ng >= 2:
                            t.wait_ge(vk.h, vk0 + ng - 1)
                        for c in range(8):
                            k += 1
                            t.wait_ge(ak.h, ak0 + 96 + k)
                            inst = t.matmul(ps_den[:, :],
                                            p.const["hsel"][:, 16 * c:16 * c + 16],
                                            sqq[:, (c % 2) * 512:(c % 2) * 512 + 512],
                                            start=(c == 0), stop=(c == 7))
                            mm.inc(inst)
                t.wait_ge(vk.h, vk0 + 11)
                mm.inc(t.transpose(ps_bc[0:1, 32:48], nrm[:, 12:13],
                                   p.const["identf"][0:16, 0:16]))
                t.wait_ge(ak.h, ak0 + 163)
                mm.inc(t.matmul(ps_bc[:, 64:80], p.const["ones_row_f"][:, :],
                                mrow[:, :], start=True, stop=True))

            @blk.scalar
            def _(s):
                g = 0
                for m_ in range(8):
                    for tc in range(4):
                        g += 1
                        psd = [ps_a, ps_b][g % 2]
                        s.wait_ge(mm.h, mm0 + g)
                        ak.inc(s.activation(
                            out=qf[:, m_ * S + tc * 512:m_ * S + tc * 512 + 512],
                            in_=psd[:, :], func=AF.Copy, scale=1.0 / 8.0))
                for m_ in range(4):
                    for tc in range(4):
                        g += 1
                        psd = [ps_a, ps_b][g % 2]
                        s.wait_ge(mm.h, mm0 + g)
                        ak.inc(s.activation(
                            out=cf[:, m_ * S + tc * 512:m_ * S + tc * 512 + 512],
                            in_=psd[:, :], func=AF.Copy))
                for m_ in range(8):
                    for tc in range(4):
                        g += 1
                        psd = [ps_a, ps_b][g % 2]
                        s.wait_ge(mm.h, mm0 + g)
                        ak.inc(s.activation(
                            out=kf[:, m_ * S + tc * 512:m_ * S + tc * 512 + 512],
                            in_=psd[:, :], func=AF.Copy))
                for tt1 in range(1, T + 1):
                    s.wait_ge(mm.h, mm0 + 80 + tt1)
                    ak.inc(s.activation(out=vt[:, (tt1 - 1) * D:tt1 * D], in_=ps_o[:, :],
                                        func=AF.Copy))
                s.wait_ge(ak.h, ak0 + 96)
                k = 0
                for which in range(2):
                    src = qf if which == 0 else kf
                    for nn_ in range(4):
                        for c in range(8):
                            k += 1
                            if k > 2:
                                s.wait_ge(mm.h, mm0 + 96 + k - 2)
                            ak.inc(s.activation(
                                out=sqq[:, (c % 2) * 512:(c % 2) * 512 + 512],
                                in_=src[:, c * S + nn_ * 512:c * S + nn_ * 512 + 512],
                                func=AF.Square))
                s.wait_ge(vk.h, vk0 + 10)
                ak.inc(s.activation(out=nrm[:, 16:17], in_=nrm[:, 8:9], func=AF.Sqrt))
                ak.inc(s.activation(out=nrm[:, 17:18], in_=nrm[:, 9:10], func=AF.Sqrt))
                s.wait_ge(mm.h, mm0 + 161)
                ak.inc(s.activation(out=mrow[:, :], in_=ps_bc[0:1, 32:48], func=AF.Copy))
                s.wait_ge(mm.h, mm0 + 162)
                ak.inc(s.activation(out=negMb[:, :], in_=ps_bc[:, 64:80], func=AF.Copy,
                                    scale=-1.0))

            @blk.vector
            def _(v):
                for ng in range(1, 9):
                    v.wait_ge(mm.h, mm0 + 96 + 8 * ng)
                    vk.inc(v.reduce_max(nrm[:, ng - 1:ng], ps_den[:, :], axis=AX.X))
                v.wait_ge(vk.h, vk0 + 8)
                vk.inc(v.reduce_max(nrm[:, 8:9], nrm[:, 0:4], axis=AX.X))
                vk.inc(v.reduce_max(nrm[:, 9:10], nrm[:, 4:8], axis=AX.X))
                v.wait_ge(ak.h, ak0 + 162)
                vk.inc(v.tensor_mul(nrm[:, 12:13], nrm[:, 16:17], nrm[:, 17:18]))
        ak_pc, vk_pc, mm_pc = ak.n, vk.n, mm.n

        ktns = [4 * qc + 4 for qc in range(4)]
        mmBs, akBs = [], []
        cm_, ca_ = mm.n, ak.n
        for qc in range(4):
            for h in range(16):
                mmBs.append(cm_)
                akBs.append(ca_)
                cm_ += 2 * ktns[qc] + 2
                ca_ += ktns[qc] + 3
        vk0_pd = vk.n
        with nc.Block() as blk:
            @blk.tensor
            def _(t):
                t.wait_ge(ak.h, ak_pc)
                for qc in range(4):
                    ktn = ktns[qc]
                    q0 = qc * 512
                    for h in range(16):
                        it = qc * 16 + h
                        mmB, akB = mmBs[it], akBs[it]
                        vkB = vk0_pd + 6 * it
                        ch = h // 2
                        r0 = 64 * (h % 2)
                        for kt in range(ktn):
                            psd = [ps_a, ps_b][kt % 2]
                            if kt >= 2:
                                t.wait_ge(ak.h, akB + kt - 1)
                            elif it > 0:
                                t.wait_ge(ak.h, akB - 3)
                            inst = t.matmul(
                                psd[:, :],
                                kf[r0:r0 + 64, ch * S + kt * 128:ch * S + kt * 128 + 128],
                                qf[r0:r0 + 64, ch * S + q0:ch * S + q0 + 512],
                                start=True, stop=True)
                            mm.inc(inst)
                        for kt in range(ktn):
                            if kt < 4 * qc:
                                t.wait_ge(ak.h, akB + kt + 1)
                            else:
                                t.wait_ge(vk.h, vkB + (kt - 4 * qc) + 1)
                            mm.inc(t.matmul(ps_den[0:1, :], p.const["ones_col_b"][:, :],
                                            es[:, kt * 512:(kt + 1) * 512],
                                            start=(kt == 0), stop=(kt == ktn - 1)))
                        t.wait_ge(vk.h, vkB + 5)
                        mm.inc(t.matmul(ps_bc[:, :], p.const["ones_row_f"][:, :],
                                        rd_row[:, :], start=True, stop=True))
                        psd = [ps_c, ps_d][it % 2]
                        if it >= 2:
                            t.wait_ge(vk.h, vkB - 6)
                        for kt in range(ktn):
                            inst = t.matmul(
                                psd[0:64, :],
                                vt[:, kt * D + 64 * h:kt * D + 64 * h + 64],
                                es[:, kt * 512:(kt + 1) * 512],
                                start=(kt == 0), stop=(kt == ktn - 1))
                        mm.inc(inst)

            @blk.vector
            def _(v):
                for qc in range(4):
                    ktn = ktns[qc]
                    for h in range(16):
                        it = qc * 16 + h
                        mmB, akB = mmBs[it], akBs[it]
                        vkB = vk0_pd + 6 * it
                        for j in range(4):
                            kt = 4 * qc + j
                            v.wait_ge(ak.h, akB + kt + 1)
                            vk.inc(v.tensor_mul(es[:, kt * 512:(kt + 1) * 512],
                                                es[:, kt * 512:(kt + 1) * 512],
                                                p.const["trimask"][:, j * 512:(j + 1) * 512]))
                        v.wait_ge(ak.h, akB + ktn + 1)
                        vk.inc(v.reciprocal(rd_row[:, :], den_sb[:, :]))
                        v.wait_ge(mm.h, mmB + 2 * ktn + 2)
                        v.wait_ge(ak.h, akB + ktn + 2)
                        if it >= 1:
                            v.wait_ge(ak.h, akB)
                        psd = [ps_c, ps_d][it % 2]
                        vk.inc(v.tensor_mul(t1[0:64, :], psd[0:64, :], rdb[0:64, :]))

            @blk.scalar
            def _(s):
                for qc in range(4):
                    ktn = ktns[qc]
                    for h in range(16):
                        it = qc * 16 + h
                        mmB, akB = mmBs[it], akBs[it]
                        vkB = vk0_pd + 6 * it
                        ch = h // 2
                        r0 = 64 * (h % 2)
                        for kt in range(ktn):
                            psd = [ps_a, ps_b][kt % 2]
                            s.wait_ge(mm.h, mmB + kt + 1)
                            ak.inc(s.activation(out=es[:, kt * 512:(kt + 1) * 512],
                                                in_=psd[:, :], func=AF.Exp,
                                                bias=negMb[:, h:h + 1]))
                        s.wait_ge(mm.h, mmB + 2 * ktn)
                        ak.inc(s.activation(out=den_sb[:, :], in_=ps_den[0:1, :],
                                            func=AF.Copy))
                        s.wait_ge(mm.h, mmB + 2 * ktn + 1)
                        if it > 0:
                            s.wait_ge(vk.h, vkB)
                        ak.inc(s.activation(out=rdb[:, :], in_=ps_bc[:, :], func=AF.Copy))
                        s.wait_ge(vk.h, vkB + 6)
                        ak.inc(s.activation(
                            out=ctxn[r0:r0 + 64, ch * S + qc * 512:ch * S + qc * 512 + 512],
                            in_=t1[0:64, :], func=AF.Copy))
        ak_pd, vk_pd, mm_pd = ak.n, vk.n, mm.n

        with nc.Block() as blk:
            mm0, ak0, vk0 = mm.n, ak.n, vk.n

            @blk.gpsimd
            def _(g):
                g.wait_ge(mm.h, mm_pd)
                for c in range(8):
                    lg.inc(g.dma_start(out=wob[:, c * D:(c + 1) * D],
                                       in_=wo_h[c * 128:(c + 1) * 128, :]), 16)
            lg_wo = lg.n

            @blk.sync
            def _(sync):
                sync.wait_ge(mm.h, mm_pd)
                for t in range(T):
                    ld.inc(sync.dma_start(out=x2[:, t * D:(t + 1) * D],
                                          in_=x_h[t * 128:(t + 1) * 128, :]), 16)
            ld_x2 = ld.n

            @blk.tensor
            def _(t):
                t.wait_ge(ak.h, ak_pd)
                t.wait_ge(lg.h, lg_wo)
                for tt1 in range(1, T + 1):
                    if tt1 >= 2:
                        t.wait_ge(vk.h, vk0 + tt1 - 1)
                    for m_ in range(8):
                        for hhx in range(2):
                            inst = t.matmul(
                                ps_o[:, hhx * 512:(hhx + 1) * 512],
                                ctxn[:, m_ * S + (tt1 - 1) * 128:
                                     m_ * S + (tt1 - 1) * 128 + 128],
                                wob[:, m_ * D + hhx * 512:m_ * D + hhx * 512 + 512],
                                start=(m_ == 0), stop=(m_ == 7))
                    mm.inc(inst)

            @blk.scalar
            def _(s):
                s.wait_ge(ld.h, ld_x2)
                s.wait_ge(ak.h, ak_pd)
                for tt in range(T):
                    if tt >= 2:
                        s.wait_ge(vk.h, vk0 + tt - 1)
                    ak.inc(s.activation(out=zre[:, (tt % 2) * D:(tt % 2) * D + D],
                                        in_=x2[:, tt * D:(tt + 1) * D], func=AF.Identity,
                                        scale=stats[:, 3 * T + tt:3 * T + tt + 1],
                                        bias=stats[:, 4 * T + tt:4 * T + tt + 1]))

            @blk.vector
            def _(v):
                for tt in range(T):
                    v.wait_ge(mm.h, mm0 + tt + 1)
                    v.wait_ge(ak.h, ak0 + tt + 1)
                    vk.inc(v.tensor_add(t_sb[:, tt * D:(tt + 1) * D],
                                        zre[:, (tt % 2) * D:(tt % 2) * D + D],
                                        ps_o[:, :]))
        ak_pe, vk_pe, mm_pe = ak.n, vk.n, mm.n

        with nc.Block() as blk:
            mm0, ak0, vk0, tx0 = mm.n, ak.n, vk.n, tx.n

            @blk.gpsimd
            def _(g):
                g.wait_ge(ak.h, ak_pe)
                lg.inc(g.dma_start(out=g2r[:, :], in_=g2_h[None, :]), 16)
                lg.inc(g.dma_start(out=gwb.rearrange("p (c e) -> p c e", e=8),
                                   in_=gw_h.rearrange("(c p) e -> p c e", p=128)), 16)
            lg_gw = lg.n

            @blk.scalar
            def _(s):
                s.wait_ge(vk.h, vk_pe)
                for t in range(T):
                    inst = s.activation(out=zre[:, (t % 2) * D:(t % 2) * D + D],
                                        in_=t_sb[:, t * D:(t + 1) * D], func=AF.Square,
                                        accum_out=stats[:, 8 * T + t:8 * T + t + 1])
                ak.inc(inst)
                s.wait_ge(vk.h, vk0 + 1)
                ak.inc(s.activation(out=stats[:, 10 * T:11 * T],
                                    in_=stats[:, 10 * T:11 * T],
                                    func=AF.Sqrt, bias=p.fc(1e-6)))
                s.wait_ge(mm.h, mm0 + 1)
                ak.inc(s.activation(out=g2b[:, 0:512], in_=ps_bc[:, 0:512],
                                    func=AF.Copy))
                s.wait_ge(mm.h, mm0 + 2)
                ak.inc(s.activation(out=g2b[:, 512:1024], in_=ps_bc[:, 0:512],
                                    func=AF.Copy))
                for tt in range(T):
                    s.wait_ge(mm.h, mm0 + 2 + tt + 1)
                    ak.inc(s.activation(out=lgs[:, tt * 8:(tt + 1) * 8],
                                        in_=ps_c[:, 0:8], func=AF.Copy))
                for tt in range(T):
                    s.wait_ge(vk.h, vk0 + 18 + 6 * tt + 2)
                    ak.inc(s.activation(out=m8[:, 8:9], in_=m8[:, 9:10],
                                        func=AF.Sigmoid))

            @blk.vector
            def _(v):
                v.wait_ge(ak.h, ak0 + 1)
                vk.inc(v.tensor_scalar_mul(stats[:, 10 * T:11 * T],
                                           stats[:, 8 * T:9 * T], 1.0 / D))
                v.wait_ge(ak.h, ak0 + 2)
                vk.inc(v.reciprocal(stats[:, 9 * T:10 * T], stats[:, 10 * T:11 * T]))
                v.wait_ge(ak.h, ak0 + 4)
                for t in range(T):
                    v.wait_ge(vk.h, vk0 + 2 + t)
                    if t >= 2:
                        v.wait_ge(tx.h, tx0 + 128 * (t - 1))
                    vk.inc(v.scalar_tensor_tensor(
                        ut[:, (t % 2) * D:(t % 2) * D + D],
                        t_sb[:, t * D:(t + 1) * D],
                        stats[:, 9 * T + t:9 * T + t + 1],
                        g2b[:, :], op0=ALU.mult, op1=ALU.mult))
                for tt in range(T):
                    s_ = vk0 + 18 + 6 * tt
                    v.wait_ge(ak.h, ak0 + 4 + tt + 1)
                    vk.inc(v.max(out=m8[:, 0:8], in_=lgs[:, tt * 8:(tt + 1) * 8]))
                    v.wait_ge(vk.h, s_ + 1)
                    vk.inc(v.tensor_sub(m8[:, 9:10], m8[:, 0:1], m8[:, 1:2]))
                    v.wait_ge(ak.h, ak0 + 20 + tt + 1)
                    vk.inc(v.tensor_scalar(m8[:, 10:11], m8[:, 8:9], -1.0, 1.0,
                                           op0=ALU.mult, op1=ALU.add))
                    v.wait_ge(vk.h, s_ + 3)
                    v.tensor_scalar(cmb[:, tt * 8:(tt + 1) * 8],
                                    lgs[:, tt * 8:(tt + 1) * 8], m8[:, 0:1], None,
                                    op0=ALU.is_equal)
                    vk.inc(v.tensor_scalar(t1[:, 0:8], lgs[:, tt * 8:(tt + 1) * 8],
                                           m8[:, 1:2], None, op0=ALU.is_equal))
                    v.wait_ge(vk.h, s_ + 4)
                    vk.inc(v.tensor_scalar(cmb[:, tt * 8:(tt + 1) * 8],
                                           cmb[:, tt * 8:(tt + 1) * 8], m8[:, 8:9],
                                           None, op0=ALU.mult))
                    v.wait_ge(vk.h, s_ + 5)
                    vk.inc(v.scalar_tensor_tensor(cmb[:, tt * 8:(tt + 1) * 8],
                                                  t1[:, 0:8], m8[:, 10:11],
                                                  cmb[:, tt * 8:(tt + 1) * 8],
                                                  op0=ALU.mult, op1=ALU.add))

            @blk.tensor
            def _(t):
                t.wait_ge(vk.h, vk_pe)
                t.wait_ge(lg.h, lg_gw)
                mm.inc(t.matmul(ps_bc[:, 0:512], p.const["ones_row_f"][:, :],
                                g2r[:, 0:512], start=True, stop=True))
                t.wait_ge(ak.h, ak0 + 3)
                mm.inc(t.matmul(ps_bc[:, 0:512], p.const["ones_row_f"][:, :],
                                g2r[:, 512:1024], start=True, stop=True))
                t.wait_ge(lg.h, lg_gw)
                for tt in range(T):
                    t.wait_ge(tx.h, tx0 + 128 * (tt + 1))
                    t.wait_ge(ak.h, ak0 + 4 + tt)
                    for c in range(8):
                        inst = t.matmul(ps_c[:, 0:8],
                                        u2_fm[:, c * S + tt * 128:c * S + tt * 128 + 128],
                                        gwb[:, c * 8:(c + 1) * 8],
                                        start=(c == 0), stop=(c == 7))
                    mm.inc(inst)

            @blk.sync
            def _(sync):
                for t in range(T):
                    sync.wait_ge(vk.h, vk0 + 2 + t + 1)
                    if t >= 1:
                        sync.wait_ge(tx.h, tx0 + 128 * t)
                    for c in range(8):
                        tx.inc(sync.dma_start_transpose(
                            u2_fm[:, c * S + t * 128:c * S + t * 128 + 128],
                            ut[:, (t % 2) * D + c * 128:(t % 2) * D + c * 128 + 128]), 16)
        ak_pf, vk_pf, mm_pf = ak.n, vk.n, mm.n
        tx_pf0 = tx.n - 2048

        with nc.Block() as blk:
            mm0, ak0, vk0, lg0, lw0 = mm.n, ak.n, vk.n, lg.n, lw.n

            @blk.gpsimd
            def _(g):
                for e in range(8):
                    if e > 0:
                        g.wait_ge(mm.h, mm0 + 80 * (e - 1) + 76)
                    for c in range(8):
                        lg.inc(g.dma_start(out=w1b[:, c * 2048:(c + 1) * 2048],
                                           in_=w1_h[e, c * 128:(c + 1) * 128, :]), 16)
                    if e > 0:
                        g.wait_ge(mm.h, mm0 + 80 * e)
                    for c in range(16):
                        lw.inc(g.dma_start(out=w2b[:, c * D:(c + 1) * D],
                                           in_=w2_h[e, c * 128:(c + 1) * 128, :]), 16)

            @blk.tensor
            def _(t):
                t.wait_ge(vk.h, vk_pf)
                for e in range(8):
                    t.wait_ge(lg.h, lg0 + 128 * (e + 1))
                    gidx = 0
                    for tc in range(4):
                        for hc in range(16):
                            gidx += 1
                            psd = [ps_a, ps_b][gidx % 2]
                            if gidx > 2:
                                t.wait_ge(ak.h, ak0 + 64 * e + gidx - 2)
                            elif e > 0:
                                t.wait_ge(ak.h, ak0 + 64 * e - 2 + gidx)
                            for c in range(8):
                                inst = t.matmul(
                                    psd[:, :],
                                    w1b[:, c * 2048 + hc * 128:c * 2048 + hc * 128 + 128],
                                    u2_fm[:, c * S + tc * 512:c * S + tc * 512 + 512],
                                    start=(c == 0), stop=(c == 7))
                            mm.inc(inst)
                        t.wait_ge(ak.h, ak0 + 64 * e + 16 * (tc + 1))
                        if tc == 0:
                            t.wait_ge(lw.h, lw0 + 256 * (e + 1))
                        for sub in range(4):
                            tt = tc * 4 + sub
                            gy = 16 * e + 4 * tc + sub
                            if gy > 0:
                                t.wait_ge(vk.h, vk0 + gy)
                            for hc in range(16):
                                for hhx in range(2):
                                    inst = t.matmul(
                                        ps_o[:, hhx * 512:(hhx + 1) * 512],
                                        hh[:, hc * 512 + sub * 128:
                                           hc * 512 + sub * 128 + 128],
                                        w2b[:, hc * D + hhx * 512:hc * D + hhx * 512 + 512],
                                        start=(hc == 0), stop=(hc == 15))
                            mm.inc(inst)

            @blk.scalar
            def _(s):
                for e in range(8):
                    for tc in range(4):
                        for hc in range(16):
                            s.wait_ge(mm.h, mm0 + 80 * e + 20 * tc + hc + 1)
                            psd = [ps_a, ps_b][(tc * 16 + hc + 1) % 2]
                            ak.inc(s.activation(
                                out=hh[:, hc * 512:(hc + 1) * 512],
                                in_=psd[:, :], func=AF.Silu))

            @blk.vector
            def _(v):
                for e in range(8):
                    for tc in range(4):
                        for sub in range(4):
                            tt = tc * 4 + sub
                            v.wait_ge(mm.h, mm0 + 80 * e + 20 * tc + 16 + sub + 1)
                            v.wait_ge(vk.h, vk0 + 16 * e + 4 * tc + sub)
                            vk.inc(v.scalar_tensor_tensor(
                                t_sb[:, tt * D:(tt + 1) * D],
                                ps_o[:, :], cmb[:, tt * 8 + e:tt * 8 + e + 1],
                                t_sb[:, tt * D:(tt + 1) * D],
                                op0=ALU.mult, op1=ALU.add))
        ak_pg, vk_pg, mm_pg = ak.n, vk.n, mm.n

        with nc.Block() as blk:
            ld0, vk0 = ld.n, vk.n

            @blk.sync
            def _(sync):
                sync.wait_ge(mm.h, mm_pg - 4)
                for t in range(T // 2):
                    ld.inc(sync.dma_start(out=x3[:, t * D:(t + 1) * D],
                                          in_=x_h[t * 128:(t + 1) * 128, :]), 16)
                sync.wait_ge(mm.h, mm_pg)
                for t in range(T // 2, T):
                    ld.inc(sync.dma_start(out=x3[:, t * D:(t + 1) * D],
                                          in_=x_h[t * 128:(t + 1) * 128, :]), 16)
            ld_x3 = ld.n

            @blk.vector
            def _(v):
                v.wait_ge(ld.h, ld_x3)
                v.wait_ge(vk.h, vk_pg)
                for tt in range(T):
                    if tt >= 2:
                        psem = [st, s2][tt % 2]
                        v.wait_ge(psem.h, 16 * ((tt - 2) // 2 + 1))
                    vk.inc(v.scalar_tensor_tensor(
                        xot[:, (tt % 2) * D:(tt % 2) * D + D],
                        t_sb[:, tt * D:(tt + 1) * D], 0.5,
                        x3[:, tt * D:(tt + 1) * D],
                        op0=ALU.mult, op1=ALU.add))

            @blk.gpsimd
            def _(g):
                for tt in range(T):
                    g.wait_ge(vk.h, vk0 + tt + 1)
                    psem = [st, s2][tt % 2]
                    psem.inc(g.dma_start(out=xout_h[tt * 128:(tt + 1) * 128, :],
                                         in_=xot[:, (tt % 2) * D:(tt % 2) * D + D]), 16)
                if dbg_h:
                    for nm, buf in [("u_fm", u_fm), ("ctxn", ctxn), ("negMb", negMb),
                                    ("t", t_sb), ("u2", u2_fm), ("lgs", lgs),
                                    ("cmb", cmb), ("kf", kf), ("vt", vt)]:
                        st.inc(g.dma_start(out=dbg_h[nm][:, :], in_=buf[:, :]), 16)
                g.wait_ge(st.h, st.n)
                g.wait_ge(s2.h, s2.n)
    return p


def _run(prog, in_maps, **kw):
    return run_bass_kernel_spmd(prog.nc, in_maps, core_ids=list(range(NCORE)), **kw)


_P_L0ATTN = None


def run_l0_attn(x_full, W_qkv, b_qkv, W_out, b_out, **kw):
    global _P_L0ATTN
    if _P_L0ATTN is None:
        _P_L0ATTN = build_l0_attn()
    cm = consts_in_map(L0ATTN_CONSTS)
    cm.update(_P_L0ATTN.fvals_in_map())
    im = []
    for c in range(NCORE):
        d = dict(cm)
        d["x"] = np.ascontiguousarray(x_full[c])
        d["W_qkv"] = W_qkv
        d["b_qkv"] = b_qkv
        d["W_out"] = W_out
        d["b_out"] = b_out
        im.append(d)
    res = _run(_P_L0ATTN, im, **kw)
    return np.stack([r["xout"] for r in res.results], axis=0)



A1_CONSTS = ["identf", "ones_col_b", "ones_col_f", "ones_row_b", "ones_row_f"]


def build_l0_ffn_a1():
    p = Prog("l0ffn_a1")
    nc = p.nc
    x_h = nc.dram_tensor("x", [S, D], F32, kind="ExternalInput").ap()
    wgt_h = nc.dram_tensor("WgT", [D, 8], F32, kind="ExternalInput").ap()
    z_h = nc.dram_tensor("zout", [D, S], BF16, kind="ExternalOutput").ap()
    sc_h = nc.dram_tensor("scout", [8, S], F32, kind="ExternalOutput").ap()

    with p.stack:
        p.decl_consts(A1_CONSTS)
        p.decl_fvals([1e-5])
        p.arena(120 * KB)
        x_sb = p.at("x_sb", 0, [128, T * D], F32)
        zf = p.at("zf_", 64 * KB, [128, 8 * S], BF16)
        zt = p.at("zt_", 96 * KB, [128, 2 * D], BF16)
        scf = p.at("scf_", 100 * KB, [8, S], F32)
        stats = p.sb("stats", [128, 8 * T], F32)
        wgb = p.sb("wgb", [128, 64], BF16)
        ps_sc = p.ps("ps_sc", [8, S])

        ld = p.sem("ld")
        lg = p.sem("lg")
        tx = p.sem("tx")
        mm = p.sem("mm")
        ak = p.sem("ak")
        vk = p.sem("vk")
        st = p.sem("st")

        with nc.Block() as blk:
            @blk.sync
            def _(sync):
                p.emit_const_loads(sync, ld)
                for t in range(T):
                    ld.inc(sync.dma_start(out=x_sb[:, t * D:(t + 1) * D],
                                          in_=x_h[t * 128:(t + 1) * 128, :]), 16)

            @blk.gpsimd
            def _(g):
                lg.inc(g.dma_start(out=wgb.rearrange("p (c e) -> p c e", e=8),
                                   in_=wgt_h.rearrange("(c p) e -> p c e", p=128)), 16)
        ld_all = ld.n
        lg_w = lg.n

        with nc.Block() as blk:
            emit_ln(p, blk, x_sb=x_sb, zt=zt, zf=zf, stats=stats,
                    ld=ld, ak=ak, vk=vk, tx=tx, wait_ld=ld_all, eps_ap=p.fc(1e-5))

            @blk.gpsimd
            def _(g):
                g.wait_ge(tx.h, 2048)
                for c in range(8):
                    st.inc(g.dma_start(out=z_h[c * 128:(c + 1) * 128, :],
                                       in_=zf[:, c * S:(c + 1) * S]), 16)
        tx_zf = tx.n
        with nc.Block() as blk:
            @blk.tensor
            def _(t):
                t.wait_ge(tx.h, tx_zf)
                t.wait_ge(lg.h, lg_w)
                for nn in range(4):
                    for c in range(8):
                        inst = t.matmul(ps_sc[:, nn * 512:(nn + 1) * 512],
                                        wgb[:, c * 8:(c + 1) * 8],
                                        zf[:, c * S + nn * 512:c * S + nn * 512 + 512],
                                        start=(c == 0), stop=(c == 7))
                    mm.inc(inst)

            @blk.scalar
            def _(s):
                for nn in range(4):
                    s.wait_ge(mm.h, nn + 1)
                    ak.inc(s.activation(out=scf[:, nn * 512:(nn + 1) * 512],
                                        in_=ps_sc[:, nn * 512:(nn + 1) * 512],
                                        func=AF.Copy))

            @blk.gpsimd
            def _(g):
                g.wait_ge(ak.h, 22)
                st.inc(g.dma_start(out=sc_h[:, :], in_=scf[:, :]), 16)
                g.wait_ge(st.h, st.n)
    return p




class Seq:

    def __init__(self, sem):
        self.sem = sem
        self.script = []

    def op(self, eng, fn, inc=1):
        self.script.append((eng, fn, inc))

    def emit(self, blk, engines):
        base = self.sem.n
        offs = []
        c = base
        for (_, _, inc) in self.script:
            offs.append(c)
            c += inc
        for name, deco in engines.items():
            def body(eng, name=name):
                for i, (en, fn, inc) in enumerate(self.script):
                    if en != name:
                        continue
                    eng.wait_ge(self.sem.h, offs[i])
                    fn(eng).then_inc(self.sem.h, inc)
            deco(body)
        self.sem.n = c


def build_l0_ffn_a2():
    p = Prog("l0ffn_a2")
    nc = p.nc
    z_h = nc.dram_tensor("z_all", [8 * S, D], BF16, kind="ExternalInput").ap()
    sc_h = nc.dram_tensor("sce", [128, 128], F32, kind="ExternalInput").ap()
    w1_h = nc.dram_tensor("W1eT", [D, 2048], F32, kind="ExternalInput").ap()
    w3_h = nc.dram_tensor("W3eT", [D, 2048], F32, kind="ExternalInput").ap()
    w2_h = nc.dram_tensor("W2eT", [2048, D], F32, kind="ExternalInput").ap()
    y_h = nc.dram_tensor("yout", [S, D], BF16, kind="ExternalOutput").ap()
    cidx_t = nc.dram_tensor("cidx", [S, 1], I32, kind="ExternalOutput")
    w_hbm = nc.dram_tensor("w_scr", [128 * 128, 1], F32, kind="Internal")

    with p.stack:
        p.decl_consts(["identf", "ones_col_b", "ones_col_f", "ones_row_b",
                       "ones_row_f", "tri_lt", "tid"])
        p.decl_fvals([2048.0, 1.0, -1.0])
        p.arena(200 * KB)
        w1b = p.at("w1b_", 0, [128, 8 * 2048], BF16)
        w3b = p.at("w3b_", 32 * KB, [128, 8 * 2048], BF16)
        w2b = p.at("w2b_", 64 * KB, [128, 16 * D], BF16)
        zg = p.at("zg_", 96 * KB, [128, T * D], BF16)
        zgf = p.at("zgf_", 128 * KB, [128, 8 * S], BF16)
        hhs = p.at("hhs_", 160 * KB, [128, 16 * 512], BF16)
        sA = p.at("sA_", 176 * KB, [128, 2 * 512], BF16)
        sc = p.at("sc_", 178 * KB, [128, 128], F32)
        msk = p.at("msk_", 178 * KB + 512, [128, 128], F32)
        aux = p.at("aux_", 179 * KB, [128, 64], F32)
        strict = p.at("strict_", 180 * KB, [128, 128], F32)
        gtlo = p.at("gtlo_", 180 * KB + 512, [128, 128], F32)
        eq = p.at("eq_", 181 * KB, [128, 128], F32)
        scn = p.at("scn_", 181 * KB + 512, [128, 128], F32)
        zz = p.at("zz_", 182 * KB, [128, 128], F32)
        zrs = p.at("zrs_", 182 * KB + 512, [128, 128], F32)
        sel = p.at("sel_", 183 * KB, [128, 128], F32)
        rnk = p.at("rnk_", 183 * KB + 512, [128, 128], F32)
        w2d = p.at("w2d_", 184 * KB, [128, 128], F32)
        offs = p.at("offs_", 184 * KB + 512, [128, 128], F32)
        offsi = p.at("offsi_", 185 * KB, [128, 128], I32)
        tidi = p.at("tidi_", 185 * KB + 512, [128, 128], I32)
        cidxs = p.at("cidxs_", 186 * KB, [128, 16], I32)
        wslot = p.at("wslot_", 186 * KB + 64, [128, 16], F32)
        smll = p.at("smll_", 186 * KB + 128, [1, 16], F32)
        yt = p.at("yt_", 187 * KB, [128, 2 * D], BF16)

        ps_a = p.ps("ps_a", [128, 512])
        ps_b = p.ps("ps_b", [128, 512])
        ps_o = p.ps("ps_o", [128, 1024])
        ps_s = p.ps("ps_s", [128, 128])
        ps_1 = p.ps("ps_1", [1, 128])

        ld = p.sem("ld")
        lg = p.sem("lg")
        tx = p.sem("tx")
        sq = p.sem("sq")
        mm = p.sem("mm")
        ak = p.sem("ak")
        vk = p.sem("vk")
        st = p.sem("st")
        s2 = p.sem("s2")

        with nc.Block() as blk:
            @blk.sync
            def _(sync):
                p.emit_const_loads(sync, ld)
                ld.inc(sync.dma_start(out=sc[:, :], in_=sc_h[:, :]), 16)
            ld_all = ld.n

            @blk.gpsimd
            def _(g):
                for c in range(8):
                    lg.inc(g.dma_start(out=w1b[:, c * 2048:(c + 1) * 2048],
                                       in_=w1_h[c * 128:(c + 1) * 128, :]), 16)
                for c in range(8):
                    lg.inc(g.dma_start(out=w3b[:, c * 2048:(c + 1) * 2048],
                                       in_=w3_h[c * 128:(c + 1) * 128, :]), 16)
                for c in range(16):
                    lg.inc(g.dma_start(out=w2b[:, c * D:(c + 1) * D],
                                       in_=w2_h[c * 128:(c + 1) * 128, :]), 16)
        lg_w = lg.n

        NI = 34
        q = Seq(sq)
        OC = p.const["ones_col_f"]
        OR = p.const["ones_row_f"]
        V = lambda fn: q.op("vector", fn)
        A = lambda fn: q.op("scalar", fn)
        M = lambda fn: q.op("tensor", fn)
        G = lambda fn, inc=16: q.op("gpsimd", fn, inc)

        def first(v):
            v.wait_ge(ld.h, ld_all)
            return v.reduce_max(aux[:, 16:17], sc[:, :], axis=AX.X)
        V(first)
        V(lambda v: v.tensor_reduce(aux[:, 17:18], sc[:, :], axis=AX.X, op=ALU.min))
        M(lambda t: t.transpose(ps_s[0:1, 0:128], aux[:, 16:17],
                                p.const["identf"][:, :]))
        A(lambda s: s.activation(out=zz[0:1, 0:128], in_=ps_s[0:1, 0:128],
                                 func=AF.Copy))
        V(lambda v: v.reduce_max(smll[:, 0:1], zz[0:1, 0:128], axis=AX.X))
        M(lambda t: t.transpose(ps_s[0:1, 0:128], aux[:, 17:18],
                                p.const["identf"][:, :]))
        A(lambda s: s.activation(out=zz[0:1, 0:128], in_=ps_s[0:1, 0:128],
                                 func=AF.Copy))
        V(lambda v: v.tensor_reduce(smll[:, 1:2], zz[0:1, 0:128], axis=AX.X,
                                    op=ALU.min))
        M(lambda t: t.matmul(ps_s[:, 0:2], OR[:, :], smll[:, 0:2],
                             start=True, stop=True))
        A(lambda s: s.activation(out=aux[:, 6:7], in_=ps_s[:, 0:1], func=AF.Copy))
        A(lambda s: s.activation(out=aux[:, 1:2], in_=ps_s[:, 0:1],
                                 func=AF.Identity, bias=p.fc(1.0)))
        A(lambda s: s.activation(out=aux[:, 0:1], in_=ps_s[:, 1:2],
                                 func=AF.Identity, bias=p.fc(-1.0)))
        V(lambda v: v.tensor_scalar(aux[:, 7:8], aux[:, 6:7], -1.0, None,
                                    op0=ALU.mult))
        for _ in range(NI):
            V(lambda v: v.tensor_add(aux[:, 2:3], aux[:, 0:1], aux[:, 1:2]))
            V(lambda v: v.tensor_scalar_mul(aux[:, 2:3], aux[:, 2:3], 0.5))
            V(lambda v: v.tensor_scalar(msk[:, :], sc[:, :], aux[:, 2:3], 0.0,
                                        op0=ALU.is_gt, op1=ALU.add,
                                        accum_out=aux[:, 3:4]))
            M(lambda t: t.matmul(ps_1[:, 0:1], OC[:, :], aux[:, 3:4],
                                 start=True, stop=True))
            A(lambda s: s.activation(out=smll[:, 2:3], in_=ps_1[:, 0:1],
                                     func=AF.Copy))
            M(lambda t: t.matmul(ps_s[:, 4:5], OR[:, :], smll[:, 2:3],
                                 start=True, stop=True))
            A(lambda s: s.activation(out=aux[:, 4:5], in_=ps_s[:, 4:5],
                                     func=AF.Copy))
            V(lambda v: v.tensor_scalar(aux[:, 5:6], aux[:, 4:5], 2047.5, None,
                                        op0=ALU.is_gt))
            V(lambda v: v.tensor_sub(aux[:, 18:19], aux[:, 2:3], aux[:, 0:1]))
            V(lambda v: v.tensor_sub(aux[:, 19:20], aux[:, 1:2], aux[:, 2:3]))
            V(lambda v: v.tensor_mul(aux[:, 18:19], aux[:, 18:19], aux[:, 5:6]))
            V(lambda v: v.tensor_mul(aux[:, 19:20], aux[:, 19:20], aux[:, 5:6]))
            V(lambda v: v.tensor_add(aux[:, 0:1], aux[:, 0:1], aux[:, 18:19]))
            V(lambda v: v.tensor_add(aux[:, 1:2], aux[:, 2:3], aux[:, 19:20]))
        V(lambda v: v.tensor_scalar(strict[:, :], sc[:, :], aux[:, 1:2], 0.0,
                                    op0=ALU.is_gt, op1=ALU.add,
                                    accum_out=aux[:, 8:9]))
        V(lambda v: v.tensor_scalar(gtlo[:, :], sc[:, :], aux[:, 0:1], 0.0,
                                    op0=ALU.is_gt, op1=ALU.add,
                                    accum_out=aux[:, 9:10]))
        V(lambda v: v.tensor_sub(eq[:, :], gtlo[:, :], strict[:, :]))
        V(lambda v: v.tensor_sub(aux[:, 10:11], aux[:, 9:10], aux[:, 8:9]))
        V(lambda v: v.memset(zrs[:, :], 0.0))
        V(lambda v: v.tensor_tensor_scan(scn[:, :], eq[:, :], zrs[:, :], 0.0,
                                         op0=ALU.add, op1=ALU.add))
        M(lambda t: t.matmul(ps_1[:, 0:1], OC[:, :], aux[:, 8:9],
                             start=True, stop=True))
        A(lambda s: s.activation(out=smll[:, 2:3], in_=ps_1[:, 0:1],
                                 func=AF.Identity, scale=-1.0,
                                 bias=p.fc(2048.0)[0:1, :]))
        M(lambda t: t.matmul(ps_s[:, 4:5], OR[:, :], smll[:, 2:3],
                             start=True, stop=True))
        A(lambda s: s.activation(out=aux[:, 11:12], in_=ps_s[:, 4:5], func=AF.Copy))
        M(lambda t: t.matmul(ps_s[:, 8:9], p.const["tri_lt"][:, :], aux[:, 10:11],
                             start=True, stop=True))
        A(lambda s: s.activation(out=aux[:, 12:13], in_=ps_s[:, 8:9], func=AF.Copy))
        V(lambda v: v.tensor_scalar_add(rnk[:, :], scn[:, :], aux[:, 12:13]))
        V(lambda v: v.tensor_scalar(msk[:, :], rnk[:, :], aux[:, 11:12], None,
                                    op0=ALU.is_le))
        V(lambda v: v.tensor_mul(msk[:, :], eq[:, :], msk[:, :]))
        V(lambda v: v.tensor_add(sel[:, :], strict[:, :], msk[:, :]))
        A(lambda s: s.activation(out=zz[:, :], in_=sc[:, :], func=AF.Exp,
                                 bias=aux[:, 7:8]))
        V(lambda v: v.scalar_tensor_tensor(w2d[:, :], zz[:, :], 1.0, sel[:, :],
                                           op0=ALU.mult, op1=ALU.mult,
                                           accum_out=aux[:, 16:17]))
        M(lambda t: t.matmul(ps_1[:, 0:1], OC[:, :], aux[:, 16:17],
                             start=True, stop=True))
        A(lambda s: s.activation(out=smll[:, 3:4], in_=ps_1[:, 0:1], func=AF.Copy))
        V(lambda v: v.reciprocal(smll[:, 4:5], smll[:, 3:4]))
        V(lambda v: v.tensor_scalar_mul(smll[:, 5:6], smll[:, 4:5], 0.5))
        M(lambda t: t.matmul(ps_s[:, 4:5], OR[:, :], smll[:, 5:6],
                             start=True, stop=True))
        A(lambda s: s.activation(out=aux[:, 15:16], in_=ps_s[:, 4:5], func=AF.Copy))
        V(lambda v: v.tensor_scalar(w2d[:, :], w2d[:, :], aux[:, 15:16], None,
                                    op0=ALU.mult))
        V(lambda v: v.tensor_scalar(msk[:, :], sel[:, :], 1.0, 0.0,
                                    op0=ALU.mult, op1=ALU.add,
                                    accum_out=aux[:, 13:14]))
        V(lambda v: v.tensor_tensor_scan(scn[:, :], sel[:, :], zrs[:, :], 0.0,
                                         op0=ALU.add, op1=ALU.add))
        M(lambda t: t.matmul(ps_s[:, 8:9], p.const["tri_lt"][:, :], aux[:, 13:14],
                             start=True, stop=True))
        A(lambda s: s.activation(out=aux[:, 14:15], in_=ps_s[:, 8:9], func=AF.Copy))
        V(lambda v: v.tensor_sub(scn[:, :], scn[:, :], sel[:, :]))
        V(lambda v: v.tensor_scalar_add(rnk[:, :], scn[:, :], aux[:, 14:15]))
        V(lambda v: v.tensor_copy(offsi[:, :], rnk[:, :]))
        V(lambda v: v.tensor_single_scalar(tidi[:, :], offsi[:, :], 7,
                                           op=ALU.arith_shift_right))
        V(lambda v: v.tensor_single_scalar(offsi[:, :], offsi[:, :], 127,
                                           op=ALU.bitwise_and))
        V(lambda v: v.tensor_copy(scn[:, :], offsi[:, :]))
        V(lambda v: v.tensor_copy(offs[:, :], tidi[:, :]))
        V(lambda v: v.scalar_tensor_tensor(eq[:, :], offs[:, :], 16384.0,
                                           p.const["tid"][:, :],
                                           op0=ALU.mult, op1=ALU.add))
        V(lambda v: v.tensor_scalar(msk[:, :], scn[:, :], 1.0, None, op0=ALU.add))
        V(lambda v: v.tensor_mul(msk[:, :], msk[:, :], sel[:, :]))
        V(lambda v: v.tensor_scalar(msk[:, :], msk[:, :], 1.0, None,
                                    op0=ALU.subtract))
        for qq in range(128):
            V(lambda v, qq=qq: v.tensor_single_scalar(gtlo[:, :], msk[:, :],
                                                      float(qq), op=ALU.is_equal))
            V(lambda v, qq=qq: v.scalar_tensor_tensor(
                zrs[:, :], eq[:, :], 1.0, gtlo[:, :], op0=ALU.mult, op1=ALU.mult,
                accum_out=strict[:, qq:qq + 1]))
        M(lambda t: t.transpose(ps_s[:, :], strict[:, :], p.const["identf"][:, :]))
        A(lambda s: s.activation(out=w2d[:, :] if False else zz[:, :],
                                 in_=ps_s[:, :], func=AF.Copy))
        V(lambda v: v.tensor_copy(offsi[:, :], zz[:, :]))
        V(lambda v: v.tensor_single_scalar(tidi[:, :], offsi[:, :], 14,
                                           op=ALU.arith_shift_right))
        V(lambda v: v.tensor_single_scalar(offsi[:, :], offsi[:, :], 16383,
                                           op=ALU.bitwise_and))
        V(lambda v: v.tensor_copy(offs[:, :], offsi[:, :]))
        V(lambda v: v.tensor_copy(zz[:, :], tidi[:, :]))
        for g_ in range(16):
            V(lambda v, g_=g_: v.tensor_single_scalar(gtlo[:, :], zz[:, :],
                                                      float(g_), op=ALU.is_equal))
            V(lambda v, g_=g_: v.scalar_tensor_tensor(
                zrs[:, :], offs[:, :], 1.0, gtlo[:, :], op0=ALU.mult, op1=ALU.mult,
                accum_out=rnk[:, g_:g_ + 1]))
        V(lambda v: v.tensor_copy(cidxs[:, :], rnk[:, 0:16]))
        rbnd = [None]

        def _setup_reg(g):
            rbnd[0] = g.alloc_register("rbnd")
            return g.reg_mov(rbnd[0], 16383)
        G(_setup_reg, inc=1)
        G(lambda g: g.dma_start(out=w_hbm.ap().rearrange("(a b) c -> a (b c)", b=128),
                                in_=w2d[:, :]))
        for g_ in range(16):
            G(lambda g, g_=g_: g.indirect_dma_start(
                wslot[:, g_:g_ + 1], None,
                w_hbm.ap()[:, :],
                IndirectOffsetOnAxis(ap=cidxs[:, g_:g_ + 1], axis=0),
                bounds_check=rbnd[0], oob_is_err=False))
        for g_ in range(16):
            G(lambda g, g_=g_: g.indirect_dma_start(
                zg[:, g_ * D:(g_ + 1) * D], None,
                z_h, IndirectOffsetOnAxis(ap=cidxs[:, g_:g_ + 1], axis=0),
                bounds_check=rbnd[0], oob_is_err=False))
        for g_ in range(16):
            G(lambda g, g_=g_: g.dma_start(
                out=cidx_t.ap()[g_ * 128:(g_ + 1) * 128, :],
                in_=cidxs[:, g_:g_ + 1]))

        with nc.Block() as blk:
            q.emit(blk, {"vector": blk.vector, "scalar": blk.scalar,
                         "tensor": blk.tensor, "gpsimd": blk.gpsimd})
        sq_gate = sq.n

        with nc.Block() as blk:
            mm0, ak0, vk0, tx0 = mm.n, ak.n, vk.n, tx.n

            @blk.sync
            def _(sync):
                sync.wait_ge(sq.h, sq_gate)
                for t in range(T):
                    for c in range(8):
                        tx.inc(sync.dma_start_transpose(
                            zgf[:, c * S + t * 128:c * S + t * 128 + 128],
                            zg[:, t * D + c * 128:t * D + c * 128 + 128]), 16)

            @blk.tensor
            def _(t):
                t.wait_ge(tx.h, tx0 + 2048)
                t.wait_ge(lg.h, lg_w)
                for tc in range(4):
                    for hc in range(16):
                        if tc * 16 + hc > 0:
                            t.wait_ge(vk.h, vk0 + 16 * tc + hc)
                        for c in range(8):
                            i1 = t.matmul(
                                ps_a[:, :],
                                w1b[:, c * 2048 + hc * 128:c * 2048 + hc * 128 + 128],
                                zgf[:, c * S + tc * 512:c * S + tc * 512 + 512],
                                start=(c == 0), stop=(c == 7))
                        mm.inc(i1)
                        for c in range(8):
                            i3 = t.matmul(
                                ps_b[:, :],
                                w3b[:, c * 2048 + hc * 128:c * 2048 + hc * 128 + 128],
                                zgf[:, c * S + tc * 512:c * S + tc * 512 + 512],
                                start=(c == 0), stop=(c == 7))
                        mm.inc(i3)
                    t.wait_ge(vk.h, vk0 + 16 * (tc + 1))
                    for sub in range(4):
                        if sub >= 1:
                            t.wait_ge(ak.h, ak0 + 20 * tc + 16 + sub)
                        elif tc >= 1:
                            t.wait_ge(ak.h, ak0 + 20 * tc)
                        for hc in range(16):
                            for hhx in range(2):
                                iy = t.matmul(
                                    ps_o[:, hhx * 512:(hhx + 1) * 512],
                                    hhs[:, hc * 512 + sub * 128:hc * 512 + sub * 128 + 128],
                                    w2b[:, hc * D + hhx * 512:hc * D + hhx * 512 + 512],
                                    start=(hc == 0), stop=(hc == 15))
                        mm.inc(iy)

            @blk.scalar
            def _(s):
                for tc in range(4):
                    for hc in range(16):
                        s.wait_ge(mm.h, mm0 + 36 * tc + 2 * hc + 1)
                        if tc * 16 + hc >= 2:
                            s.wait_ge(vk.h, vk0 + 16 * tc + hc - 1)
                        ak.inc(s.activation(
                            out=sA[:, (hc % 2) * 512:(hc % 2) * 512 + 512],
                            in_=ps_a[:, :], func=AF.Silu))
                    for sub in range(4):
                        s.wait_ge(mm.h, mm0 + 36 * tc + 32 + sub + 1)
                        tt = 4 * tc + sub
                        if tt >= 2:
                            psem = [st, s2][tt % 2]
                            s.wait_ge(psem.h, 16 * ((tt - 2) // 2 + 1))
                        ak.inc(s.activation(
                            out=yt[:, (sub % 2) * D:(sub % 2) * D + D],
                            in_=ps_o[:, :], func=AF.Identity,
                            scale=wslot[:, 4 * tc + sub:4 * tc + sub + 1]))

            @blk.vector
            def _(v):
                for tc in range(4):
                    for hc in range(16):
                        v.wait_ge(mm.h, mm0 + 36 * tc + 2 * hc + 2)
                        v.wait_ge(ak.h, ak0 + 20 * tc + hc + 1)
                        vk.inc(v.tensor_mul(
                            hhs[:, hc * 512:(hc + 1) * 512],
                            ps_b[:, :],
                            sA[:, (hc % 2) * 512:(hc % 2) * 512 + 512]))

            @blk.gpsimd
            def _(g):
                for tc in range(4):
                    for sub in range(4):
                        tt = 4 * tc + sub
                        g.wait_ge(ak.h, ak0 + 20 * tc + 16 + sub + 1)
                        psem = [st, s2][tt % 2]
                        psem.inc(g.dma_start(out=y_h[tt * 128:(tt + 1) * 128, :],
                                             in_=yt[:, (sub % 2) * D:(sub % 2) * D + D]), 16)
                g.wait_ge(st.h, st.n)
                g.wait_ge(s2.h, s2.n)
    return p


def build_l0_ffn_a3(nexp=8):
    p = Prog("l0ffn_a3")
    nc = p.nc
    x_h = nc.dram_tensor("x", [S, D], F32, kind="ExternalInput").ap()
    y_h = nc.dram_tensor("y_all", [8 * S, D], BF16, kind="ExternalInput").ap()
    ci_h = nc.dram_tensor("cidx_cols", [128, 128], I32, kind="ExternalInput").ap()
    rb_h = nc.dram_tensor("rank_base", [1, 1], F32, kind="ExternalInput").ap()
    xout_h = nc.dram_tensor("xout", [S, D], F32, kind="ExternalOutput").ap()
    acc_t = nc.dram_tensor("acc", [S, D], BF16, kind="Internal")

    with p.stack:
        p.decl_consts(["identf", "ones_col_f", "ones_row_f"])
        p.arena(160 * KB)
        x_sb = p.at("x_sb", 0, [128, T * D], F32)
        ybuf = p.at("ybuf_", 64 * KB, [128, 2 * D], BF16)
        zrot = p.at("zrot_", 68 * KB, [128, D], BF16)
        accs = p.at("accs_", 72 * KB, [128, 2 * D], BF16)
        xo = p.at("xo_", 80 * KB, [128, 2 * D], F32)
        cif = p.at("cif_", 96 * KB, [128, 128], F32)
        cii = p.at("cii_", 96 * KB + 512, [128, 128], I32)
        loc = p.at("loc_", 97 * KB, [128, 128], F32)
        loci = p.at("loci_", 97 * KB + 512, [128, 128], I32)
        msk = p.at("msk_", 98 * KB, [128, 128], F32)
        rbb = p.at("rbb_", 98 * KB + 512, [128, 1], F32)

        ps_s = p.ps("ps_s", [128, 128])

        ld = p.sem("ld")
        lg = p.sem("lg")
        sq = p.sem("sq")
        mm = p.sem("mm")
        ak = p.sem("ak")
        vk = p.sem("vk")
        st = p.sem("st")
        s2 = p.sem("s2")

        with nc.Block() as blk:
            @blk.sync
            def _(sync):
                p.emit_const_loads(sync, ld)
                ld.inc(sync.dma_start(out=cii[:, :], in_=ci_h[:, :]), 16)
                ld.inc(sync.dma_start(out=rbb[0:1, 0:1], in_=rb_h[:, :]), 16)
                for t in range(T):
                    ld.inc(sync.dma_start(out=x_sb[:, t * D:(t + 1) * D],
                                          in_=x_h[t * 128:(t + 1) * 128, :]), 16)
        ld_all = ld.n

        import os as _os
        _noprep = _os.environ.get("A3_NOPREP") == "1"
        q = Seq(sq)

        def first(v):
            v.wait_ge(ld.h, ld_all)
            return v.tensor_copy(cif[:, :], cii[:, :])
        if not _noprep:
            q.op("vector", first)
            q.op("tensor", lambda t: t.matmul(ps_s[:, 0:1], p.const["ones_row_f"][:, :],
                                              rbb[0:1, 0:1], start=True, stop=True))
            q.op("scalar", lambda s: s.activation(out=rbb[:, :], in_=ps_s[:, 0:1],
                                                  func=AF.Copy))
            q.op("vector", lambda v: v.tensor_scalar(loc[:, :], cif[:, :], rbb[:, 0:1],
                                                     None, op0=ALU.subtract))
            q.op("vector", lambda v: v.tensor_single_scalar(msk[:, :], loc[:, :], 0.0,
                                                            op=ALU.is_lt))
            q.op("vector", lambda v: v.scalar_tensor_tensor(loc[:, :], msk[:, :], 1e9,
                                                            loc[:, :], op0=ALU.mult,
                                                            op1=ALU.add))
            q.op("vector", lambda v: v.tensor_copy(loci[:, :], loc[:, :]))

        def first2(v):
            if _noprep:
                v.wait_ge(ld.h, ld_all)
            return v.memset(zrot[:, :], 0.0)
        q.op("vector", first2)
        for t in range(T):
            q.op("gpsimd", lambda g, t=t: g.dma_start(
                out=acc_t.ap()[t * 128:(t + 1) * 128, :], in_=zrot[:, :]), 16)
        with nc.Block() as blk:
            q.emit(blk, {"vector": blk.vector, "scalar": blk.scalar,
                         "tensor": blk.tensor, "gpsimd": blk.gpsimd})
        sq_prep = sq.n

        with nc.Block() as blk:
            lg0, sq0 = lg.n, sq.n

            @blk.sync
            def _(sync):
                sync.wait_ge(sq.h, sq_prep)
                for i in range(nexp * 16):
                    if i >= 2:
                        sync.wait_ge(sq.h, sq0 + 16 * (i - 1))
                    psem = [ld, lg][i % 2]
                    psem.inc(sync.dma_start(
                        out=ybuf[:, (i % 2) * D:(i % 2) * D + D],
                        in_=y_h[i * 128:(i + 1) * 128, :]), 16)

            @blk.gpsimd
            def _(g):
                rb = g.alloc_register("rb_a3")
                g.reg_mov(rb, S - 1)
                nld = ld_all
                nlg = lg0
                for e in range(nexp):
                    if e > 0:
                        g.wait_ge(sq.h, sq0 + 256 * e)
                    for g_ in range(16):
                        i = e * 16 + g_
                        if i % 2 == 0:
                            nld += 16
                            g.wait_ge(ld.h, nld)
                        else:
                            nlg += 16
                            g.wait_ge(lg.h, nlg)
                        sq.inc(g.indirect_dma_start(
                            acc_t.ap()[:, :],
                            IndirectOffsetOnAxis(ap=loci[:, i:i + 1], axis=0),
                            ybuf[:, (i % 2) * D:(i % 2) * D + D], None,
                            bounds_check=rb, oob_is_err=False,
                            compute_op=ALU.add), 16)
        sq_sc = sq.n

        _noread = _os.environ.get("A3_NOREAD") == "1"
        with nc.Block() as blk:
            lg1, vk0 = lg.n, vk.n

            if _noread:
                @blk.gpsimd
                def _(g):
                    g.wait_ge(sq.h, sq_sc)
                    for t in range(T):
                        st.inc(g.dma_start(out=xout_h[t * 128:(t + 1) * 128, :],
                                           in_=x_sb[:, t * D:(t + 1) * D]), 16)
                    g.wait_ge(st.h, st.n)
            else:
                mm0 = mm.n

                def load(g, t):
                    psem = [lg, mm][t % 2]
                    psem.inc(g.dma_start(
                        out=accs[:, (t % 2) * D:(t % 2) * D + D],
                        in_=acc_t.ap()[t * 128:(t + 1) * 128, :]), 16)

                @blk.gpsimd
                def _(g):
                    g.wait_ge(sq.h, sq_sc)
                    load(g, 0)
                    load(g, 1)
                    for t in range(T):
                        g.wait_ge(vk.h, vk0 + t + 1)
                        psem = [st, s2][t % 2]
                        psem.inc(g.dma_start(out=xout_h[t * 128:(t + 1) * 128, :],
                                             in_=xo[:, (t % 2) * D:(t % 2) * D + D]), 16)
                        if t + 2 < T:
                            load(g, t + 2)
                    g.wait_ge(st.h, st.n)
                    g.wait_ge(s2.h, s2.n)

                @blk.vector
                def _(v):
                    for t in range(T):
                        psem = [lg, mm][t % 2]
                        base = lg1 if t % 2 == 0 else mm0
                        v.wait_ge(psem.h, base + 16 * (t // 2 + 1))
                        if t >= 2:
                            psem = [st, s2][t % 2]
                            v.wait_ge(psem.h, 16 * ((t - 2) // 2 + 1))
                        vk.inc(v.tensor_add(xo[:, (t % 2) * D:(t % 2) * D + D],
                                            x_sb[:, t * D:(t + 1) * D],
                                            accs[:, (t % 2) * D:(t % 2) * D + D]))
    return p


_P_A1 = _P_A2 = _P_A3 = None


def run_l0_ffn(x_full, Wg, W1, W2, W3, **kw):
    global _P_A1, _P_A2, _P_A3
    if _P_A1 is None:
        _P_A1 = build_l0_ffn_a1()
        _P_A2 = build_l0_ffn_a2()
        _P_A3 = build_l0_ffn_a3()
    cm1 = consts_in_map(A1_CONSTS)
    cm1.update(_P_A1.fvals_in_map())
    im = []
    for c in range(NCORE):
        d = dict(cm1)
        d["x"] = np.ascontiguousarray(x_full[c])
        d["WgT"] = np.ascontiguousarray(Wg.T)
        im.append(d)
    r1 = _run(_P_A1, im, **kw)
    z_all = np.concatenate([np.ascontiguousarray(r["zout"].T) for r in r1.results], 0)
    sc_all = np.stack([r["scout"] for r in r1.results], 0)
    cm2 = consts_in_map(["identf", "ones_col_b", "ones_col_f", "ones_row_b",
                         "ones_row_f", "tri_lt", "tid"])
    cm2.update(_P_A2.fvals_in_map())
    im = []
    for e in range(NCORE):
        d = dict(cm2)
        d["z_all"] = z_all
        d["sce"] = np.ascontiguousarray(
            sc_all[:, e, :].reshape(16384).reshape(128, 128))
        d["W1eT"] = np.ascontiguousarray(W1[e].T)
        d["W3eT"] = np.ascontiguousarray(W3[e].T)
        d["W2eT"] = np.ascontiguousarray(W2[e].T)
        im.append(d)
    r2 = _run(_P_A2, im, **kw)
    y_all = np.concatenate([r["yout"] for r in r2.results], 0)
    cidx_all = np.stack([r["cidx"].reshape(2048) for r in r2.results], 0)
    cm3 = consts_in_map(["identf", "ones_col_f", "ones_row_f"])
    im = []
    cidx_cols = np.zeros((128, 128), np.int32)
    for e in range(8):
        for g in range(16):
            cidx_cols[:, e * 16 + g] = cidx_all[e, g * 128:(g + 1) * 128]
    for c in range(NCORE):
        d = dict(cm3)
        d["x"] = np.ascontiguousarray(x_full[c])
        d["y_all"] = y_all
        d["cidx_cols"] = cidx_cols
        d["rank_base"] = np.full((1, 1), 2048.0 * c, np.float32)
        im.append(d)
    r3 = _run(_P_A3, im, **kw)
    return np.stack([r["xout"] for r in r3.results], 0)


_P_L1ATTN = _P_L1FFN = None


def run_l1_attn(x_full, Wqkv_w, b_qkv, out_w, b_out, dd, **kw):
    global _P_L1ATTN
    if _P_L1ATTN is None:
        _P_L1ATTN = build_l1_attn()
    cm = consts_in_map(["identf", "ones_col_b", "ones_row_b", "ones_row_f",
                        "dist2d", "hsel"])
    cm.update(_P_L1ATTN.fvals_in_map())
    wqkvT = np.ascontiguousarray(Wqkv_w.T)
    woT = np.ascontiguousarray(out_w.T)
    ddr = np.asarray(dd, np.float32).reshape(1, 1)
    im = []
    for c in range(NCORE):
        d = dict(cm)
        d["x"] = np.ascontiguousarray(x_full[c])
        d["WqkvT"] = wqkvT
        d["b_qkv"] = b_qkv
        d["WoT"] = woT
        d["b_out"] = b_out
        d["dd"] = ddr
        im.append(d)
    res = _run(_P_L1ATTN, im, **kw)
    return np.stack([r["xout"] for r in res.results], axis=0)


def run_l1_ffn(x_full, g1, g2, Wq, Wdkv, Wuk, Wuv, Wo, gate_w, w1, w2, **kw):
    global _P_L1FFN
    if _P_L1FFN is None:
        _P_L1FFN = build_l1_ffn()
    cm = consts_in_map(["identf", "ones_col_b", "ones_row_b", "ones_row_f",
                        "hsel", "trimask"])
    cm.update(_P_L1FFN.fvals_in_map())
    im = []
    for c in range(NCORE):
        d = dict(cm)
        d["x"] = np.ascontiguousarray(x_full[c])
        d.update(g1=g1, g2=g2, Wq=Wq, Wdkv=Wdkv, Wuk=Wuk, Wuv=Wuv, Wo=Wo,
                 gate_w=gate_w, w1=w1, w2=w2)
        im.append(d)
    res = _run(_P_L1FFN, im, **kw)
    return np.stack([r["xout"] for r in res.results], axis=0)


def kernel(**inputs):
    x = np.ascontiguousarray(np.asarray(inputs["L0_attn_x"], np.float32))
    Wg = np.asarray(inputs["L0_ffn_Wg"], np.float32)
    W1 = np.asarray(inputs["L0_ffn_W1"], np.float32)
    W2 = np.asarray(inputs["L0_ffn_W2"], np.float32)
    W3 = np.asarray(inputs["L0_ffn_W3"], np.float32)
    x = run_l0_ffn(x, Wg, W1, W2, W3)
    x = run_l0_attn(x,
                    np.asarray(inputs["L0_attn_W_qkv"], np.float32),
                    np.asarray(inputs["L0_attn_b_qkv"], np.float32),
                    np.asarray(inputs["L0_attn_W_out"], np.float32),
                    np.asarray(inputs["L0_attn_b_out"], np.float32))
    x = run_l0_ffn(x, Wg, W1, W2, W3)
    ffn1 = dict(
        g1=np.asarray(inputs["L1_ffn_g1"], np.float32),
        g2=np.asarray(inputs["L1_ffn_g2"], np.float32),
        Wq=np.asarray(inputs["L1_ffn_Wq"], np.float32),
        Wdkv=np.asarray(inputs["L1_ffn_Wdkv"], np.float32),
        Wuk=np.asarray(inputs["L1_ffn_Wuk"], np.float32),
        Wuv=np.asarray(inputs["L1_ffn_Wuv"], np.float32),
        Wo=np.asarray(inputs["L1_ffn_Wo"], np.float32),
        gate_w=np.asarray(inputs["L1_ffn_gate_w"], np.float32),
        w1=np.asarray(inputs["L1_ffn_w1"], np.float32),
        w2=np.asarray(inputs["L1_ffn_w2"], np.float32))
    x = run_l1_ffn(x, **ffn1)
    x = run_l1_attn(x,
                    np.asarray(inputs["L1_attn_Wqkv_w"], np.float32),
                    np.asarray(inputs["L1_attn_Wqkv_b"], np.float32),
                    np.asarray(inputs["L1_attn_out_w"], np.float32),
                    np.asarray(inputs["L1_attn_out_b"], np.float32),
                    np.asarray(inputs["L1_attn_distance_decay"], np.float32))
    x = run_l1_ffn(x, **ffn1)
    return np.ascontiguousarray(x.astype(np.float32))


# version 04 (final)
# speedup vs baseline: 1.1162x; 1.0725x over previous
import math
import numpy as np
import ml_dtypes
import contextlib
import concourse.bass as bass
import concourse.mybir as mybir
from concourse.bass import IndirectOffsetOnAxis
from concourse.bass_utils import run_bass_kernel_spmd

F32 = mybir.dt.float32
BF16 = mybir.dt.bfloat16
I32 = mybir.dt.int32
AF = mybir.ActivationFunctionType
ALU = mybir.AluOpType
AX = mybir.AxisListType

S = 2048
T = 16
D = 1024
NCORE = 8
BF = ml_dtypes.bfloat16
KB = 1024


def host_consts():
    c = {}
    c["identf"] = np.eye(128, dtype=np.float32)
    c["ones_col_b"] = np.ones((128, 1), dtype=BF)
    c["ones_col_f"] = np.ones((128, 1), dtype=np.float32)
    c["ones_row_b"] = np.ones((1, 128), dtype=BF)
    c["ones_row_f"] = np.ones((1, 128), dtype=np.float32)
    c["tri_lt"] = (np.arange(128)[:, None] < np.arange(128)[None, :]).astype(np.float32)
    c["tid"] = (128.0 * np.arange(128)[:, None] + np.arange(128)[None, :]).astype(np.float32)
    tm = np.zeros((128, 4 * 512), dtype=BF)
    for j in range(4):
        f = np.arange(512)[None, :]
        pp = np.arange(128)[:, None]
        tm[:, j * 512:(j + 1) * 512] = (f - pp >= 128 * j).astype(BF)
    c["trimask"] = tm
    c["ramp4096"] = np.abs((32.0 * np.arange(128)[:, None] + np.arange(32)[None, :]) - 2048.0).astype(np.float32)
    c["dist2d"] = np.abs(np.arange(4096)[None, :] - 2048 - np.arange(128)[:, None]).astype(np.int16)
    hs = np.zeros((128, 8 * 16), dtype=BF)
    for cc in range(8):
        for p_ in range(128):
            hs[p_, 16 * cc + 2 * cc + (p_ >= 64)] = 1.0
    c["hsel"] = hs
    c["eoffs"] = np.tile(2048.0 * (np.arange(128)[None, :] // 16),
                         (128, 1)).astype(np.float32)
    c["ones_sq_b"] = np.ones((128, 128), dtype=BF)
    return c


CONST_SPECS = {
    "identf": ([128, 128], F32),
    "ones_col_b": ([128, 1], BF16), "ones_col_f": ([128, 1], F32),
    "ones_row_b": ([1, 128], BF16),
    "ones_row_f": ([1, 128], F32),
    "tri_lt": ([128, 128], F32), "tid": ([128, 128], F32),
    "trimask": ([128, 4 * 512], BF16), "ramp4096": ([128, 32], F32),
    "hsel": ([128, 8 * 16], BF16), "dist2d": ([128, 4096], mybir.dt.int16),
    "eoffs": ([128, 128], F32), "ones_sq_b": ([128, 128], BF16),
}


class Sem:
    def __init__(self, h):
        self.h = h
        self.n = 0

    def inc(self, inst, k=1):
        inst.then_inc(self.h, k)
        self.n += k
        return inst


class Prog:
    def __init__(self, name):
        self.nc = bass.Bass("TRN2", target_bir_lowering=False, name=name)
        self.stack = contextlib.ExitStack()
        self.const = {}

    def sem(self, name):
        return Sem(self.stack.enter_context(self.nc.semaphore(name)))

    def sb(self, name, shape, dtype):
        return self.stack.enter_context(self.nc.sbuf_tensor(name, shape, dtype))

    def ps(self, name, shape, dtype=F32):
        return self.stack.enter_context(self.nc.psum_tensor(name, shape, dtype))

    def arena(self, nbytes):
        self._arena = self.stack.enter_context(
            self.nc.sbuf_tensor("arena", [128, nbytes], mybir.dt.uint8,
                                is_transpose_dest=True))
        self._arena_addr = self.nc.lookup_mloc(self._arena).addr

    def at(self, name, off, shape, dtype):
        assert off % 64 == 0
        return self.nc.alloc_sbuf_tensor_at(name, shape, dtype,
                                            offset=self._arena_addr + off)

    def decl_fvals(self, values):
        self.fvals = list(values)
        nc = self.nc
        self.fvals_ap = nc.dram_tensor("c_fvals", [128, max(1, len(self.fvals))], F32,
                                       kind="ExternalInput").ap()
        self.fvals_sb = self.sb("k_fvals", [128, max(1, len(self.fvals))], F32)

    def fc(self, v):
        i = self.fvals.index(v)
        return self.fvals_sb[:, i:i + 1]

    def fvals_in_map(self):
        a = np.zeros((128, max(1, len(self.fvals))), np.float32)
        for i, v in enumerate(self.fvals):
            a[:, i] = v
        return {"c_fvals": a}

    def decl_consts(self, names=None):
        nc = self.nc
        self.const_aps = {}
        self.const_names = list(names) if names else list(CONST_SPECS)
        for nm in self.const_names:
            shape, dt = CONST_SPECS[nm]
            self.const_aps[nm] = nc.dram_tensor("c_" + nm, shape, dt, kind="ExternalInput").ap()
            self.const[nm] = self.sb("k_" + nm, shape, dt)

    def emit_const_loads(self, sync, ld):
        for nm in self.const_names:
            ld.inc(sync.dma_start(out=self.const[nm][:, :], in_=self.const_aps[nm][:, :]), 16)
        if hasattr(self, "fvals_ap"):
            ld.inc(sync.dma_start(out=self.fvals_sb[:, :], in_=self.fvals_ap[:, :]), 16)


def consts_in_map(names=None):
    hc = host_consts()
    names = names or list(CONST_SPECS)
    return {("c_" + k): hc[k] for k in names}


L0ATTN_CONSTS = ["identf", "ones_col_b", "ones_row_b", "ones_row_f"]



def emit_ln(p, blk, *, x_sb, zt, zf, stats, ld, ak, vk, tx, wait_ld, eps_ap):
    nc = p.nc
    ak0, vk0, tx0 = ak.n, vk.n, tx.n
    rs = p.sem(f"lnrs{ak0}")
    rv = p.sem(f"lnrv{ak0}")
    rs0, rv0 = rs.n, rv.n

    @blk.scalar
    def _(s):
        s.wait_ge(ld.h, wait_ld)
        for t in range(T):
            if t >= 2:
                s.wait_ge(rs.h, rs0 + t - 1)
            inst = s.activation(out=zt[:, (t % 2) * D:(t % 2) * D + D],
                                in_=x_sb[:, t * D:(t + 1) * D], func=AF.Square,
                                accum_out=stats[:, T + t:T + t + 1])
            if t == T - 1:
                ak.inc(inst)
            else:
                rs.inc(inst)

    @blk.vector
    def _(v):
        v.wait_ge(ld.h, wait_ld)
        for t in range(T):
            rv.inc(v.tensor_reduce(stats[:, t:t + 1], x_sb[:, t * D:(t + 1) * D],
                                   axis=AX.X, op=ALU.add))

    @blk.vector
    def _(v):
        v.wait_ge(rs.h, rs0 + T - 1)
        v.wait_ge(ak.h, ak0 + 1)
        v.wait_ge(rv.h, rv0 + T)
        vk.inc(v.tensor_scalar_mul(stats[:, 2 * T:3 * T], stats[:, 0:T], 1.0 / D))
        v.wait_ge(vk.h, vk0 + 1)
        m = stats[:, 2 * T:3 * T]
        v.tensor_mul(stats[:, 5 * T:6 * T], m, m)
        vk.inc(v.tensor_scalar(stats[:, 6 * T:7 * T], stats[:, T:2 * T], 1.0 / D,
                               None, op0=ALU.mult))
        v.wait_ge(vk.h, vk0 + 2)
        vk.inc(v.tensor_sub(stats[:, 6 * T:7 * T], stats[:, 6 * T:7 * T],
                            stats[:, 5 * T:6 * T]))

    @blk.scalar
    def _(s):
        s.wait_ge(vk.h, vk0 + 3)
        ak.inc(s.activation(out=stats[:, 5 * T:6 * T], in_=stats[:, 6 * T:7 * T],
                            func=AF.Sqrt, bias=eps_ap))

    @blk.vector
    def _(v):
        v.wait_ge(ak.h, ak0 + 2)
        vk.inc(v.reciprocal(stats[:, 3 * T:4 * T], stats[:, 5 * T:6 * T]))
        v.wait_ge(vk.h, vk0 + 4)
        vk.inc(v.scalar_tensor_tensor(stats[:, 4 * T:5 * T], stats[:, 2 * T:3 * T],
                                      -1.0, stats[:, 3 * T:4 * T],
                                      op0=ALU.mult, op1=ALU.mult))

    @blk.scalar
    def _(s):
        s.wait_ge(vk.h, vk0 + 5)
        for t in range(T):
            if t >= 2:
                s.wait_ge(tx.h, tx0 + 8 * 16 * (t - 1))
            ak.inc(s.activation(out=zt[:, (t % 2) * D:(t % 2) * D + D],
                                in_=x_sb[:, t * D:(t + 1) * D], func=AF.Identity,
                                scale=stats[:, 3 * T + t:3 * T + t + 1],
                                bias=stats[:, 4 * T + t:4 * T + t + 1]))

    @blk.sync
    def _(sync):
        for t in range(T):
            sync.wait_ge(ak.h, ak0 + 2 + t + 1)
            if t >= 1:
                sync.wait_ge(tx.h, tx0 + 128 * t)
            for c in range(8):
                tx.inc(sync.dma_start_transpose(
                    zf[:, c * S + t * 128:c * S + t * 128 + 128],
                    zt[:, (t % 2) * D + c * 128:(t % 2) * D + c * 128 + 128]), 16)


def build_l0_attn(debug=False):
    p = Prog("l0attn")
    nc = p.nc
    x_h = nc.dram_tensor("x", [S, D], F32, kind="ExternalInput").ap()
    wqkv_h = nc.dram_tensor("W_qkv", [D, 3 * D], F32, kind="ExternalInput").ap()
    bqkv_h = nc.dram_tensor("b_qkv", [3 * D], F32, kind="ExternalInput").ap()
    wout_h = nc.dram_tensor("W_out", [D, D], F32, kind="ExternalInput").ap()
    bout_h = nc.dram_tensor("b_out", [D], F32, kind="ExternalInput").ap()
    xout_h = nc.dram_tensor("xout", [S, D], F32, kind="ExternalOutput").ap()
    dbg_h = {}
    if debug:
        for nm, shape, dt in [("zf", [128, 8 * S], BF16), ("qf", [128, 8 * S], BF16),
                              ("kf", [128, 8 * S], BF16), ("vt", [128, T * D], BF16),
                              ("ctxn", [128, 8 * S], BF16), ("negMb", [128, 1], F32),
                              ("stats", [128, 8 * T], F32), ("es", [128, 16 * 512], BF16),
                              ("den", [1, 512], F32), ("rdb", [128, 512], F32),
                              ("bqk", [128, 32], F32)]:
            dbg_h[nm] = nc.dram_tensor("dbg_" + nm, shape, dt, kind="ExternalOutput").ap()

    with p.stack:
        p.decl_consts(L0ATTN_CONSTS)
        p.decl_fvals([1e-5])
        p.arena(192 * KB)
        x_sb = p.at("x_sb", 0, [128, T * D], F32)
        kf = p.at("kf_", 0, [128, 8 * S], BF16)
        vt = p.at("vt_", 32 * KB, [128, T * D], BF16)
        zf = p.at("zf_", 64 * KB, [128, 8 * S], BF16)
        es = p.at("es_", 64 * KB, [128, 16 * 512], BF16)
        sqq = p.at("sqq_", 80 * KB, [128, 2 * 512], BF16)
        t1 = p.at("t1_", 82 * KB, [128, 2 * 512], F32)
        rdb = p.at("rdb_", 86 * KB, [128, 512], F32)
        xt2 = p.at("xt2_", 88 * KB, [128, 2 * D], F32)
        wqkv = p.at("wqkv_", 96 * KB, [128, 8 * 3 * D], BF16)
        ctxn = p.at("ctxn_", 96 * KB, [128, 8 * S], BF16)
        wo = p.at("wo_", 128 * KB, [128, 8 * D], BF16)
        qf = p.at("qf_", 144 * KB, [128, 8 * S], BF16)
        boutb = p.at("boutb_", 144 * KB, [128, D], F32)
        zt = p.at("zt_", 176 * KB, [128, 2 * D], BF16)
        xo = p.at("xo_", 180 * KB, [128, 2 * D], F32)
        x2 = p.at("x2_", 0, [128, T * D], F32)

        stats = p.sb("stats", [128, 8 * T], F32)
        bqk_r = p.sb("bqk_r", [24, 128], F32)
        bqk = p.sb("bqk", [128, 32], F32)
        nrm = p.sb("nrm", [1, 16], F32)
        negMb = p.sb("negMb", [128, 1], F32)
        den_sb = p.sb("den_sb", [1, 512], F32)
        rd_row = p.sb("rd_row", [1, 512], F32)
        bout_bf = p.sb("bout_bf", [1, D], BF16)
        bvb = p.sb("bvb", [128, 8], BF16)
        bvwo = p.sb("bvwo", [1, D], BF16)

        ps_a = p.ps("ps_a", [128, 512])
        ps_b = p.ps("ps_b", [128, 512])
        ps_c = p.ps("ps_c", [128, 512])
        ps_d = p.ps("ps_d", [128, 512])
        ps_o = p.ps("ps_o", [128, 1024])
        ps_den = p.ps("ps_den", [1, 512])
        ps_bc = p.ps("ps_bc", [128, 512])

        ld = p.sem("ld")
        lg = p.sem("lg")
        tx = p.sem("tx")
        mm = p.sem("mm")
        ak = p.sem("ak")
        vk = p.sem("vk")
        st = p.sem("st")
        s2 = p.sem("s2")

        with nc.Block() as blk:
            @blk.sync
            def _(sync):
                p.emit_const_loads(sync, ld)
                for t in range(T):
                    ld.inc(sync.dma_start(out=x_sb[:, t * D:(t + 1) * D],
                                          in_=x_h[t * 128:(t + 1) * 128, :]), 16)
                ld.inc(sync.dma_start(out=bqk_r[:, :],
                                      in_=bqkv_h.rearrange("(a b) -> a b", b=128)), 16)

            @blk.gpsimd
            def _(g):
                for c in range(8):
                    lg.inc(g.dma_start(out=wqkv[:, c * 3072:(c + 1) * 3072],
                                       in_=wqkv_h[c * 128:(c + 1) * 128, :]), 16)
                lg.inc(g.dma_start(out=bout_bf[:, :], in_=bout_h[None, :]), 16)
        ld_all = ld.n
        lg_w = lg.n

        with nc.Block() as blk:
            emit_ln(p, blk, x_sb=x_sb, zt=zt, zf=zf, stats=stats,
                    ld=ld, ak=ak, vk=vk, tx=tx, wait_ld=ld_all, eps_ap=p.fc(1e-5))
        tx_zf = tx.n

        with nc.Block() as blk:
            mm0, ak0, vk0 = mm.n, ak.n, vk.n

            @blk.tensor
            def _(t):
                t.wait_ge(ld.h, ld_all)
                mm.inc(t.transpose(ps_bc[:, 0:24], bqk_r[:, :], p.const["identf"][0:24, 0:24]))
                t.wait_ge(tx.h, tx_zf)
                t.wait_ge(lg.h, lg_w)
                g = 0
                for proj in range(2):
                    for m_ in range(8):
                        for tc in range(4):
                            g += 1
                            psd = [ps_a, ps_b][g % 2]
                            if g > 2:
                                t.wait_ge(ak.h, ak0 + 2 + g - 2)
                            for c in range(8):
                                inst = t.matmul(
                                    psd[:, :],
                                    wqkv[:, c * 3072 + proj * 1024 + m_ * 128:
                                         c * 3072 + proj * 1024 + m_ * 128 + 128],
                                    zf[:, c * S + tc * 512:c * S + tc * 512 + 512],
                                    start=(c == 0), stop=(c == 7))
                            mm.inc(inst)
                for tt1 in range(1, T + 1):
                    if tt1 >= 2:
                        t.wait_ge(ak.h, ak0 + 66 + tt1 - 1)
                    for c in range(8):
                        for hh in range(2):
                            inst = t.matmul(
                                ps_o[:, hh * 512:(hh + 1) * 512],
                                zf[:, c * S + (tt1 - 1) * 128:c * S + (tt1 - 1) * 128 + 128],
                                wqkv[:, c * 3072 + 2048 + hh * 512:
                                     c * 3072 + 2048 + hh * 512 + 512],
                                start=(c == 0), stop=(c == 7))
                    mm.inc(inst)
                k = 0
                for which in range(2):
                    for nn_ in range(4):
                        ng = which * 4 + nn_ + 1
                        if ng >= 2:
                            t.wait_ge(vk.h, vk0 + ng - 1)
                        for c in range(8):
                            k += 1
                            t.wait_ge(ak.h, ak0 + 82 + k)
                            inst = t.matmul(ps_den[:, :], p.const["ones_col_b"][:, :],
                                            sqq[:, (c % 2) * 512:(c % 2) * 512 + 512],
                                            start=(c == 0), stop=(c == 7))
                            mm.inc(inst)
                t.wait_ge(vk.h, vk0 + 10)
                mm.inc(t.matmul(ps_bc[:, 100:101], p.const["ones_row_f"][:, :],
                                nrm[:, 8:9], start=True, stop=True))

            @blk.scalar
            def _(s):
                s.wait_ge(mm.h, mm0 + 1)
                ak.inc(s.activation(out=bqk[:, 8:32], in_=ps_bc[:, 0:24], func=AF.Copy))
                ak.inc(s.activation(out=bqk[:, 0:8], in_=ps_bc[:, 0:8], func=AF.Copy,
                                    scale=1.0 / 32.0))
                s.wait_ge(ak.h, ak0 + 2)
                g = 0
                for proj in range(2):
                    dst = qf if proj == 0 else kf
                    for m_ in range(8):
                        for tc in range(4):
                            g += 1
                            psd = [ps_a, ps_b][g % 2]
                            s.wait_ge(mm.h, mm0 + 1 + g)
                            ak.inc(s.activation(
                                out=dst[:, m_ * S + tc * 512:m_ * S + tc * 512 + 512],
                                in_=psd[:, :], func=AF.Identity,
                                scale=(1.0 / 32.0 if proj == 0 else 1.0),
                                bias=bqk[:, (0 if proj == 0 else 16) + m_:
                                         (0 if proj == 0 else 16) + m_ + 1]))
                for tt1 in range(1, T + 1):
                    s.wait_ge(mm.h, mm0 + 65 + tt1)
                    ak.inc(s.activation(out=vt[:, (tt1 - 1) * D:tt1 * D], in_=ps_o[:, :],
                                        func=AF.Copy))
                s.wait_ge(ak.h, ak0 + 82)
                k = 0
                for which in range(2):
                    src = qf if which == 0 else kf
                    for nn_ in range(4):
                        for c in range(8):
                            k += 1
                            if k > 2:
                                s.wait_ge(mm.h, mm0 + 81 + k - 2)
                            ak.inc(s.activation(
                                out=sqq[:, (c % 2) * 512:(c % 2) * 512 + 512],
                                in_=src[:, c * S + nn_ * 512:c * S + nn_ * 512 + 512],
                                func=AF.Square))
                s.wait_ge(vk.h, vk0 + 9)
                ak.inc(s.activation(out=nrm[:, 9:10], in_=nrm[:, 10:11], func=AF.Sqrt))
                ak.inc(s.activation(out=nrm[:, 11:12], in_=nrm[:, 12:13], func=AF.Sqrt))
                s.wait_ge(mm.h, mm0 + 146)
                ak.inc(s.activation(out=negMb[:, :], in_=ps_bc[:, 100:101], func=AF.Copy,
                                    scale=-1.0))

            @blk.vector
            def _(v):
                for ng in range(1, 9):
                    v.wait_ge(mm.h, mm0 + 81 + 8 * ng)
                    vk.inc(v.reduce_max(nrm[:, ng - 1:ng], ps_den[:, :], axis=AX.X))
                v.wait_ge(vk.h, vk0 + 8)
                v.reduce_max(nrm[:, 10:11], nrm[:, 0:4], axis=AX.X)
                vk.inc(v.reduce_max(nrm[:, 12:13], nrm[:, 4:8], axis=AX.X))
                v.wait_ge(ak.h, ak0 + 148)
                vk.inc(v.tensor_mul(nrm[:, 8:9], nrm[:, 9:10], nrm[:, 11:12]))
        ak_M = ak.n

        with nc.Block() as blk:
            mm0, ak0, vk0 = mm.n, ak.n, vk.n
            NT, NA, NV = 41, 26, 9

            @blk.gpsimd
            def _(g):
                g.wait_ge(mm.h, mm0)
                for c in range(8):
                    lg.inc(g.dma_start(out=wo[:, c * D:(c + 1) * D],
                                       in_=wout_h[c * 128:(c + 1) * 128, :]), 16)

            @blk.tensor
            def _(t):
                t.wait_ge(ak.h, ak_M)
                for qc in range(4):
                    mmB, akB, vkB = mm0 + NT * qc, ak0 + NA * qc, vk0 + NV * qc
                    q0 = qc * 512
                    for kt in range(16):
                        psd = [ps_a, ps_b][kt % 2]
                        if kt >= 2:
                            t.wait_ge(ak.h, akB + kt - 1)
                        elif qc > 0:
                            t.wait_ge(ak.h, akB - 10)
                        for c in range(8):
                            inst = t.matmul(
                                psd[:, :],
                                kf[:, c * S + kt * 128:c * S + kt * 128 + 128],
                                qf[:, c * S + q0:c * S + q0 + 512],
                                start=(c == 0), stop=(c == 7))
                        mm.inc(inst)
                    for kt in range(16):
                        t.wait_ge(ak.h, akB + kt + 1)
                        mm.inc(t.matmul(ps_den[:, :], p.const["ones_col_b"][:, :],
                                        es[:, kt * 512:(kt + 1) * 512],
                                        start=(kt == 0), stop=(kt == 15)))
                    t.wait_ge(vk.h, vkB + 1)
                    mm.inc(t.matmul(ps_bc[:, :], p.const["ones_row_f"][:, :],
                                    rd_row[:, :], start=True, stop=True))
                    for m_ in range(8):
                        psd = [ps_c, ps_d][m_ % 2]
                        if m_ >= 2:
                            t.wait_ge(vk.h, vkB + 1 + m_ - 1)
                        elif qc > 0:
                            t.wait_ge(vk.h, vkB)
                        for kt in range(16):
                            inst = t.matmul(
                                psd[:, :],
                                vt[:, kt * D + m_ * 128:kt * D + m_ * 128 + 128],
                                es[:, kt * 512:(kt + 1) * 512],
                                start=(kt == 0), stop=(kt == 15))
                        mm.inc(inst)

            @blk.scalar
            def _(s):
                for qc in range(4):
                    mmB, akB, vkB = mm0 + NT * qc, ak0 + NA * qc, vk0 + NV * qc
                    for kt in range(16):
                        psd = [ps_a, ps_b][kt % 2]
                        s.wait_ge(mm.h, mmB + kt + 1)
                        ak.inc(s.activation(out=es[:, kt * 512:(kt + 1) * 512],
                                            in_=psd[:, :], func=AF.Exp,
                                            bias=negMb[:, :]))
                    s.wait_ge(mm.h, mmB + 32)
                    ak.inc(s.activation(out=den_sb[:, :], in_=ps_den[:, :], func=AF.Copy))
                    s.wait_ge(mm.h, mmB + 33)
                    if qc > 0:
                        s.wait_ge(vk.h, vkB)
                    ak.inc(s.activation(out=rdb[:, :], in_=ps_bc[:, :], func=AF.Copy))
                    for m_ in range(8):
                        s.wait_ge(vk.h, vkB + 1 + m_ + 1)
                        ak.inc(s.activation(
                            out=ctxn[:, m_ * S + qc * 512:m_ * S + qc * 512 + 512],
                            in_=t1[:, (m_ % 2) * 512:(m_ % 2) * 512 + 512],
                            func=AF.Copy))

            @blk.vector
            def _(v):
                for qc in range(4):
                    mmB, akB, vkB = mm0 + NT * qc, ak0 + NA * qc, vk0 + NV * qc
                    v.wait_ge(ak.h, akB + 17)
                    vk.inc(v.reciprocal(rd_row[:, :], den_sb[:, :]))
                    for m_ in range(8):
                        psd = [ps_c, ps_d][m_ % 2]
                        v.wait_ge(mm.h, mmB + 33 + m_ + 1)
                        v.wait_ge(ak.h, akB + 18)
                        if m_ >= 2:
                            v.wait_ge(ak.h, akB + 18 + m_ - 1)
                        vk.inc(v.tensor_mul(t1[:, (m_ % 2) * 512:(m_ % 2) * 512 + 512],
                                            psd[:, :], rdb[:, :]))
        ak_p3 = ak.n
        vk_p3 = vk.n

        lg_all = lg.n
        with nc.Block() as blk:
            mm0, ak0, vk0, ld0, st0 = mm.n, ak.n, vk.n, ld.n, st.n

            @blk.sync
            def _(sync):
                sync.wait_ge(ak.h, ak_p3)
                for tt in range(T):
                    ld.inc(sync.dma_start(out=x2[:, tt * D:(tt + 1) * D],
                                          in_=x_h[tt * 128:(tt + 1) * 128, :]), 16)

            @blk.scalar
            def _(s):
                ak.inc(s.activation(out=bvb[:, :], in_=bqk[:, 24:32], func=AF.Copy))
                s.wait_ge(mm.h, mm0 + 1)
                ak.inc(s.activation(out=bvwo[:, 0:512], in_=ps_den[:, :], func=AF.Copy))
                s.wait_ge(mm.h, mm0 + 2)
                ak.inc(s.activation(out=bvwo[:, 512:1024], in_=ps_den[:, :], func=AF.Copy))
                s.wait_ge(mm.h, mm0 + 3)
                ak.inc(s.activation(out=boutb[:, 0:512], in_=ps_c[:, :], func=AF.Copy))
                s.wait_ge(mm.h, mm0 + 4)
                ak.inc(s.activation(out=boutb[:, 512:1024], in_=ps_d[:, :], func=AF.Copy))

            @blk.tensor
            def _(t):
                t.wait_ge(ak.h, ak0 + 1)
                t.wait_ge(lg.h, lg_all)
                t.wait_ge(vk.h, vk_p3)
                for hh in range(2):
                    if hh == 1:
                        t.wait_ge(ak.h, ak0 + 2)
                    for m_ in range(8):
                        inst = t.matmul(ps_den[:, :], bvb[:, m_:m_ + 1],
                                        wo[:, m_ * D + hh * 512:m_ * D + hh * 512 + 512],
                                        start=(m_ == 0), stop=(m_ == 7))
                    mm.inc(inst)
                for hh, psd in ((0, ps_c), (1, ps_d)):
                    t.wait_ge(ak.h, ak0 + 3)
                    t.matmul(psd[:, :], p.const["ones_row_b"][:, :],
                             bout_bf[:, hh * 512:hh * 512 + 512], start=True, stop=False)
                    inst = t.matmul(psd[:, :], p.const["ones_row_b"][:, :],
                                    bvwo[:, hh * 512:hh * 512 + 512], start=False, stop=True)
                    mm.inc(inst)
                for tt1 in range(1, T + 1):
                    if tt1 >= 2:
                        t.wait_ge(vk.h, vk0 + 2 * (tt1 - 1) - 1)
                    for m_ in range(8):
                        for hh in range(2):
                            inst = t.matmul(
                                ps_o[:, hh * 512:(hh + 1) * 512],
                                ctxn[:, m_ * S + (tt1 - 1) * 128:m_ * S + (tt1 - 1) * 128 + 128],
                                wo[:, m_ * D + hh * 512:m_ * D + hh * 512 + 512],
                                start=(m_ == 0), stop=(m_ == 7))
                    mm.inc(inst)

            @blk.vector
            def _(v):
                v.wait_ge(ak.h, ak0 + 5)
                v.wait_ge(ld.h, ld0 + 16 * T)
                for tt in range(T):
                    v.wait_ge(mm.h, mm0 + 4 + tt + 1)
                    if tt >= 2:
                        psem = [st, s2][tt % 2]
                        v.wait_ge(psem.h, 16 * ((tt - 2) // 2 + 1))
                    xoh = xo[:, (tt % 2) * D:(tt % 2) * D + D]
                    vk.inc(v.tensor_add(xoh, ps_o[:, :], boutb[:, :]))
                    v.wait_ge(vk.h, vk0 + 2 * tt + 1)
                    vk.inc(v.tensor_add(xoh, xoh, x2[:, tt * D:(tt + 1) * D]))

            @blk.gpsimd
            def _(g):
                for tt in range(T):
                    g.wait_ge(vk.h, vk0 + 2 * (tt + 1))
                    psem = [st, s2][tt % 2]
                    psem.inc(g.dma_start(out=xout_h[tt * 128:(tt + 1) * 128, :],
                                         in_=xo[:, (tt % 2) * D:(tt % 2) * D + D]), 16)
                g.wait_ge(st.h, st.n)
                g.wait_ge(s2.h, s2.n)
    return p



def build_l1_attn(debug=False):
    p = Prog("l1attn")
    nc = p.nc
    x_h = nc.dram_tensor("x", [S, D], F32, kind="ExternalInput").ap()
    wqkv_h = nc.dram_tensor("WqkvT", [D, 3 * D], F32, kind="ExternalInput").ap()
    bqkv_h = nc.dram_tensor("b_qkv", [3 * D], F32, kind="ExternalInput").ap()
    wout_h = nc.dram_tensor("WoT", [D, D], F32, kind="ExternalInput").ap()
    bout_h = nc.dram_tensor("b_out", [D], F32, kind="ExternalInput").ap()
    dd_h = nc.dram_tensor("dd", [1, 1], F32, kind="ExternalInput").ap()
    xout_h = nc.dram_tensor("xout", [S, D], F32, kind="ExternalOutput").ap()
    dbg_h = {}
    if debug:
        for nm, shape, dt in [("qf", [128, 8 * S], BF16), ("kf", [128, 8 * S], BF16),
                              ("vt", [128, T * D], BF16), ("ctxn", [128, 8 * S], BF16),
                              ("negMb", [128, 16], F32), ("es", [128, 16 * 512], BF16),
                              ("decay", [128, 4096], BF16)]:
            dbg_h[nm] = nc.dram_tensor("dbg_" + nm, shape, dt, kind="ExternalOutput").ap()

    with p.stack:
        p.decl_consts(["identf", "ones_col_b", "ones_row_b", "ones_row_f",
                       "dist2d", "hsel", "ones_sq_b"])
        p.decl_fvals([1e-5])
        p.arena(188 * KB)
        x_sb = p.at("x_sb", 0, [128, T * D], F32)
        kf = p.at("kf_", 0, [128, 8 * S], BF16)
        vt = p.at("vt_", 32 * KB, [128, T * D], BF16)
        zf = p.at("zf_", 64 * KB, [128, 8 * S], BF16)
        ctxn = p.at("ctxn_", 64 * KB, [128, 8 * S], BF16)
        wqkv = p.at("wqkv_", 96 * KB, [128, 8 * 3 * D], BF16)
        es = p.at("es_", 96 * KB, [128, 16 * 512], BF16)
        wo = p.at("wo_", 112 * KB, [128, 8 * D], BF16)
        sd = p.at("sd_", 128 * KB, [128, 4 * 512], F32)
        t1 = p.at("t1_", 138 * KB, [128, 512], F32)
        rdb = p.at("rdb_", 140 * KB, [128, 512], F32)
        sqq = p.at("sqq_", 136 * KB, [128, 2 * 512], BF16)
        qf = p.at("qf_", 144 * KB, [128, 8 * S], BF16)
        xt2 = p.at("xt2_", 144 * KB, [128, 2 * D], F32)
        xo = p.at("xo_", 152 * KB, [128, 2 * D], F32)
        boutb = p.at("boutb_", 160 * KB, [128, D], F32)
        decay2d = p.at("decay2d_", 176 * KB, [128, 4096], BF16)
        zt = p.at("zt_", 184 * KB, [128, 2 * D], BF16)
        x2 = p.at("x2_", 0, [128, T * D], F32)

        stats = p.sb("stats", [128, 8 * T], F32)
        bqk_r = p.sb("bqk_r", [24, 128], F32)
        bqk = p.sb("bqk", [128, 32], F32)
        nrm = p.sb("nrm", [16, 24], F32)
        negMb = p.sb("negMb", [128, 16], F32)
        den_sb = p.sb("den_sb", [1, 512], F32)
        rd_row = p.sb("rd_row", [1, 512], F32)
        bout_bf = p.sb("bout_bf", [1, D], BF16)
        bvb = p.sb("bvb", [128, 8], BF16)
        bvwo = p.sb("bvwo", [1, D], BF16)
        mrow = p.sb("mrow", [1, 16], F32)
        dd_sb = p.sb("dd_sb", [1, 1], F32)
        aa = p.sb("aa", [1, 1], F32)
        nab = p.sb("nab", [128, 1], F32)

        ps_a = p.ps("ps_a", [128, 512])
        ps_b = p.ps("ps_b", [128, 512])
        ps_c = p.ps("ps_c", [128, 512])
        ps_d = p.ps("ps_d", [128, 512])
        ps_o = p.ps("ps_o", [128, 1024])
        ps_den = p.ps("ps_den", [128, 512])
        ps_bc = p.ps("ps_bc", [128, 512])

        ld = p.sem("ld")
        lg = p.sem("lg")
        tx = p.sem("tx")
        mm = p.sem("mm")
        ak = p.sem("ak")
        vk = p.sem("vk")
        st = p.sem("st")
        s2 = p.sem("s2")
        lgd = p.sem("lgd")

        with nc.Block() as blk:
            @blk.sync
            def _(sync):
                p.emit_const_loads(sync, ld)
                for t in range(T):
                    ld.inc(sync.dma_start(out=x_sb[:, t * D:(t + 1) * D],
                                          in_=x_h[t * 128:(t + 1) * 128, :]), 16)
                ld.inc(sync.dma_start(out=bqk_r[:, :],
                                      in_=bqkv_h.rearrange("(a b) -> a b", b=128)), 16)
                ld.inc(sync.dma_start(out=dd_sb[:, :], in_=dd_h[:, :]), 16)

            @blk.gpsimd
            def _(g):
                for c in range(8):
                    lg.inc(g.dma_start(out=wqkv[:, c * 3072:(c + 1) * 3072],
                                       in_=wqkv_h[c * 128:(c + 1) * 128, :]), 16)
                lg.inc(g.dma_start(out=bout_bf[:, :], in_=bout_h[None, :]), 16)
        ld_all = ld.n
        lg_w = lg.n

        with nc.Block() as blk:
            emit_ln(p, blk, x_sb=x_sb, zt=zt, zf=zf, stats=stats,
                    ld=ld, ak=ak, vk=vk, tx=tx, wait_ld=ld_all, eps_ap=p.fc(1e-5))
            a_abs = ak.n + 1
            m_bc = mm.n + 1
            v_nab = vk.n + 1

            @blk.scalar
            def _(s):
                ak.inc(s.activation(out=aa[:, :], in_=dd_sb[:, :], func=AF.Abs))

            @blk.tensor
            def _(t):
                t.wait_ge(ak.h, a_abs)
                mm.inc(t.matmul(ps_bc[:, 0:1], p.const["ones_row_f"][:, :], aa[:, :],
                                start=True, stop=True))

            @blk.vector
            def _(v):
                v.wait_ge(mm.h, m_bc)
                vk.inc(v.tensor_scalar(nab[:, :], ps_bc[:, 0:1], -1.0, None, op0=ALU.mult))

            @blk.scalar
            def _(s):
                s.wait_ge(vk.h, v_nab)
                ak.inc(s.activation(out=decay2d[:, :], in_=p.const["dist2d"][:, :],
                                    func=AF.Exp, scale=nab[:, :]))
            a_dv = ak.n
        tx_zf = tx.n

        with nc.Block() as blk:
            mm0, ak0, vk0 = mm.n, ak.n, vk.n

            @blk.tensor
            def _(t):
                t.wait_ge(ld.h, ld_all)
                t.wait_ge(vk.h, v_nab)
                mm.inc(t.transpose(ps_bc[:, 0:24], bqk_r[:, :], p.const["identf"][0:24, 0:24]))
                t.wait_ge(tx.h, tx_zf)
                t.wait_ge(lg.h, lg_w)
                g = 0
                for proj in range(2):
                    for m_ in range(8):
                        for tc in range(4):
                            g += 1
                            psd = [ps_a, ps_b][g % 2]
                            if g > 2:
                                t.wait_ge(ak.h, ak0 + 2 + g - 2)
                            for c in range(8):
                                inst = t.matmul(
                                    psd[:, :],
                                    wqkv[:, c * 3072 + proj * 1024 + m_ * 128:
                                         c * 3072 + proj * 1024 + m_ * 128 + 128],
                                    zf[:, c * S + tc * 512:c * S + tc * 512 + 512],
                                    start=(c == 0), stop=(c == 7))
                            mm.inc(inst)
                for tt1 in range(1, T + 1):
                    if tt1 >= 2:
                        t.wait_ge(ak.h, ak0 + 66 + tt1 - 1)
                    for c in range(8):
                        for hh in range(2):
                            inst = t.matmul(
                                ps_o[:, hh * 512:(hh + 1) * 512],
                                zf[:, c * S + (tt1 - 1) * 128:c * S + (tt1 - 1) * 128 + 128],
                                wqkv[:, c * 3072 + 2048 + hh * 512:
                                     c * 3072 + 2048 + hh * 512 + 512],
                                start=(c == 0), stop=(c == 7))
                    mm.inc(inst)
                k = 0
                for which in range(2):
                    for nn_ in range(4):
                        ng = which * 4 + nn_ + 1
                        if ng >= 2:
                            t.wait_ge(vk.h, vk0 + ng - 1)
                        for c in range(8):
                            k += 1
                            t.wait_ge(ak.h, ak0 + 82 + k)
                            inst = t.matmul(ps_den[0:16, :], p.const["hsel"][:, 16 * c:16 * c + 16],
                                            sqq[:, (c % 2) * 512:(c % 2) * 512 + 512],
                                            start=(c == 0), stop=(c == 7))
                            mm.inc(inst)
                t.wait_ge(vk.h, vk0 + 11)
                mm.inc(t.transpose(ps_bc[0:1, 32:48], nrm[:, 12:13],
                                   p.const["identf"][0:16, 0:16]))
                t.wait_ge(ak.h, ak0 + 149)
                mm.inc(t.matmul(ps_bc[:, 64:80], p.const["ones_row_f"][:, :],
                                mrow[:, :], start=True, stop=True))

            @blk.scalar
            def _(s):
                s.wait_ge(mm.h, mm0 + 1)
                ak.inc(s.activation(out=bqk[:, 8:32], in_=ps_bc[:, 0:24], func=AF.Copy))
                ak.inc(s.activation(out=bqk[:, 0:8], in_=ps_bc[:, 0:8], func=AF.Copy,
                                    scale=1.0 / 8.0))
                s.wait_ge(ak.h, ak0 + 2)
                g = 0
                for proj in range(2):
                    dst = qf if proj == 0 else kf
                    for m_ in range(8):
                        for tc in range(4):
                            g += 1
                            psd = [ps_a, ps_b][g % 2]
                            s.wait_ge(mm.h, mm0 + 1 + g)
                            ak.inc(s.activation(
                                out=dst[:, m_ * S + tc * 512:m_ * S + tc * 512 + 512],
                                in_=psd[:, :], func=AF.Identity,
                                scale=(1.0 / 8.0 if proj == 0 else 1.0),
                                bias=bqk[:, (0 if proj == 0 else 16) + m_:
                                         (0 if proj == 0 else 16) + m_ + 1]))
                for tt1 in range(1, T + 1):
                    s.wait_ge(mm.h, mm0 + 65 + tt1)
                    ak.inc(s.activation(out=vt[:, (tt1 - 1) * D:tt1 * D], in_=ps_o[:, :],
                                        func=AF.Copy))
                s.wait_ge(ak.h, ak0 + 82)
                k = 0
                for which in range(2):
                    src = qf if which == 0 else kf
                    for nn_ in range(4):
                        for c in range(8):
                            k += 1
                            if k > 2:
                                s.wait_ge(mm.h, mm0 + 81 + k - 2)
                            ak.inc(s.activation(
                                out=sqq[:, (c % 2) * 512:(c % 2) * 512 + 512],
                                in_=src[:, c * S + nn_ * 512:c * S + nn_ * 512 + 512],
                                func=AF.Square))
                s.wait_ge(vk.h, vk0 + 10)
                ak.inc(s.activation(out=nrm[:, 16:17], in_=nrm[:, 8:9], func=AF.Sqrt))
                ak.inc(s.activation(out=nrm[:, 17:18], in_=nrm[:, 9:10], func=AF.Sqrt))
                s.wait_ge(mm.h, mm0 + 146)
                ak.inc(s.activation(out=mrow[:, :], in_=ps_bc[0:1, 32:48], func=AF.Copy))
                s.wait_ge(mm.h, mm0 + 147)
                ak.inc(s.activation(out=negMb[:, :], in_=ps_bc[:, 64:80], func=AF.Copy,
                                    scale=-1.0))

            @blk.vector
            def _(v):
                for ng in range(1, 9):
                    v.wait_ge(mm.h, mm0 + 81 + 8 * ng)
                    vk.inc(v.reduce_max(nrm[:, ng - 1:ng], ps_den[0:16, :], axis=AX.X))
                v.wait_ge(vk.h, vk0 + 8)
                vk.inc(v.reduce_max(nrm[:, 8:9], nrm[:, 0:4], axis=AX.X))
                vk.inc(v.reduce_max(nrm[:, 9:10], nrm[:, 4:8], axis=AX.X))
                v.wait_ge(ak.h, ak0 + 148)
                vk.inc(v.tensor_mul(nrm[:, 12:13], nrm[:, 16:17], nrm[:, 17:18]))
        ak_M = ak.n
        mm_p2 = mm.n

        with nc.Block() as blk:
            mm0, ak0, vk0 = mm.n, ak.n, vk.n
            NT, NA, NV = 33, 17, 18

            @blk.gpsimd
            def _(g):
                g.wait_ge(mm.h, mm0)
                for c in range(8):
                    lg.inc(g.dma_start(out=wo[:, c * D:(c + 1) * D],
                                       in_=wout_h[c * 128:(c + 1) * 128, :]), 16)

            @blk.tensor
            def _(t):
                t.wait_ge(ak.h, ak_M)
                for qc in range(4):
                    q0 = qc * 512
                    for h in range(16):
                        it = qc * 16 + h
                        mmB, akB, vkB = mm0 + NT * it, ak0 + NA * it, vk0 + NV * it
                        ch = h // 2
                        r0 = 64 * (h % 2)
                        for kt in range(16):
                            psd = [ps_a, ps_b][kt % 2]
                            if kt >= 2:
                                t.wait_ge(vk.h, vkB + kt - 1)
                            elif it > 0:
                                t.wait_ge(vk.h, vkB + kt - 3)
                            inst = t.matmul(
                                psd[:, :],
                                kf[r0:r0 + 64, ch * S + kt * 128:ch * S + kt * 128 + 128],
                                qf[r0:r0 + 64, ch * S + q0:ch * S + q0 + 512],
                                start=True, stop=True)
                            mm.inc(inst)
                        for kt in range(16):
                            t.wait_ge(ak.h, akB + kt + 1)
                            mm.inc(t.matmul(ps_den[:, :], p.const["ones_sq_b"][:, :],
                                            es[:, kt * 512:(kt + 1) * 512],
                                            start=(kt == 0), stop=(kt == 15)))
                        psd = [ps_c, ps_d][it % 2]
                        if it >= 2:
                            t.wait_ge(vk.h, vkB - NV)
                        for kt in range(16):
                            inst = t.matmul(
                                psd[0:64, :],
                                vt[:, kt * D + 64 * h:kt * D + 64 * h + 64],
                                es[:, kt * 512:(kt + 1) * 512],
                                start=(kt == 0), stop=(kt == 15))
                        mm.inc(inst)

            @blk.vector
            def _(v):
                v.wait_ge(ak.h, a_dv)
                for qc in range(4):
                    for h in range(16):
                        it = qc * 16 + h
                        mmB, akB, vkB = mm0 + NT * it, ak0 + NA * it, vk0 + NV * it
                        for kt in range(16):
                            psd = [ps_a, ps_b][kt % 2]
                            v.wait_ge(mm.h, mmB + kt + 1)
                            if kt >= 4:
                                v.wait_ge(ak.h, akB + kt - 3)
                            elif it > 0:
                                v.wait_ge(ak.h, akB - NA + kt + 13)
                            off = 2048 + 128 * (4 * qc - kt)
                            vk.inc(v.tensor_mul(
                                sd[:, (kt % 4) * 512:(kt % 4) * 512 + 512],
                                psd[:, :], decay2d[:, off:off + 512]))
                        v.wait_ge(mm.h, mmB + 32)
                        vk.inc(v.reciprocal(rdb[:, :], ps_den[:, :]))
                        v.wait_ge(mm.h, mmB + 33)
                        v.wait_ge(vk.h, vkB + 17)
                        if it >= 1:
                            v.wait_ge(ak.h, akB)
                        psd = [ps_c, ps_d][it % 2]
                        vk.inc(v.tensor_mul(t1[0:64, :], psd[0:64, :], rdb[0:64, :]))

            @blk.scalar
            def _(s):
                for qc in range(4):
                    for h in range(16):
                        it = qc * 16 + h
                        mmB, akB, vkB = mm0 + NT * it, ak0 + NA * it, vk0 + NV * it
                        ch = h // 2
                        r0 = 64 * (h % 2)
                        for kt in range(16):
                            s.wait_ge(vk.h, vkB + kt + 1)
                            ak.inc(s.activation(out=es[:, kt * 512:(kt + 1) * 512],
                                                in_=sd[:, (kt % 4) * 512:(kt % 4) * 512 + 512],
                                                func=AF.Exp, bias=negMb[:, h:h + 1]))
                        s.wait_ge(vk.h, vkB + 18)
                        ak.inc(s.activation(
                            out=ctxn[r0:r0 + 64, ch * S + qc * 512:ch * S + qc * 512 + 512],
                            in_=t1[0:64, :], func=AF.Copy))
        ak_p3 = ak.n
        vk_p3 = vk.n

        lg_all = lg.n
        with nc.Block() as blk:
            mm0, ak0, vk0, ld0, st0 = mm.n, ak.n, vk.n, ld.n, st.n

            @blk.sync
            def _(sync):
                sync.wait_ge(ak.h, ak_p3)
                for tt in range(T):
                    ld.inc(sync.dma_start(out=x2[:, tt * D:(tt + 1) * D],
                                          in_=x_h[tt * 128:(tt + 1) * 128, :]), 16)

            @blk.scalar
            def _(s):
                s.wait_ge(ak.h, ak_p3)
                ak.inc(s.activation(out=bvb[:, :], in_=bqk[:, 24:32], func=AF.Copy))
                s.wait_ge(mm.h, mm0 + 1)
                ak.inc(s.activation(out=bvwo[:, 0:512], in_=ps_den[0:1, :], func=AF.Copy))
                s.wait_ge(mm.h, mm0 + 2)
                ak.inc(s.activation(out=bvwo[:, 512:1024], in_=ps_den[0:1, :], func=AF.Copy))
                s.wait_ge(mm.h, mm0 + 3)
                ak.inc(s.activation(out=boutb[:, 0:512], in_=ps_c[:, :], func=AF.Copy))
                s.wait_ge(mm.h, mm0 + 4)
                ak.inc(s.activation(out=boutb[:, 512:1024], in_=ps_d[:, :], func=AF.Copy))

            @blk.tensor
            def _(t):
                t.wait_ge(ak.h, ak0 + 1)
                t.wait_ge(lg.h, lg_all)
                t.wait_ge(vk.h, vk_p3)
                for hh in range(2):
                    if hh == 1:
                        t.wait_ge(ak.h, ak0 + 2)
                    for m_ in range(8):
                        inst = t.matmul(ps_den[0:1, :], bvb[:, m_:m_ + 1],
                                        wo[:, m_ * D + hh * 512:m_ * D + hh * 512 + 512],
                                        start=(m_ == 0), stop=(m_ == 7))
                    mm.inc(inst)
                for hh, psd in ((0, ps_c), (1, ps_d)):
                    t.wait_ge(ak.h, ak0 + 3)
                    t.matmul(psd[:, :], p.const["ones_row_b"][:, :],
                             bout_bf[:, hh * 512:hh * 512 + 512], start=True, stop=False)
                    inst = t.matmul(psd[:, :], p.const["ones_row_b"][:, :],
                                    bvwo[:, hh * 512:hh * 512 + 512], start=False, stop=True)
                    mm.inc(inst)
                for tt1 in range(1, T + 1):
                    if tt1 >= 2:
                        t.wait_ge(vk.h, vk0 + 2 * (tt1 - 1) - 1)
                    for m_ in range(8):
                        for hh in range(2):
                            inst = t.matmul(
                                ps_o[:, hh * 512:(hh + 1) * 512],
                                ctxn[:, m_ * S + (tt1 - 1) * 128:m_ * S + (tt1 - 1) * 128 + 128],
                                wo[:, m_ * D + hh * 512:m_ * D + hh * 512 + 512],
                                start=(m_ == 0), stop=(m_ == 7))
                    mm.inc(inst)

            @blk.vector
            def _(v):
                v.wait_ge(ak.h, ak0 + 5)
                v.wait_ge(ld.h, ld0 + 16 * T)
                for tt in range(T):
                    v.wait_ge(mm.h, mm0 + 4 + tt + 1)
                    if tt >= 2:
                        psem = [st, s2][tt % 2]
                        v.wait_ge(psem.h, 16 * ((tt - 2) // 2 + 1))
                    xoh = xo[:, (tt % 2) * D:(tt % 2) * D + D]
                    vk.inc(v.tensor_add(xoh, ps_o[:, :], boutb[:, :]))
                    v.wait_ge(vk.h, vk0 + 2 * tt + 1)
                    vk.inc(v.tensor_add(xoh, xoh, x2[:, tt * D:(tt + 1) * D]))

            @blk.gpsimd
            def _(g):
                for tt in range(T):
                    g.wait_ge(vk.h, vk0 + 2 * (tt + 1))
                    psem = [st, s2][tt % 2]
                    psem.inc(g.dma_start(out=xout_h[tt * 128:(tt + 1) * 128, :],
                                         in_=xo[:, (tt % 2) * D:(tt % 2) * D + D]), 16)
                if dbg_h:
                    for nm, buf in [("qf", qf), ("kf", kf), ("vt", vt), ("ctxn", ctxn),
                                    ("negMb", negMb), ("es", es), ("decay", decay2d)]:
                        st.inc(g.dma_start(out=dbg_h[nm][:, :], in_=buf[:, :]), 16)
                g.wait_ge(st.h, st.n)
                g.wait_ge(s2.h, s2.n)
    return p




def build_l1_ffn(debug=False):
    p = Prog("l1ffn")
    nc = p.nc
    x_h = nc.dram_tensor("x", [S, D], F32, kind="ExternalInput").ap()
    g1_h = nc.dram_tensor("g1", [D], F32, kind="ExternalInput").ap()
    g2_h = nc.dram_tensor("g2", [D], F32, kind="ExternalInput").ap()
    wq_h = nc.dram_tensor("Wq", [D, D], F32, kind="ExternalInput").ap()
    wdkv_h = nc.dram_tensor("Wdkv", [D, 512], F32, kind="ExternalInput").ap()
    wuk_h = nc.dram_tensor("Wuk", [512, D], F32, kind="ExternalInput").ap()
    wuv_h = nc.dram_tensor("Wuv", [512, D], F32, kind="ExternalInput").ap()
    wo_h = nc.dram_tensor("Wo", [D, D], F32, kind="ExternalInput").ap()
    gw_h = nc.dram_tensor("gate_w", [D, 8], F32, kind="ExternalInput").ap()
    w1_h = nc.dram_tensor("w1", [8, D, 2048], F32, kind="ExternalInput").ap()
    w2_h = nc.dram_tensor("w2", [8, 2048, D], F32, kind="ExternalInput").ap()
    xout_h = nc.dram_tensor("xout", [S, D], F32, kind="ExternalOutput").ap()
    dbg_h = {}
    if debug:
        for nm, shape, dt in [("u_fm", [128, 8 * S], BF16),
                              ("ctxn", [128, 8 * S], BF16), ("negMb", [128, 16], F32),
                              ("t", [128, T * D], F32), ("u2", [128, 8 * S], BF16),
                              ("lgs", [128, 128], F32), ("cmb", [128, 128], F32),
                              ("kf", [128, 8 * S], BF16), ("vt", [128, T * D], BF16)]:
            dbg_h[nm] = nc.dram_tensor("dbg_" + nm, shape, dt, kind="ExternalOutput").ap()

    with p.stack:
        p.decl_consts(["identf", "ones_col_b", "ones_row_b", "ones_row_f",
                       "hsel", "trimask"])
        p.decl_fvals([1e-5, 1e-6])
        p.arena(190 * KB)
        u_fm = p.at("u_fm_", 0, [128, 8 * S], BF16)
        t_sb = p.at("t_sb_", 0, [128, T * D], F32)
        qf = p.at("qf_", 32 * KB, [128, 8 * S], BF16)
        kf = p.at("kf_", 64 * KB, [128, 8 * S], BF16)
        wob = p.at("wob_", 64 * KB, [128, 8 * D], BF16)
        w1b = p.at("w1b_", 64 * KB, [128, 8 * 2048], BF16)
        x3 = p.at("x3_", 64 * KB, [128, T * D], F32)
        vt = p.at("vt_", 96 * KB, [128, T * D], BF16)
        x1 = p.at("x1_", 96 * KB, [128, T * D], F32)
        w2b = p.at("w2b_", 96 * KB, [128, 16 * D], BF16)
        x2 = p.at("x2_", 80 * KB, [128, T * D], F32)
        cf = p.at("cf_", 128 * KB, [128, 4 * S], BF16)
        es = p.at("es_", 128 * KB, [128, 16 * 512], BF16)
        u2_fm = p.at("u2_fm_", 128 * KB, [128, 8 * S], BF16)
        ctxn = p.at("ctxn_", 144 * KB, [128, 8 * S], BF16)
        wbuf = p.at("wbuf_", 144 * KB, [128, 8 * D], BF16)
        hh = p.at("hh_", 160 * KB, [128, 16 * 512], BF16)
        zt = p.at("zt_", 176 * KB, [128, 2 * D], BF16)
        ut = p.at("ut_", 180 * KB, [128, 2 * D], BF16)
        zre = p.at("zre_", 176 * KB, [128, 2 * D], F32)
        xot = p.at("xot_", 176 * KB, [128, 2 * D], F32)
        sqq = p.at("sqq_", 184 * KB, [128, 2 * 512], BF16)
        t1 = p.at("t1_", 186 * KB, [128, 512], F32)
        rdb = p.at("rdb_", 188 * KB, [128, 512], F32)
        g1r = p.at("g1r_", 80 * KB, [1, D], F32)
        g1b = p.at("g1b_", 84 * KB, [128, D], F32)
        g2r = p.at("g2r_", 88 * KB, [1, D], F32)
        g2b = p.at("g2b_", 92 * KB, [128, D], F32)

        stats = p.sb("stats", [128, 12 * T], F32)
        nrm = p.sb("nrm", [16, 24], F32)
        negMb = p.sb("negMb", [128, 16], F32)
        den_sb = p.sb("den_sb", [1, 512], F32)
        rd_row = p.sb("rd_row", [1, 512], F32)
        mrow = p.sb("mrow", [1, 16], F32)
        m8 = p.sb("m8", [128, 16], F32)
        lgs = p.sb("lgs", [128, 128], F32)
        cmb = p.sb("cmb", [128, 128], F32)
        gwb = p.sb("gwb", [128, 64], BF16)

        ps_a = p.ps("ps_a", [128, 512])
        ps_b = p.ps("ps_b", [128, 512])
        ps_c = p.ps("ps_c", [128, 512])
        ps_d = p.ps("ps_d", [128, 512])
        ps_o = p.ps("ps_o", [128, 1024])
        ps_den = p.ps("ps_den", [16, 512])
        ps_bc = p.ps("ps_bc", [128, 512])

        ld = p.sem("ld")
        lg = p.sem("lg")
        tx = p.sem("tx")
        mm = p.sem("mm")
        ak = p.sem("ak")
        vk = p.sem("vk")
        st = p.sem("st")
        s2 = p.sem("s2")
        lw = p.sem("lw")

        with nc.Block() as blk:
            mm0, ak0, vk0 = mm.n, ak.n, vk.n

            @blk.sync
            def _(sync):
                p.emit_const_loads(sync, ld)
                ld.inc(sync.dma_start(out=g1r[:, :], in_=g1_h[None, :]), 16)
                for t in range(T):
                    ld.inc(sync.dma_start(out=x1[:, t * D:(t + 1) * D],
                                          in_=x_h[t * 128:(t + 1) * 128, :]), 16)
            ld_all = ld.n

            @blk.sync
            def _(sync):
                for t in range(T):
                    sync.wait_ge(vk.h, vk0 + 7 + t + 1)
                    if t >= 1:
                        sync.wait_ge(tx.h, 128 * t)
                    for c in range(8):
                        tx.inc(sync.dma_start_transpose(
                            u_fm[:, c * S + t * 128:c * S + t * 128 + 128],
                            ut[:, (t % 2) * D + c * 128:(t % 2) * D + c * 128 + 128]), 16)

            @blk.scalar
            def _(s):
                s.wait_ge(ld.h, ld_all)
                for t in range(T):
                    s.activation(out=zt[:, (t % 2) * D:(t % 2) * D + D],
                                 in_=x1[:, t * D:(t + 1) * D], func=AF.Square,
                                 accum_out=stats[:, T + t:T + t + 1])
                    inst = s.activation(out=zt[:, (t % 2) * D:(t % 2) * D + D],
                                        in_=x1[:, t * D:(t + 1) * D], func=AF.Copy,
                                        accum_out=stats[:, t:t + 1])
                ak.inc(inst)
                s.wait_ge(vk.h, vk0 + 3)
                ak.inc(s.activation(out=stats[:, 10 * T:11 * T], in_=stats[:, 7 * T:8 * T],
                                    func=AF.Sqrt, bias=p.fc(1e-5)))
                s.wait_ge(vk.h, vk0 + 6)
                ak.inc(s.activation(out=stats[:, 11 * T:12 * T], in_=stats[:, 11 * T:12 * T],
                                    func=AF.Sqrt, bias=p.fc(1e-6)))
                s.wait_ge(mm.h, mm0 + 1)
                ak.inc(s.activation(out=g1b[:, 0:512], in_=ps_bc[:, 0:512],
                                    func=AF.Copy))
                s.wait_ge(mm.h, mm0 + 2)
                ak.inc(s.activation(out=g1b[:, 512:1024], in_=ps_bc[:, 0:512],
                                    func=AF.Copy))
                s.wait_ge(vk.h, vk0 + 7)
                for t in range(T):
                    if t >= 2:
                        s.wait_ge(vk.h, vk0 + 7 + t - 1)
                    ak.inc(s.activation(out=zt[:, (t % 2) * D:(t % 2) * D + D],
                                        in_=x1[:, t * D:(t + 1) * D], func=AF.Identity,
                                        scale=stats[:, 3 * T + t:3 * T + t + 1],
                                        bias=stats[:, 4 * T + t:4 * T + t + 1]))

            @blk.vector
            def _(v):
                v.wait_ge(ak.h, ak0 + 1)
                vk.inc(v.tensor_scalar_mul(stats[:, 2 * T:3 * T], stats[:, 0:T],
                                           1.0 / D))
                v.wait_ge(vk.h, vk0 + 1)
                m = stats[:, 2 * T:3 * T]
                v.tensor_mul(stats[:, 10 * T:11 * T], m, m)
                vk.inc(v.tensor_scalar(stats[:, 11 * T:12 * T], stats[:, T:2 * T],
                                       1.0 / D, None, op0=ALU.mult))
                v.wait_ge(vk.h, vk0 + 2)
                vk.inc(v.tensor_sub(stats[:, 7 * T:8 * T], stats[:, 11 * T:12 * T],
                                    stats[:, 10 * T:11 * T]))
                v.wait_ge(ak.h, ak0 + 2)
                vk.inc(v.reciprocal(stats[:, 3 * T:4 * T], stats[:, 10 * T:11 * T]))
                v.wait_ge(vk.h, vk0 + 4)
                v.scalar_tensor_tensor(stats[:, 4 * T:5 * T], stats[:, 2 * T:3 * T],
                                       -1.0, stats[:, 3 * T:4 * T],
                                       op0=ALU.mult, op1=ALU.mult)
                vk.inc(v.tensor_mul(stats[:, 10 * T:11 * T], stats[:, 3 * T:4 * T],
                                    stats[:, 3 * T:4 * T]))
                v.wait_ge(vk.h, vk0 + 5)
                vk.inc(v.tensor_mul(stats[:, 11 * T:12 * T], stats[:, 7 * T:8 * T],
                                    stats[:, 10 * T:11 * T]))
                v.wait_ge(ak.h, ak0 + 3)
                vk.inc(v.reciprocal(stats[:, 6 * T:7 * T], stats[:, 11 * T:12 * T]))
                v.wait_ge(ak.h, ak0 + 5)
                for t in range(T):
                    v.wait_ge(ak.h, ak0 + 5 + t + 1)
                    if t >= 2:
                        v.wait_ge(tx.h, 128 * (t - 1))
                    vk.inc(v.scalar_tensor_tensor(
                        ut[:, (t % 2) * D:(t % 2) * D + D],
                        zt[:, (t % 2) * D:(t % 2) * D + D],
                        stats[:, 6 * T + t:6 * T + t + 1],
                        g1b[:, :], op0=ALU.mult, op1=ALU.mult))

            @blk.tensor
            def _(t):
                t.wait_ge(ld.h, ld_all)
                mm.inc(t.matmul(ps_bc[:, 0:512], p.const["ones_row_f"][:, :],
                                g1r[:, 0:512], start=True, stop=True))
                t.wait_ge(ak.h, ak0 + 4)
                mm.inc(t.matmul(ps_bc[:, 0:512], p.const["ones_row_f"][:, :],
                                g1r[:, 512:1024], start=True, stop=True))
        tx_u = tx.n
        ak_pa, vk_pa, mm_pa = ak.n, vk.n, mm.n

        with nc.Block() as blk:
            mm0, ak0, vk0, lg0 = mm.n, ak.n, vk.n, lg.n

            @blk.gpsimd
            def _(g):
                for c in range(8):
                    lg.inc(g.dma_start(out=wbuf[:, c * D:(c + 1) * D],
                                       in_=wq_h[c * 128:(c + 1) * 128, :]), 16)
                g.wait_ge(mm.h, mm0 + 32)
                for c in range(8):
                    lg.inc(g.dma_start(out=wbuf[:, c * 512:(c + 1) * 512],
                                       in_=wdkv_h[c * 128:(c + 1) * 128, :]), 16)
                g.wait_ge(mm.h, mm0 + 48)
                for c in range(4):
                    lg.inc(g.dma_start(out=wbuf[:, c * D:(c + 1) * D],
                                       in_=wuk_h[c * 128:(c + 1) * 128, :]), 16)
                for c in range(4):
                    lg.inc(g.dma_start(out=wbuf[:, 4 * D + c * D:4 * D + (c + 1) * D],
                                       in_=wuv_h[c * 128:(c + 1) * 128, :]), 16)

            @blk.tensor
            def _(t):
                t.wait_ge(lg.h, lg0 + 128)
                g = 0
                for m_ in range(8):
                    for tc in range(4):
                        g += 1
                        psd = [ps_a, ps_b][g % 2]
                        if g > 2:
                            t.wait_ge(ak.h, ak0 + g - 2)
                        if m_ == 0:
                            t.wait_ge(tx.h, 128 * (4 * tc + 4))
                        for c in range(8):
                            inst = t.matmul(
                                psd[:, :], wbuf[:, c * D + m_ * 128:c * D + m_ * 128 + 128],
                                u_fm[:, c * S + tc * 512:c * S + tc * 512 + 512],
                                start=(c == 0), stop=(c == 7))
                        mm.inc(inst)
                t.wait_ge(lg.h, lg0 + 256)
                for m_ in range(4):
                    for tc in range(4):
                        g += 1
                        psd = [ps_a, ps_b][g % 2]
                        t.wait_ge(ak.h, ak0 + g - 2)
                        for c in range(8):
                            inst = t.matmul(
                                psd[:, :],
                                wbuf[:, c * 512 + m_ * 128:c * 512 + m_ * 128 + 128],
                                u_fm[:, c * S + tc * 512:c * S + tc * 512 + 512],
                                start=(c == 0), stop=(c == 7))
                        mm.inc(inst)
                t.wait_ge(lg.h, lg0 + 384)
                for m_ in range(8):
                    for tc in range(4):
                        g += 1
                        psd = [ps_a, ps_b][g % 2]
                        t.wait_ge(ak.h, ak0 + g - 2)
                        for c in range(4):
                            inst = t.matmul(
                                psd[:, :],
                                wbuf[:, c * D + m_ * 128:c * D + m_ * 128 + 128],
                                cf[:, c * S + tc * 512:c * S + tc * 512 + 512],
                                start=(c == 0), stop=(c == 3))
                        mm.inc(inst)
                for tt1 in range(1, T + 1):
                    if tt1 >= 2:
                        t.wait_ge(ak.h, ak0 + 80 + tt1 - 1)
                    for c in range(4):
                        for hhx in range(2):
                            inst = t.matmul(
                                ps_o[:, hhx * 512:(hhx + 1) * 512],
                                cf[:, c * S + (tt1 - 1) * 128:c * S + (tt1 - 1) * 128 + 128],
                                wbuf[:, 4 * D + c * D + hhx * 512:
                                     4 * D + c * D + hhx * 512 + 512],
                                start=(c == 0), stop=(c == 3))
                    mm.inc(inst)
                k = 0
                for which in range(2):
                    for nn_ in range(4):
                        ng = which * 4 + nn_ + 1
                        if ng >= 2:
                            t.wait_ge(vk.h, vk0 + ng - 1)
                        for c in range(8):
                            k += 1
                            t.wait_ge(ak.h, ak0 + 96 + k)
                            inst = t.matmul(ps_den[:, :],
                                            p.const["hsel"][:, 16 * c:16 * c + 16],
                                            sqq[:, (c % 2) * 512:(c % 2) * 512 + 512],
                                            start=(c == 0), stop=(c == 7))
                            mm.inc(inst)
                t.wait_ge(vk.h, vk0 + 11)
                mm.inc(t.transpose(ps_bc[0:1, 32:48], nrm[:, 12:13],
                                   p.const["identf"][0:16, 0:16]))
                t.wait_ge(ak.h, ak0 + 163)
                mm.inc(t.matmul(ps_bc[:, 64:80], p.const["ones_row_f"][:, :],
                                mrow[:, :], start=True, stop=True))

            @blk.scalar
            def _(s):
                g = 0
                for m_ in range(8):
                    for tc in range(4):
                        g += 1
                        psd = [ps_a, ps_b][g % 2]
                        s.wait_ge(mm.h, mm0 + g)
                        ak.inc(s.activation(
                            out=qf[:, m_ * S + tc * 512:m_ * S + tc * 512 + 512],
                            in_=psd[:, :], func=AF.Copy, scale=1.0 / 8.0))
                for m_ in range(4):
                    for tc in range(4):
                        g += 1
                        psd = [ps_a, ps_b][g % 2]
                        s.wait_ge(mm.h, mm0 + g)
                        ak.inc(s.activation(
                            out=cf[:, m_ * S + tc * 512:m_ * S + tc * 512 + 512],
                            in_=psd[:, :], func=AF.Copy))
                for m_ in range(8):
                    for tc in range(4):
                        g += 1
                        psd = [ps_a, ps_b][g % 2]
                        s.wait_ge(mm.h, mm0 + g)
                        ak.inc(s.activation(
                            out=kf[:, m_ * S + tc * 512:m_ * S + tc * 512 + 512],
                            in_=psd[:, :], func=AF.Copy))
                for tt1 in range(1, T + 1):
                    s.wait_ge(mm.h, mm0 + 80 + tt1)
                    ak.inc(s.activation(out=vt[:, (tt1 - 1) * D:tt1 * D], in_=ps_o[:, :],
                                        func=AF.Copy))
                s.wait_ge(ak.h, ak0 + 96)
                k = 0
                for which in range(2):
                    src = qf if which == 0 else kf
                    for nn_ in range(4):
                        for c in range(8):
                            k += 1
                            if k > 2:
                                s.wait_ge(mm.h, mm0 + 96 + k - 2)
                            ak.inc(s.activation(
                                out=sqq[:, (c % 2) * 512:(c % 2) * 512 + 512],
                                in_=src[:, c * S + nn_ * 512:c * S + nn_ * 512 + 512],
                                func=AF.Square))
                s.wait_ge(vk.h, vk0 + 10)
                ak.inc(s.activation(out=nrm[:, 16:17], in_=nrm[:, 8:9], func=AF.Sqrt))
                ak.inc(s.activation(out=nrm[:, 17:18], in_=nrm[:, 9:10], func=AF.Sqrt))
                s.wait_ge(mm.h, mm0 + 161)
                ak.inc(s.activation(out=mrow[:, :], in_=ps_bc[0:1, 32:48], func=AF.Copy))
                s.wait_ge(mm.h, mm0 + 162)
                ak.inc(s.activation(out=negMb[:, :], in_=ps_bc[:, 64:80], func=AF.Copy,
                                    scale=-1.0))

            @blk.vector
            def _(v):
                for ng in range(1, 9):
                    v.wait_ge(mm.h, mm0 + 96 + 8 * ng)
                    vk.inc(v.reduce_max(nrm[:, ng - 1:ng], ps_den[:, :], axis=AX.X))
                v.wait_ge(vk.h, vk0 + 8)
                vk.inc(v.reduce_max(nrm[:, 8:9], nrm[:, 0:4], axis=AX.X))
                vk.inc(v.reduce_max(nrm[:, 9:10], nrm[:, 4:8], axis=AX.X))
                v.wait_ge(ak.h, ak0 + 162)
                vk.inc(v.tensor_mul(nrm[:, 12:13], nrm[:, 16:17], nrm[:, 17:18]))
        ak_pc, vk_pc, mm_pc = ak.n, vk.n, mm.n

        ktns = [4 * qc + 4 for qc in range(4)]
        mmBs, akBs = [], []
        cm_, ca_ = mm.n, ak.n
        for qc in range(4):
            for h in range(16):
                mmBs.append(cm_)
                akBs.append(ca_)
                cm_ += 2 * ktns[qc] + 2
                ca_ += ktns[qc] + 3
        vk0_pd = vk.n
        with nc.Block() as blk:
            @blk.tensor
            def _(t):
                t.wait_ge(ak.h, ak_pc)
                for qc in range(4):
                    ktn = ktns[qc]
                    q0 = qc * 512
                    for h in range(16):
                        it = qc * 16 + h
                        mmB, akB = mmBs[it], akBs[it]
                        vkB = vk0_pd + 6 * it
                        ch = h // 2
                        r0 = 64 * (h % 2)
                        for kt in range(ktn):
                            psd = [ps_a, ps_b][kt % 2]
                            if kt >= 2:
                                t.wait_ge(ak.h, akB + kt - 1)
                            elif it > 0:
                                t.wait_ge(ak.h, akB - 3)
                            inst = t.matmul(
                                psd[:, :],
                                kf[r0:r0 + 64, ch * S + kt * 128:ch * S + kt * 128 + 128],
                                qf[r0:r0 + 64, ch * S + q0:ch * S + q0 + 512],
                                start=True, stop=True)
                            mm.inc(inst)
                        for kt in range(ktn):
                            if kt < 4 * qc:
                                t.wait_ge(ak.h, akB + kt + 1)
                            else:
                                t.wait_ge(vk.h, vkB + (kt - 4 * qc) + 1)
                            mm.inc(t.matmul(ps_den[0:1, :], p.const["ones_col_b"][:, :],
                                            es[:, kt * 512:(kt + 1) * 512],
                                            start=(kt == 0), stop=(kt == ktn - 1)))
                        t.wait_ge(vk.h, vkB + 5)
                        mm.inc(t.matmul(ps_bc[:, :], p.const["ones_row_f"][:, :],
                                        rd_row[:, :], start=True, stop=True))
                        psd = [ps_c, ps_d][it % 2]
                        if it >= 2:
                            t.wait_ge(vk.h, vkB - 6)
                        for kt in range(ktn):
                            inst = t.matmul(
                                psd[0:64, :],
                                vt[:, kt * D + 64 * h:kt * D + 64 * h + 64],
                                es[:, kt * 512:(kt + 1) * 512],
                                start=(kt == 0), stop=(kt == ktn - 1))
                        mm.inc(inst)

            @blk.vector
            def _(v):
                for qc in range(4):
                    ktn = ktns[qc]
                    for h in range(16):
                        it = qc * 16 + h
                        mmB, akB = mmBs[it], akBs[it]
                        vkB = vk0_pd + 6 * it
                        for j in range(4):
                            kt = 4 * qc + j
                            v.wait_ge(ak.h, akB + kt + 1)
                            vk.inc(v.tensor_mul(es[:, kt * 512:(kt + 1) * 512],
                                                es[:, kt * 512:(kt + 1) * 512],
                                                p.const["trimask"][:, j * 512:(j + 1) * 512]))
                        v.wait_ge(ak.h, akB + ktn + 1)
                        vk.inc(v.reciprocal(rd_row[:, :], den_sb[:, :]))
                        v.wait_ge(mm.h, mmB + 2 * ktn + 2)
                        v.wait_ge(ak.h, akB + ktn + 2)
                        if it >= 1:
                            v.wait_ge(ak.h, akB)
                        psd = [ps_c, ps_d][it % 2]
                        vk.inc(v.tensor_mul(t1[0:64, :], psd[0:64, :], rdb[0:64, :]))

            @blk.scalar
            def _(s):
                for qc in range(4):
                    ktn = ktns[qc]
                    for h in range(16):
                        it = qc * 16 + h
                        mmB, akB = mmBs[it], akBs[it]
                        vkB = vk0_pd + 6 * it
                        ch = h // 2
                        r0 = 64 * (h % 2)
                        for kt in range(ktn):
                            psd = [ps_a, ps_b][kt % 2]
                            s.wait_ge(mm.h, mmB + kt + 1)
                            ak.inc(s.activation(out=es[:, kt * 512:(kt + 1) * 512],
                                                in_=psd[:, :], func=AF.Exp,
                                                bias=negMb[:, h:h + 1]))
                        s.wait_ge(mm.h, mmB + 2 * ktn)
                        ak.inc(s.activation(out=den_sb[:, :], in_=ps_den[0:1, :],
                                            func=AF.Copy))
                        s.wait_ge(mm.h, mmB + 2 * ktn + 1)
                        if it > 0:
                            s.wait_ge(vk.h, vkB)
                        ak.inc(s.activation(out=rdb[:, :], in_=ps_bc[:, :], func=AF.Copy))
                        s.wait_ge(vk.h, vkB + 6)
                        ak.inc(s.activation(
                            out=ctxn[r0:r0 + 64, ch * S + qc * 512:ch * S + qc * 512 + 512],
                            in_=t1[0:64, :], func=AF.Copy))
        ak_pd, vk_pd, mm_pd = ak.n, vk.n, mm.n

        with nc.Block() as blk:
            mm0, ak0, vk0 = mm.n, ak.n, vk.n

            @blk.gpsimd
            def _(g):
                g.wait_ge(mm.h, mm_pd)
                for c in range(8):
                    lg.inc(g.dma_start(out=wob[:, c * D:(c + 1) * D],
                                       in_=wo_h[c * 128:(c + 1) * 128, :]), 16)
            lg_wo = lg.n

            @blk.sync
            def _(sync):
                sync.wait_ge(mm.h, mm_pd)
                for t in range(T):
                    ld.inc(sync.dma_start(out=x2[:, t * D:(t + 1) * D],
                                          in_=x_h[t * 128:(t + 1) * 128, :]), 16)
            ld_x2 = ld.n

            @blk.tensor
            def _(t):
                t.wait_ge(ak.h, ak_pd)
                t.wait_ge(lg.h, lg_wo)
                for tt1 in range(1, T + 1):
                    if tt1 >= 2:
                        t.wait_ge(vk.h, vk0 + tt1 - 1)
                    for m_ in range(8):
                        for hhx in range(2):
                            inst = t.matmul(
                                ps_o[:, hhx * 512:(hhx + 1) * 512],
                                ctxn[:, m_ * S + (tt1 - 1) * 128:
                                     m_ * S + (tt1 - 1) * 128 + 128],
                                wob[:, m_ * D + hhx * 512:m_ * D + hhx * 512 + 512],
                                start=(m_ == 0), stop=(m_ == 7))
                    mm.inc(inst)

            @blk.scalar
            def _(s):
                s.wait_ge(ld.h, ld_x2)
                s.wait_ge(ak.h, ak_pd)
                for tt in range(T):
                    if tt >= 2:
                        s.wait_ge(vk.h, vk0 + tt - 1)
                    ak.inc(s.activation(out=zre[:, (tt % 2) * D:(tt % 2) * D + D],
                                        in_=x2[:, tt * D:(tt + 1) * D], func=AF.Identity,
                                        scale=stats[:, 3 * T + tt:3 * T + tt + 1],
                                        bias=stats[:, 4 * T + tt:4 * T + tt + 1]))

            @blk.vector
            def _(v):
                for tt in range(T):
                    v.wait_ge(mm.h, mm0 + tt + 1)
                    v.wait_ge(ak.h, ak0 + tt + 1)
                    vk.inc(v.tensor_add(t_sb[:, tt * D:(tt + 1) * D],
                                        zre[:, (tt % 2) * D:(tt % 2) * D + D],
                                        ps_o[:, :]))
        ak_pe, vk_pe, mm_pe = ak.n, vk.n, mm.n

        with nc.Block() as blk:
            mm0, ak0, vk0, tx0 = mm.n, ak.n, vk.n, tx.n

            @blk.gpsimd
            def _(g):
                g.wait_ge(ak.h, ak_pe)
                lg.inc(g.dma_start(out=g2r[:, :], in_=g2_h[None, :]), 16)
                lg.inc(g.dma_start(out=gwb.rearrange("p (c e) -> p c e", e=8),
                                   in_=gw_h.rearrange("(c p) e -> p c e", p=128)), 16)
            lg_gw = lg.n

            @blk.scalar
            def _(s):
                s.wait_ge(vk.h, vk_pe)
                for t in range(T):
                    inst = s.activation(out=zre[:, (t % 2) * D:(t % 2) * D + D],
                                        in_=t_sb[:, t * D:(t + 1) * D], func=AF.Square,
                                        accum_out=stats[:, 8 * T + t:8 * T + t + 1])
                ak.inc(inst)
                s.wait_ge(vk.h, vk0 + 1)
                ak.inc(s.activation(out=stats[:, 10 * T:11 * T],
                                    in_=stats[:, 10 * T:11 * T],
                                    func=AF.Sqrt, bias=p.fc(1e-6)))
                s.wait_ge(mm.h, mm0 + 1)
                ak.inc(s.activation(out=g2b[:, 0:512], in_=ps_bc[:, 0:512],
                                    func=AF.Copy))
                s.wait_ge(mm.h, mm0 + 2)
                ak.inc(s.activation(out=g2b[:, 512:1024], in_=ps_bc[:, 0:512],
                                    func=AF.Copy))
                for tt in range(T):
                    s.wait_ge(mm.h, mm0 + 2 + tt + 1)
                    ak.inc(s.activation(out=lgs[:, tt * 8:(tt + 1) * 8],
                                        in_=ps_c[:, 0:8], func=AF.Copy))
                for tt in range(T):
                    s.wait_ge(vk.h, vk0 + 18 + 6 * tt + 2)
                    ak.inc(s.activation(out=m8[:, 8:9], in_=m8[:, 9:10],
                                        func=AF.Sigmoid))

            @blk.vector
            def _(v):
                v.wait_ge(ak.h, ak0 + 1)
                vk.inc(v.tensor_scalar_mul(stats[:, 10 * T:11 * T],
                                           stats[:, 8 * T:9 * T], 1.0 / D))
                v.wait_ge(ak.h, ak0 + 2)
                vk.inc(v.reciprocal(stats[:, 9 * T:10 * T], stats[:, 10 * T:11 * T]))
                v.wait_ge(ak.h, ak0 + 4)
                for t in range(T):
                    v.wait_ge(vk.h, vk0 + 2 + t)
                    if t >= 2:
                        v.wait_ge(tx.h, tx0 + 128 * (t - 1))
                    vk.inc(v.scalar_tensor_tensor(
                        ut[:, (t % 2) * D:(t % 2) * D + D],
                        t_sb[:, t * D:(t + 1) * D],
                        stats[:, 9 * T + t:9 * T + t + 1],
                        g2b[:, :], op0=ALU.mult, op1=ALU.mult))
                for tt in range(T):
                    s_ = vk0 + 18 + 6 * tt
                    v.wait_ge(ak.h, ak0 + 4 + tt + 1)
                    vk.inc(v.max(out=m8[:, 0:8], in_=lgs[:, tt * 8:(tt + 1) * 8]))
                    v.wait_ge(vk.h, s_ + 1)
                    vk.inc(v.tensor_sub(m8[:, 9:10], m8[:, 0:1], m8[:, 1:2]))
                    v.wait_ge(ak.h, ak0 + 20 + tt + 1)
                    vk.inc(v.tensor_scalar(m8[:, 10:11], m8[:, 8:9], -1.0, 1.0,
                                           op0=ALU.mult, op1=ALU.add))
                    v.wait_ge(vk.h, s_ + 3)
                    v.tensor_scalar(cmb[:, tt * 8:(tt + 1) * 8],
                                    lgs[:, tt * 8:(tt + 1) * 8], m8[:, 0:1], None,
                                    op0=ALU.is_equal)
                    vk.inc(v.tensor_scalar(t1[:, 0:8], lgs[:, tt * 8:(tt + 1) * 8],
                                           m8[:, 1:2], None, op0=ALU.is_equal))
                    v.wait_ge(vk.h, s_ + 4)
                    vk.inc(v.tensor_scalar(cmb[:, tt * 8:(tt + 1) * 8],
                                           cmb[:, tt * 8:(tt + 1) * 8], m8[:, 8:9],
                                           None, op0=ALU.mult))
                    v.wait_ge(vk.h, s_ + 5)
                    vk.inc(v.scalar_tensor_tensor(cmb[:, tt * 8:(tt + 1) * 8],
                                                  t1[:, 0:8], m8[:, 10:11],
                                                  cmb[:, tt * 8:(tt + 1) * 8],
                                                  op0=ALU.mult, op1=ALU.add))

            @blk.tensor
            def _(t):
                t.wait_ge(vk.h, vk_pe)
                t.wait_ge(lg.h, lg_gw)
                mm.inc(t.matmul(ps_bc[:, 0:512], p.const["ones_row_f"][:, :],
                                g2r[:, 0:512], start=True, stop=True))
                t.wait_ge(ak.h, ak0 + 3)
                mm.inc(t.matmul(ps_bc[:, 0:512], p.const["ones_row_f"][:, :],
                                g2r[:, 512:1024], start=True, stop=True))
                t.wait_ge(lg.h, lg_gw)
                for tt in range(T):
                    t.wait_ge(tx.h, tx0 + 128 * (tt + 1))
                    t.wait_ge(ak.h, ak0 + 4 + tt)
                    for c in range(8):
                        inst = t.matmul(ps_c[:, 0:8],
                                        u2_fm[:, c * S + tt * 128:c * S + tt * 128 + 128],
                                        gwb[:, c * 8:(c + 1) * 8],
                                        start=(c == 0), stop=(c == 7))
                    mm.inc(inst)

            @blk.sync
            def _(sync):
                for t in range(T):
                    sync.wait_ge(vk.h, vk0 + 2 + t + 1)
                    if t >= 1:
                        sync.wait_ge(tx.h, tx0 + 128 * t)
                    for c in range(8):
                        tx.inc(sync.dma_start_transpose(
                            u2_fm[:, c * S + t * 128:c * S + t * 128 + 128],
                            ut[:, (t % 2) * D + c * 128:(t % 2) * D + c * 128 + 128]), 16)
        ak_pf, vk_pf, mm_pf = ak.n, vk.n, mm.n
        tx_pf0 = tx.n - 2048

        with nc.Block() as blk:
            mm0, ak0, vk0, lg0, lw0 = mm.n, ak.n, vk.n, lg.n, lw.n

            @blk.gpsimd
            def _(g):
                for e in range(8):
                    if e > 0:
                        g.wait_ge(mm.h, mm0 + 80 * (e - 1) + 76)
                    for c in range(8):
                        lg.inc(g.dma_start(out=w1b[:, c * 2048:(c + 1) * 2048],
                                           in_=w1_h[e, c * 128:(c + 1) * 128, :]), 16)
                    if e > 0:
                        g.wait_ge(mm.h, mm0 + 80 * e)
                    for c in range(16):
                        lw.inc(g.dma_start(out=w2b[:, c * D:(c + 1) * D],
                                           in_=w2_h[e, c * 128:(c + 1) * 128, :]), 16)

            @blk.tensor
            def _(t):
                t.wait_ge(vk.h, vk_pf)
                for e in range(8):
                    t.wait_ge(lg.h, lg0 + 128 * (e + 1))
                    gidx = 0
                    for tc in range(4):
                        for hc in range(16):
                            gidx += 1
                            psd = [ps_a, ps_b][gidx % 2]
                            if gidx > 2:
                                t.wait_ge(ak.h, ak0 + 64 * e + gidx - 2)
                            elif e > 0:
                                t.wait_ge(ak.h, ak0 + 64 * e - 2 + gidx)
                            for c in range(8):
                                inst = t.matmul(
                                    psd[:, :],
                                    w1b[:, c * 2048 + hc * 128:c * 2048 + hc * 128 + 128],
                                    u2_fm[:, c * S + tc * 512:c * S + tc * 512 + 512],
                                    start=(c == 0), stop=(c == 7))
                            mm.inc(inst)
                        t.wait_ge(ak.h, ak0 + 64 * e + 16 * (tc + 1))
                        if tc == 0:
                            t.wait_ge(lw.h, lw0 + 256 * (e + 1))
                        for sub in range(4):
                            tt = tc * 4 + sub
                            gy = 16 * e + 4 * tc + sub
                            if gy > 0:
                                t.wait_ge(vk.h, vk0 + gy)
                            for hc in range(16):
                                for hhx in range(2):
                                    inst = t.matmul(
                                        ps_o[:, hhx * 512:(hhx + 1) * 512],
                                        hh[:, hc * 512 + sub * 128:
                                           hc * 512 + sub * 128 + 128],
                                        w2b[:, hc * D + hhx * 512:hc * D + hhx * 512 + 512],
                                        start=(hc == 0), stop=(hc == 15))
                            mm.inc(inst)

            @blk.scalar
            def _(s):
                for e in range(8):
                    for tc in range(4):
                        for hc in range(16):
                            s.wait_ge(mm.h, mm0 + 80 * e + 20 * tc + hc + 1)
                            psd = [ps_a, ps_b][(tc * 16 + hc + 1) % 2]
                            ak.inc(s.activation(
                                out=hh[:, hc * 512:(hc + 1) * 512],
                                in_=psd[:, :], func=AF.Silu))

            @blk.vector
            def _(v):
                for e in range(8):
                    for tc in range(4):
                        for sub in range(4):
                            tt = tc * 4 + sub
                            v.wait_ge(mm.h, mm0 + 80 * e + 20 * tc + 16 + sub + 1)
                            v.wait_ge(vk.h, vk0 + 16 * e + 4 * tc + sub)
                            vk.inc(v.scalar_tensor_tensor(
                                t_sb[:, tt * D:(tt + 1) * D],
                                ps_o[:, :], cmb[:, tt * 8 + e:tt * 8 + e + 1],
                                t_sb[:, tt * D:(tt + 1) * D],
                                op0=ALU.mult, op1=ALU.add))
        ak_pg, vk_pg, mm_pg = ak.n, vk.n, mm.n

        with nc.Block() as blk:
            ld0, vk0 = ld.n, vk.n

            @blk.sync
            def _(sync):
                sync.wait_ge(mm.h, mm_pg - 4)
                for t in range(T // 2):
                    ld.inc(sync.dma_start(out=x3[:, t * D:(t + 1) * D],
                                          in_=x_h[t * 128:(t + 1) * 128, :]), 16)
                sync.wait_ge(mm.h, mm_pg)
                for t in range(T // 2, T):
                    ld.inc(sync.dma_start(out=x3[:, t * D:(t + 1) * D],
                                          in_=x_h[t * 128:(t + 1) * 128, :]), 16)
            ld_x3 = ld.n

            @blk.vector
            def _(v):
                v.wait_ge(ld.h, ld_x3)
                v.wait_ge(vk.h, vk_pg)
                for tt in range(T):
                    if tt >= 2:
                        psem = [st, s2][tt % 2]
                        v.wait_ge(psem.h, 16 * ((tt - 2) // 2 + 1))
                    vk.inc(v.scalar_tensor_tensor(
                        xot[:, (tt % 2) * D:(tt % 2) * D + D],
                        t_sb[:, tt * D:(tt + 1) * D], 0.5,
                        x3[:, tt * D:(tt + 1) * D],
                        op0=ALU.mult, op1=ALU.add))

            @blk.gpsimd
            def _(g):
                for tt in range(T):
                    g.wait_ge(vk.h, vk0 + tt + 1)
                    psem = [st, s2][tt % 2]
                    psem.inc(g.dma_start(out=xout_h[tt * 128:(tt + 1) * 128, :],
                                         in_=xot[:, (tt % 2) * D:(tt % 2) * D + D]), 16)
                if dbg_h:
                    for nm, buf in [("u_fm", u_fm), ("ctxn", ctxn), ("negMb", negMb),
                                    ("t", t_sb), ("u2", u2_fm), ("lgs", lgs),
                                    ("cmb", cmb), ("kf", kf), ("vt", vt)]:
                        st.inc(g.dma_start(out=dbg_h[nm][:, :], in_=buf[:, :]), 16)
                g.wait_ge(st.h, st.n)
                g.wait_ge(s2.h, s2.n)
    return p


def _run(prog, in_maps, **kw):
    return run_bass_kernel_spmd(prog.nc, in_maps, core_ids=list(range(NCORE)), **kw)


_P_L0ATTN = None


def run_l0_attn(x_full, W_qkv, b_qkv, W_out, b_out, **kw):
    global _P_L0ATTN
    if _P_L0ATTN is None:
        _P_L0ATTN = build_l0_attn()
    cm = consts_in_map(L0ATTN_CONSTS)
    cm.update(_P_L0ATTN.fvals_in_map())
    im = []
    for c in range(NCORE):
        d = dict(cm)
        d["x"] = np.ascontiguousarray(x_full[c])
        d["W_qkv"] = W_qkv
        d["b_qkv"] = b_qkv
        d["W_out"] = W_out
        d["b_out"] = b_out
        im.append(d)
    res = _run(_P_L0ATTN, im, **kw)
    return np.stack([r["xout"] for r in res.results], axis=0)



A1_CONSTS = ["identf", "ones_col_b", "ones_col_f", "ones_row_b", "ones_row_f"]


def build_l0_ffn_a1():
    p = Prog("l0ffn_a1")
    nc = p.nc
    x_h = nc.dram_tensor("x", [S, D], F32, kind="ExternalInput").ap()
    wgt_h = nc.dram_tensor("WgT", [D, 8], F32, kind="ExternalInput").ap()
    z_h = nc.dram_tensor("zout", [D, S], BF16, kind="ExternalOutput").ap()
    sc_h = nc.dram_tensor("scout", [8, S], F32, kind="ExternalOutput").ap()

    with p.stack:
        p.decl_consts(A1_CONSTS)
        p.decl_fvals([1e-5])
        p.arena(120 * KB)
        x_sb = p.at("x_sb", 0, [128, T * D], F32)
        zf = p.at("zf_", 64 * KB, [128, 8 * S], BF16)
        zt = p.at("zt_", 96 * KB, [128, 2 * D], BF16)
        scf = p.at("scf_", 100 * KB, [8, S], F32)
        stats = p.sb("stats", [128, 8 * T], F32)
        wgb = p.sb("wgb", [128, 64], BF16)
        ps_sc = p.ps("ps_sc", [8, S])

        ld = p.sem("ld")
        lg = p.sem("lg")
        tx = p.sem("tx")
        mm = p.sem("mm")
        ak = p.sem("ak")
        vk = p.sem("vk")
        st = p.sem("st")

        with nc.Block() as blk:
            @blk.sync
            def _(sync):
                p.emit_const_loads(sync, ld)
                for t in range(T):
                    ld.inc(sync.dma_start(out=x_sb[:, t * D:(t + 1) * D],
                                          in_=x_h[t * 128:(t + 1) * 128, :]), 16)

            @blk.gpsimd
            def _(g):
                lg.inc(g.dma_start(out=wgb.rearrange("p (c e) -> p c e", e=8),
                                   in_=wgt_h.rearrange("(c p) e -> p c e", p=128)), 16)
        ld_all = ld.n
        lg_w = lg.n

        with nc.Block() as blk:
            emit_ln(p, blk, x_sb=x_sb, zt=zt, zf=zf, stats=stats,
                    ld=ld, ak=ak, vk=vk, tx=tx, wait_ld=ld_all, eps_ap=p.fc(1e-5))

            @blk.gpsimd
            def _(g):
                g.wait_ge(tx.h, 2048)
                for c in range(8):
                    st.inc(g.dma_start(out=z_h[c * 128:(c + 1) * 128, :],
                                       in_=zf[:, c * S:(c + 1) * S]), 16)
        tx_zf = tx.n
        with nc.Block() as blk:
            @blk.tensor
            def _(t):
                t.wait_ge(tx.h, tx_zf)
                t.wait_ge(lg.h, lg_w)
                for nn in range(4):
                    for c in range(8):
                        inst = t.matmul(ps_sc[:, nn * 512:(nn + 1) * 512],
                                        wgb[:, c * 8:(c + 1) * 8],
                                        zf[:, c * S + nn * 512:c * S + nn * 512 + 512],
                                        start=(c == 0), stop=(c == 7))
                    mm.inc(inst)

            @blk.scalar
            def _(s):
                for nn in range(4):
                    s.wait_ge(mm.h, nn + 1)
                    ak.inc(s.activation(out=scf[:, nn * 512:(nn + 1) * 512],
                                        in_=ps_sc[:, nn * 512:(nn + 1) * 512],
                                        func=AF.Copy))

            @blk.gpsimd
            def _(g):
                g.wait_ge(ak.h, 22)
                st.inc(g.dma_start(out=sc_h[:, :], in_=scf[:, :]), 16)
                g.wait_ge(st.h, st.n)
    return p




class Seq:

    def __init__(self, sem):
        self.sem = sem
        self.script = []

    def op(self, eng, fn, inc=1):
        self.script.append((eng, fn, inc))

    def emit(self, blk, engines):
        base = self.sem.n
        offs = []
        c = base
        for (_, _, inc) in self.script:
            offs.append(c)
            c += inc
        for name, deco in engines.items():
            def body(eng, name=name):
                for i, (en, fn, inc) in enumerate(self.script):
                    if en != name:
                        continue
                    eng.wait_ge(self.sem.h, offs[i])
                    fn(eng).then_inc(self.sem.h, inc)
            deco(body)
        self.sem.n = c


def build_l0_ffn_a2():
    p = Prog("l0ffn_a2")
    nc = p.nc
    z_h = nc.dram_tensor("z_all", [8 * S, D], BF16, kind="ExternalInput").ap()
    sc_h = nc.dram_tensor("sce", [128, 128], F32, kind="ExternalInput").ap()
    w1_h = nc.dram_tensor("W1eT", [D, 2048], F32, kind="ExternalInput").ap()
    w3_h = nc.dram_tensor("W3eT", [D, 2048], F32, kind="ExternalInput").ap()
    w2_h = nc.dram_tensor("W2eT", [2048, D], F32, kind="ExternalInput").ap()
    y_h = nc.dram_tensor("yout", [S, D], BF16, kind="ExternalOutput").ap()
    cidx_t = nc.dram_tensor("cidx", [S, 1], I32, kind="ExternalOutput")
    inv_h = nc.dram_tensor("invout", [128, 128], I32, kind="ExternalOutput").ap()
    w_hbm = nc.dram_tensor("w_scr", [128 * 128, 1], F32, kind="Internal")

    with p.stack:
        p.decl_consts(["identf", "ones_col_b", "ones_col_f", "ones_row_b",
                       "ones_row_f", "tri_lt", "tid"])
        p.decl_fvals([2048.0, 1.0, -1.0])
        p.arena(200 * KB)
        w1b = p.at("w1b_", 0, [128, 8 * 2048], BF16)
        w3b = p.at("w3b_", 32 * KB, [128, 8 * 2048], BF16)
        w2b = p.at("w2b_", 64 * KB, [128, 16 * D], BF16)
        zg = p.at("zg_", 96 * KB, [128, T * D], BF16)
        zgf = p.at("zgf_", 128 * KB, [128, 8 * S], BF16)
        hhs = p.at("hhs_", 160 * KB, [128, 16 * 512], BF16)
        sA = p.at("sA_", 176 * KB, [128, 2 * 512], BF16)
        sc = p.at("sc_", 178 * KB, [128, 128], F32)
        msk = p.at("msk_", 178 * KB + 512, [128, 128], F32)
        aux = p.at("aux_", 179 * KB, [128, 64], F32)
        strict = p.at("strict_", 180 * KB, [128, 128], F32)
        gtlo = p.at("gtlo_", 180 * KB + 512, [128, 128], F32)
        eq = p.at("eq_", 181 * KB, [128, 128], F32)
        scn = p.at("scn_", 181 * KB + 512, [128, 128], F32)
        zz = p.at("zz_", 182 * KB, [128, 128], F32)
        zrs = p.at("zrs_", 182 * KB + 512, [128, 128], F32)
        sel = p.at("sel_", 183 * KB, [128, 128], F32)
        rnk = p.at("rnk_", 183 * KB + 512, [128, 128], F32)
        w2d = p.at("w2d_", 184 * KB, [128, 128], F32)
        offs = p.at("offs_", 184 * KB + 512, [128, 128], F32)
        offsi = p.at("offsi_", 185 * KB, [128, 128], I32)
        tidi = p.at("tidi_", 185 * KB + 512, [128, 128], I32)
        cidxs = p.at("cidxs_", 186 * KB, [128, 16], I32)
        wslot = p.at("wslot_", 186 * KB + 64, [128, 16], F32)
        smll = p.at("smll_", 186 * KB + 128, [1, 16], F32)
        yt = p.at("yt_", 187 * KB, [128, 2 * D], BF16)

        ps_a = p.ps("ps_a", [128, 512])
        ps_b = p.ps("ps_b", [128, 512])
        ps_o = p.ps("ps_o", [128, 1024])
        ps_s = p.ps("ps_s", [128, 128])
        ps_1 = p.ps("ps_1", [1, 128])

        ld = p.sem("ld")
        lg = p.sem("lg")
        tx = p.sem("tx")
        sq = p.sem("sq")
        mm = p.sem("mm")
        ak = p.sem("ak")
        vk = p.sem("vk")
        st = p.sem("st")
        s2 = p.sem("s2")

        with nc.Block() as blk:
            @blk.sync
            def _(sync):
                p.emit_const_loads(sync, ld)
                ld.inc(sync.dma_start(out=sc[:, :], in_=sc_h[:, :]), 16)
            ld_all = ld.n

            @blk.gpsimd
            def _(g):
                for c in range(8):
                    lg.inc(g.dma_start(out=w1b[:, c * 2048:(c + 1) * 2048],
                                       in_=w1_h[c * 128:(c + 1) * 128, :]), 16)
                for c in range(8):
                    lg.inc(g.dma_start(out=w3b[:, c * 2048:(c + 1) * 2048],
                                       in_=w3_h[c * 128:(c + 1) * 128, :]), 16)
                for c in range(16):
                    lg.inc(g.dma_start(out=w2b[:, c * D:(c + 1) * D],
                                       in_=w2_h[c * 128:(c + 1) * 128, :]), 16)
        lg_w = lg.n

        NI = 34
        q = Seq(sq)
        OC = p.const["ones_col_f"]
        OR = p.const["ones_row_f"]
        V = lambda fn: q.op("vector", fn)
        A = lambda fn: q.op("scalar", fn)
        M = lambda fn: q.op("tensor", fn)
        G = lambda fn, inc=16: q.op("gpsimd", fn, inc)

        def first(v):
            v.wait_ge(ld.h, ld_all)
            return v.reduce_max(aux[:, 16:17], sc[:, :], axis=AX.X)
        V(first)
        V(lambda v: v.tensor_reduce(aux[:, 17:18], sc[:, :], axis=AX.X, op=ALU.min))
        M(lambda t: t.transpose(ps_s[0:1, 0:128], aux[:, 16:17],
                                p.const["identf"][:, :]))
        A(lambda s: s.activation(out=zz[0:1, 0:128], in_=ps_s[0:1, 0:128],
                                 func=AF.Copy))
        V(lambda v: v.reduce_max(smll[:, 0:1], zz[0:1, 0:128], axis=AX.X))
        M(lambda t: t.transpose(ps_s[0:1, 0:128], aux[:, 17:18],
                                p.const["identf"][:, :]))
        A(lambda s: s.activation(out=zz[0:1, 0:128], in_=ps_s[0:1, 0:128],
                                 func=AF.Copy))
        V(lambda v: v.tensor_reduce(smll[:, 1:2], zz[0:1, 0:128], axis=AX.X,
                                    op=ALU.min))
        M(lambda t: t.matmul(ps_s[:, 0:2], OR[:, :], smll[:, 0:2],
                             start=True, stop=True))
        A(lambda s: s.activation(out=aux[:, 6:7], in_=ps_s[:, 0:1], func=AF.Copy))
        A(lambda s: s.activation(out=aux[:, 1:2], in_=ps_s[:, 0:1],
                                 func=AF.Identity, bias=p.fc(1.0)))
        A(lambda s: s.activation(out=aux[:, 0:1], in_=ps_s[:, 1:2],
                                 func=AF.Identity, bias=p.fc(-1.0)))
        V(lambda v: v.tensor_scalar(aux[:, 7:8], aux[:, 6:7], -1.0, None,
                                    op0=ALU.mult))
        for _ in range(NI):
            V(lambda v: v.tensor_add(aux[:, 2:3], aux[:, 0:1], aux[:, 1:2]))
            V(lambda v: v.tensor_scalar_mul(aux[:, 2:3], aux[:, 2:3], 0.5))
            V(lambda v: v.tensor_scalar(msk[:, :], sc[:, :], aux[:, 2:3], 0.0,
                                        op0=ALU.is_gt, op1=ALU.add,
                                        accum_out=aux[:, 3:4]))
            M(lambda t: t.matmul(ps_1[:, 0:1], OC[:, :], aux[:, 3:4],
                                 start=True, stop=True))
            A(lambda s: s.activation(out=smll[:, 2:3], in_=ps_1[:, 0:1],
                                     func=AF.Copy))
            M(lambda t: t.matmul(ps_s[:, 4:5], OR[:, :], smll[:, 2:3],
                                 start=True, stop=True))
            A(lambda s: s.activation(out=aux[:, 4:5], in_=ps_s[:, 4:5],
                                     func=AF.Copy))
            V(lambda v: v.tensor_scalar(aux[:, 5:6], aux[:, 4:5], 2047.5, None,
                                        op0=ALU.is_gt))
            V(lambda v: v.tensor_sub(aux[:, 18:19], aux[:, 2:3], aux[:, 0:1]))
            V(lambda v: v.tensor_sub(aux[:, 19:20], aux[:, 1:2], aux[:, 2:3]))
            V(lambda v: v.tensor_mul(aux[:, 18:19], aux[:, 18:19], aux[:, 5:6]))
            V(lambda v: v.tensor_mul(aux[:, 19:20], aux[:, 19:20], aux[:, 5:6]))
            V(lambda v: v.tensor_add(aux[:, 0:1], aux[:, 0:1], aux[:, 18:19]))
            V(lambda v: v.tensor_add(aux[:, 1:2], aux[:, 2:3], aux[:, 19:20]))
        V(lambda v: v.tensor_scalar(strict[:, :], sc[:, :], aux[:, 1:2], 0.0,
                                    op0=ALU.is_gt, op1=ALU.add,
                                    accum_out=aux[:, 8:9]))
        V(lambda v: v.tensor_scalar(gtlo[:, :], sc[:, :], aux[:, 0:1], 0.0,
                                    op0=ALU.is_gt, op1=ALU.add,
                                    accum_out=aux[:, 9:10]))
        V(lambda v: v.tensor_sub(eq[:, :], gtlo[:, :], strict[:, :]))
        V(lambda v: v.tensor_sub(aux[:, 10:11], aux[:, 9:10], aux[:, 8:9]))
        V(lambda v: v.memset(zrs[:, :], 0.0))
        V(lambda v: v.tensor_tensor_scan(scn[:, :], eq[:, :], zrs[:, :], 0.0,
                                         op0=ALU.add, op1=ALU.add))
        M(lambda t: t.matmul(ps_1[:, 0:1], OC[:, :], aux[:, 8:9],
                             start=True, stop=True))
        A(lambda s: s.activation(out=smll[:, 2:3], in_=ps_1[:, 0:1],
                                 func=AF.Identity, scale=-1.0,
                                 bias=p.fc(2048.0)[0:1, :]))
        M(lambda t: t.matmul(ps_s[:, 4:5], OR[:, :], smll[:, 2:3],
                             start=True, stop=True))
        A(lambda s: s.activation(out=aux[:, 11:12], in_=ps_s[:, 4:5], func=AF.Copy))
        M(lambda t: t.matmul(ps_s[:, 8:9], p.const["tri_lt"][:, :], aux[:, 10:11],
                             start=True, stop=True))
        A(lambda s: s.activation(out=aux[:, 12:13], in_=ps_s[:, 8:9], func=AF.Copy))
        V(lambda v: v.tensor_scalar_add(rnk[:, :], scn[:, :], aux[:, 12:13]))
        V(lambda v: v.tensor_scalar(msk[:, :], rnk[:, :], aux[:, 11:12], None,
                                    op0=ALU.is_le))
        V(lambda v: v.tensor_mul(msk[:, :], eq[:, :], msk[:, :]))
        V(lambda v: v.tensor_add(sel[:, :], strict[:, :], msk[:, :]))
        A(lambda s: s.activation(out=zz[:, :], in_=sc[:, :], func=AF.Exp,
                                 bias=aux[:, 7:8]))
        V(lambda v: v.scalar_tensor_tensor(w2d[:, :], zz[:, :], 1.0, sel[:, :],
                                           op0=ALU.mult, op1=ALU.mult,
                                           accum_out=aux[:, 16:17]))
        M(lambda t: t.matmul(ps_1[:, 0:1], OC[:, :], aux[:, 16:17],
                             start=True, stop=True))
        A(lambda s: s.activation(out=smll[:, 3:4], in_=ps_1[:, 0:1], func=AF.Copy))
        V(lambda v: v.reciprocal(smll[:, 4:5], smll[:, 3:4]))
        V(lambda v: v.tensor_scalar_mul(smll[:, 5:6], smll[:, 4:5], 0.5))
        M(lambda t: t.matmul(ps_s[:, 4:5], OR[:, :], smll[:, 5:6],
                             start=True, stop=True))
        A(lambda s: s.activation(out=aux[:, 15:16], in_=ps_s[:, 4:5], func=AF.Copy))
        V(lambda v: v.tensor_scalar(w2d[:, :], w2d[:, :], aux[:, 15:16], None,
                                    op0=ALU.mult))
        V(lambda v: v.tensor_scalar(msk[:, :], sel[:, :], 1.0, 0.0,
                                    op0=ALU.mult, op1=ALU.add,
                                    accum_out=aux[:, 13:14]))
        V(lambda v: v.tensor_tensor_scan(scn[:, :], sel[:, :], zrs[:, :], 0.0,
                                         op0=ALU.add, op1=ALU.add))
        M(lambda t: t.matmul(ps_s[:, 8:9], p.const["tri_lt"][:, :], aux[:, 13:14],
                             start=True, stop=True))
        A(lambda s: s.activation(out=aux[:, 14:15], in_=ps_s[:, 8:9], func=AF.Copy))
        V(lambda v: v.tensor_sub(scn[:, :], scn[:, :], sel[:, :]))
        V(lambda v: v.tensor_scalar_add(rnk[:, :], scn[:, :], aux[:, 14:15]))
        V(lambda v: v.tensor_mul(zz[:, :], rnk[:, :], sel[:, :]))
        V(lambda v: v.tensor_scalar(gtlo[:, :], sel[:, :], -1e9, 1e9,
                                    op0=ALU.mult, op1=ALU.add))
        V(lambda v: v.tensor_add(zz[:, :], zz[:, :], gtlo[:, :]))
        V(lambda v: v.tensor_copy(offsi[:, :], zz[:, :]))
        G(lambda g: g.dma_start(out=inv_h[:, :], in_=offsi[:, :]))
        V(lambda v: v.tensor_copy(offsi[:, :], rnk[:, :]))
        V(lambda v: v.tensor_single_scalar(tidi[:, :], offsi[:, :], 7,
                                           op=ALU.arith_shift_right))
        V(lambda v: v.tensor_single_scalar(offsi[:, :], offsi[:, :], 127,
                                           op=ALU.bitwise_and))
        V(lambda v: v.tensor_copy(scn[:, :], offsi[:, :]))
        V(lambda v: v.tensor_copy(offs[:, :], tidi[:, :]))
        V(lambda v: v.scalar_tensor_tensor(eq[:, :], offs[:, :], 16384.0,
                                           p.const["tid"][:, :],
                                           op0=ALU.mult, op1=ALU.add))
        V(lambda v: v.tensor_scalar(msk[:, :], scn[:, :], 1.0, None, op0=ALU.add))
        V(lambda v: v.tensor_mul(msk[:, :], msk[:, :], sel[:, :]))
        V(lambda v: v.tensor_scalar(msk[:, :], msk[:, :], 1.0, None,
                                    op0=ALU.subtract))
        for qq in range(128):
            V(lambda v, qq=qq: v.tensor_single_scalar(gtlo[:, :], msk[:, :],
                                                      float(qq), op=ALU.is_equal))
            V(lambda v, qq=qq: v.scalar_tensor_tensor(
                zrs[:, :], eq[:, :], 1.0, gtlo[:, :], op0=ALU.mult, op1=ALU.mult,
                accum_out=strict[:, qq:qq + 1]))
        M(lambda t: t.transpose(ps_s[:, :], strict[:, :], p.const["identf"][:, :]))
        A(lambda s: s.activation(out=w2d[:, :] if False else zz[:, :],
                                 in_=ps_s[:, :], func=AF.Copy))
        V(lambda v: v.tensor_copy(offsi[:, :], zz[:, :]))
        V(lambda v: v.tensor_single_scalar(tidi[:, :], offsi[:, :], 14,
                                           op=ALU.arith_shift_right))
        V(lambda v: v.tensor_single_scalar(offsi[:, :], offsi[:, :], 16383,
                                           op=ALU.bitwise_and))
        V(lambda v: v.tensor_copy(offs[:, :], offsi[:, :]))
        V(lambda v: v.tensor_copy(zz[:, :], tidi[:, :]))
        for g_ in range(16):
            V(lambda v, g_=g_: v.tensor_single_scalar(gtlo[:, :], zz[:, :],
                                                      float(g_), op=ALU.is_equal))
            V(lambda v, g_=g_: v.scalar_tensor_tensor(
                zrs[:, :], offs[:, :], 1.0, gtlo[:, :], op0=ALU.mult, op1=ALU.mult,
                accum_out=rnk[:, g_:g_ + 1]))
        V(lambda v: v.tensor_copy(cidxs[:, :], rnk[:, 0:16]))
        rbnd = [None]

        def _setup_reg(g):
            rbnd[0] = g.alloc_register("rbnd")
            return g.reg_mov(rbnd[0], 16383)
        G(_setup_reg, inc=1)
        G(lambda g: g.dma_start(out=w_hbm.ap().rearrange("(a b) c -> a (b c)", b=128),
                                in_=w2d[:, :]))
        for g_ in range(16):
            G(lambda g, g_=g_: g.indirect_dma_start(
                wslot[:, g_:g_ + 1], None,
                w_hbm.ap()[:, :],
                IndirectOffsetOnAxis(ap=cidxs[:, g_:g_ + 1], axis=0),
                bounds_check=rbnd[0], oob_is_err=False))
        for g_ in range(16):
            G(lambda g, g_=g_: g.indirect_dma_start(
                zg[:, g_ * D:(g_ + 1) * D], None,
                z_h, IndirectOffsetOnAxis(ap=cidxs[:, g_:g_ + 1], axis=0),
                bounds_check=rbnd[0], oob_is_err=False))
        for g_ in range(16):
            G(lambda g, g_=g_: g.dma_start(
                out=cidx_t.ap()[g_ * 128:(g_ + 1) * 128, :],
                in_=cidxs[:, g_:g_ + 1]))

        with nc.Block() as blk:
            q.emit(blk, {"vector": blk.vector, "scalar": blk.scalar,
                         "tensor": blk.tensor, "gpsimd": blk.gpsimd})
        sq_gate = sq.n

        with nc.Block() as blk:
            mm0, ak0, vk0, tx0 = mm.n, ak.n, vk.n, tx.n

            @blk.sync
            def _(sync):
                sync.wait_ge(sq.h, sq_gate)
                for t in range(T):
                    for c in range(8):
                        tx.inc(sync.dma_start_transpose(
                            zgf[:, c * S + t * 128:c * S + t * 128 + 128],
                            zg[:, t * D + c * 128:t * D + c * 128 + 128]), 16)

            @blk.tensor
            def _(t):
                t.wait_ge(tx.h, tx0 + 2048)
                t.wait_ge(lg.h, lg_w)
                for tc in range(4):
                    for hc in range(16):
                        if tc * 16 + hc > 0:
                            t.wait_ge(vk.h, vk0 + 16 * tc + hc)
                        for c in range(8):
                            i1 = t.matmul(
                                ps_a[:, :],
                                w1b[:, c * 2048 + hc * 128:c * 2048 + hc * 128 + 128],
                                zgf[:, c * S + tc * 512:c * S + tc * 512 + 512],
                                start=(c == 0), stop=(c == 7))
                        mm.inc(i1)
                        for c in range(8):
                            i3 = t.matmul(
                                ps_b[:, :],
                                w3b[:, c * 2048 + hc * 128:c * 2048 + hc * 128 + 128],
                                zgf[:, c * S + tc * 512:c * S + tc * 512 + 512],
                                start=(c == 0), stop=(c == 7))
                        mm.inc(i3)
                    t.wait_ge(vk.h, vk0 + 16 * (tc + 1))
                    for sub in range(4):
                        if sub >= 1:
                            t.wait_ge(ak.h, ak0 + 20 * tc + 16 + sub)
                        elif tc >= 1:
                            t.wait_ge(ak.h, ak0 + 20 * tc)
                        for hc in range(16):
                            for hhx in range(2):
                                iy = t.matmul(
                                    ps_o[:, hhx * 512:(hhx + 1) * 512],
                                    hhs[:, hc * 512 + sub * 128:hc * 512 + sub * 128 + 128],
                                    w2b[:, hc * D + hhx * 512:hc * D + hhx * 512 + 512],
                                    start=(hc == 0), stop=(hc == 15))
                        mm.inc(iy)

            @blk.scalar
            def _(s):
                for tc in range(4):
                    for hc in range(16):
                        s.wait_ge(mm.h, mm0 + 36 * tc + 2 * hc + 1)
                        if tc * 16 + hc >= 2:
                            s.wait_ge(vk.h, vk0 + 16 * tc + hc - 1)
                        ak.inc(s.activation(
                            out=sA[:, (hc % 2) * 512:(hc % 2) * 512 + 512],
                            in_=ps_a[:, :], func=AF.Silu))
                    for sub in range(4):
                        s.wait_ge(mm.h, mm0 + 36 * tc + 32 + sub + 1)
                        tt = 4 * tc + sub
                        if tt >= 2:
                            psem = [st, s2][tt % 2]
                            s.wait_ge(psem.h, 16 * ((tt - 2) // 2 + 1))
                        ak.inc(s.activation(
                            out=yt[:, (sub % 2) * D:(sub % 2) * D + D],
                            in_=ps_o[:, :], func=AF.Identity,
                            scale=wslot[:, 4 * tc + sub:4 * tc + sub + 1]))

            @blk.vector
            def _(v):
                for tc in range(4):
                    for hc in range(16):
                        v.wait_ge(mm.h, mm0 + 36 * tc + 2 * hc + 2)
                        v.wait_ge(ak.h, ak0 + 20 * tc + hc + 1)
                        vk.inc(v.tensor_mul(
                            hhs[:, hc * 512:(hc + 1) * 512],
                            ps_b[:, :],
                            sA[:, (hc % 2) * 512:(hc % 2) * 512 + 512]))

            @blk.gpsimd
            def _(g):
                for tc in range(4):
                    for sub in range(4):
                        tt = 4 * tc + sub
                        g.wait_ge(ak.h, ak0 + 20 * tc + 16 + sub + 1)
                        psem = [st, s2][tt % 2]
                        psem.inc(g.dma_start(out=y_h[tt * 128:(tt + 1) * 128, :],
                                             in_=yt[:, (sub % 2) * D:(sub % 2) * D + D]), 16)
                g.wait_ge(st.h, st.n)
                g.wait_ge(s2.h, s2.n)
    return p


def build_l0_ffn_a3():
    p = Prog("l0ffn_a3")
    nc = p.nc
    x_h = nc.dram_tensor("x", [S, D], F32, kind="ExternalInput").ap()
    y_h = nc.dram_tensor("y_all", [8 * S, D], BF16, kind="ExternalInput").ap()
    ci_h = nc.dram_tensor("idx_cols", [128, 128], I32, kind="ExternalInput").ap()
    xout_h = nc.dram_tensor("xout", [S, D], F32, kind="ExternalOutput").ap()

    with p.stack:
        p.decl_consts(["eoffs"])
        p.arena(120 * KB)
        x_sb = p.at("x_sb", 0, [128, T * D], F32)
        acc = p.at("acc_", 64 * KB, [128, T * D], BF16)
        xo = p.at("xo_", 96 * KB, [128, 2 * D], F32)
        cif = p.at("cif_", 104 * KB, [128, 128], F32)
        cii = p.at("cii_", 104 * KB + 512, [128, 128], I32)
        loci = p.at("loci_", 105 * KB, [128, 128], I32)

        ld = p.sem("ld")
        sq = p.sem("sq")
        lg = p.sem("lg")
        vk = p.sem("vk")
        st = p.sem("st")
        s2 = p.sem("s2")

        with nc.Block() as blk:
            @blk.sync
            def _(sync):
                p.emit_const_loads(sync, ld)
                ld.inc(sync.dma_start(out=cii[:, :], in_=ci_h[:, :]), 16)
                for t in range(T):
                    ld.inc(sync.dma_start(out=x_sb[:, t * D:(t + 1) * D],
                                          in_=x_h[t * 128:(t + 1) * 128, :]), 16)
            ld_all = ld.n

            @blk.vector
            def _(v):
                v.wait_ge(ld.h, ld_all)
                vk.inc(v.tensor_copy(cif[:, :], cii[:, :]))
                v.wait_ge(vk.h, 1)
                vk.inc(v.tensor_add(cif[:, :], cif[:, :], p.const["eoffs"][:, :]))
                v.wait_ge(vk.h, 2)
                vk.inc(v.tensor_copy(loci[:, :], cif[:, :]))
                vk.inc(v.memset(acc[:, 0:8 * D], 0.0))
                vk.inc(v.memset(acc[:, 8 * D:16 * D], 0.0))
            vk_prep = vk.n

            @blk.gpsimd
            def _(g):
                import os as _os
                NE = int(_os.environ.get("A3_NE", "8"))
                NJ = int(_os.environ.get("A3_NJ", str(T)))
                rb = g.alloc_register("rb_a3")
                g.reg_mov(rb, 8 * S - 1)
                g.wait_ge(vk.h, vk_prep)
                SER = _os.environ.get("A3_SER", "1") == "1"
                for e in range(NE):
                    for j in range(NJ):
                        if SER:
                            psem = sq
                            if e > 0 and j == 0:
                                g.wait_ge(sq.h, 16 * NJ * e)
                        else:
                            psem = sq if j < 8 else lg
                            if e > 0 and j == 0:
                                g.wait_ge(sq.h, 128 * e)
                            if e > 0 and j == 8:
                                g.wait_ge(lg.h, 128 * e)
                        psem.inc(g.indirect_dma_start(
                            acc[:, j * D:(j + 1) * D], None, y_h[:, :],
                            IndirectOffsetOnAxis(ap=loci[:, e * 16 + j:e * 16 + j + 1],
                                                 axis=0),
                            bounds_check=rb, oob_is_err=False,
                            compute_op=ALU.add), 16)

            @blk.vector
            def _(v):
                v.wait_ge(sq.h, sq.n)
                v.wait_ge(lg.h, lg.n)
                for t in range(T):
                    if t >= 2:
                        psem = [st, s2][t % 2]
                        v.wait_ge(psem.h, 16 * ((t - 2) // 2 + 1))
                    vk.inc(v.tensor_add(xo[:, (t % 2) * D:(t % 2) * D + D],
                                        x_sb[:, t * D:(t + 1) * D],
                                        acc[:, t * D:(t + 1) * D]))

            @blk.gpsimd
            def _(g):
                for t in range(T):
                    g.wait_ge(vk.h, vk_prep + t + 1)
                    psem = [st, s2][t % 2]
                    psem.inc(g.dma_start(out=xout_h[t * 128:(t + 1) * 128, :],
                                         in_=xo[:, (t % 2) * D:(t % 2) * D + D]), 16)
                g.wait_ge(st.h, st.n)
                g.wait_ge(s2.h, s2.n)
    return p


_P_A1 = _P_A2 = _P_A3 = None


def run_l0_ffn(x_full, Wg, W1, W2, W3, **kw):
    global _P_A1, _P_A2, _P_A3
    if _P_A1 is None:
        _P_A1 = build_l0_ffn_a1()
        _P_A2 = build_l0_ffn_a2()
        _P_A3 = build_l0_ffn_a3()
    cm1 = consts_in_map(A1_CONSTS)
    cm1.update(_P_A1.fvals_in_map())
    im = []
    for c in range(NCORE):
        d = dict(cm1)
        d["x"] = np.ascontiguousarray(x_full[c])
        d["WgT"] = np.ascontiguousarray(Wg.T)
        im.append(d)
    r1 = _run(_P_A1, im, **kw)
    z_all = np.concatenate([np.ascontiguousarray(r["zout"].T) for r in r1.results], 0)
    sc_all = np.stack([r["scout"] for r in r1.results], 0)
    cm2 = consts_in_map(["identf", "ones_col_b", "ones_col_f", "ones_row_b",
                         "ones_row_f", "tri_lt", "tid"])
    cm2.update(_P_A2.fvals_in_map())
    im = []
    for e in range(NCORE):
        d = dict(cm2)
        d["z_all"] = z_all
        d["sce"] = np.ascontiguousarray(
            sc_all[:, e, :].reshape(16384).reshape(128, 128))
        d["W1eT"] = np.ascontiguousarray(W1[e].T)
        d["W3eT"] = np.ascontiguousarray(W3[e].T)
        d["W2eT"] = np.ascontiguousarray(W2[e].T)
        im.append(d)
    r2 = _run(_P_A2, im, **kw)
    y_all = np.concatenate([r["yout"] for r in r2.results], 0)
    cm3 = consts_in_map(["eoffs"])
    im = []
    for c in range(NCORE):
        d = dict(cm3)
        d["x"] = np.ascontiguousarray(x_full[c])
        d["y_all"] = y_all
        idx_cols = np.zeros((128, 128), np.int32)
        for e in range(NCORE):
            idx_cols[:, e * 16:(e + 1) * 16] = \
                r2.results[e]["invout"][16 * c:16 * c + 16, :].T
        d["idx_cols"] = np.ascontiguousarray(idx_cols)
        im.append(d)
    r3 = _run(_P_A3, im, **kw)
    return np.stack([r["xout"] for r in r3.results], 0)


_P_L1ATTN = _P_L1FFN = None


def run_l1_attn(x_full, Wqkv_w, b_qkv, out_w, b_out, dd, **kw):
    global _P_L1ATTN
    if _P_L1ATTN is None:
        _P_L1ATTN = build_l1_attn()
    cm = consts_in_map(["identf", "ones_col_b", "ones_row_b", "ones_row_f",
                        "dist2d", "hsel", "ones_sq_b"])
    cm.update(_P_L1ATTN.fvals_in_map())
    wqkvT = np.ascontiguousarray(Wqkv_w.T)
    woT = np.ascontiguousarray(out_w.T)
    ddr = np.asarray(dd, np.float32).reshape(1, 1)
    im = []
    for c in range(NCORE):
        d = dict(cm)
        d["x"] = np.ascontiguousarray(x_full[c])
        d["WqkvT"] = wqkvT
        d["b_qkv"] = b_qkv
        d["WoT"] = woT
        d["b_out"] = b_out
        d["dd"] = ddr
        im.append(d)
    res = _run(_P_L1ATTN, im, **kw)
    return np.stack([r["xout"] for r in res.results], axis=0)


def run_l1_ffn(x_full, g1, g2, Wq, Wdkv, Wuk, Wuv, Wo, gate_w, w1, w2, **kw):
    global _P_L1FFN
    if _P_L1FFN is None:
        _P_L1FFN = build_l1_ffn()
    cm = consts_in_map(["identf", "ones_col_b", "ones_row_b", "ones_row_f",
                        "hsel", "trimask"])
    cm.update(_P_L1FFN.fvals_in_map())
    im = []
    for c in range(NCORE):
        d = dict(cm)
        d["x"] = np.ascontiguousarray(x_full[c])
        d.update(g1=g1, g2=g2, Wq=Wq, Wdkv=Wdkv, Wuk=Wuk, Wuv=Wuv, Wo=Wo,
                 gate_w=gate_w, w1=w1, w2=w2)
        im.append(d)
    res = _run(_P_L1FFN, im, **kw)
    return np.stack([r["xout"] for r in res.results], axis=0)


def kernel(**inputs):
    x = np.ascontiguousarray(np.asarray(inputs["L0_attn_x"], np.float32))
    Wg = np.asarray(inputs["L0_ffn_Wg"], np.float32)
    W1 = np.asarray(inputs["L0_ffn_W1"], np.float32)
    W2 = np.asarray(inputs["L0_ffn_W2"], np.float32)
    W3 = np.asarray(inputs["L0_ffn_W3"], np.float32)
    x = run_l0_ffn(x, Wg, W1, W2, W3)
    x = run_l0_attn(x,
                    np.asarray(inputs["L0_attn_W_qkv"], np.float32),
                    np.asarray(inputs["L0_attn_b_qkv"], np.float32),
                    np.asarray(inputs["L0_attn_W_out"], np.float32),
                    np.asarray(inputs["L0_attn_b_out"], np.float32))
    x = run_l0_ffn(x, Wg, W1, W2, W3)
    ffn1 = dict(
        g1=np.asarray(inputs["L1_ffn_g1"], np.float32),
        g2=np.asarray(inputs["L1_ffn_g2"], np.float32),
        Wq=np.asarray(inputs["L1_ffn_Wq"], np.float32),
        Wdkv=np.asarray(inputs["L1_ffn_Wdkv"], np.float32),
        Wuk=np.asarray(inputs["L1_ffn_Wuk"], np.float32),
        Wuv=np.asarray(inputs["L1_ffn_Wuv"], np.float32),
        Wo=np.asarray(inputs["L1_ffn_Wo"], np.float32),
        gate_w=np.asarray(inputs["L1_ffn_gate_w"], np.float32),
        w1=np.asarray(inputs["L1_ffn_w1"], np.float32),
        w2=np.asarray(inputs["L1_ffn_w2"], np.float32))
    x = run_l1_ffn(x, **ffn1)
    x = run_l1_attn(x,
                    np.asarray(inputs["L1_attn_Wqkv_w"], np.float32),
                    np.asarray(inputs["L1_attn_Wqkv_b"], np.float32),
                    np.asarray(inputs["L1_attn_out_w"], np.float32),
                    np.asarray(inputs["L1_attn_out_b"], np.float32),
                    np.asarray(inputs["L1_attn_distance_decay"], np.float32))
    x = run_l1_ffn(x, **ffn1)
    return np.ascontiguousarray(x.astype(np.float32))
